# version 27
# baseline (speedup 1.0000x reference)
import contextlib
import numpy as np
import concourse.bass as bass
import concourse.mybir as mybir

F32 = mybir.dt.float32
F32R = mybir.dt.float32r
BF16 = mybir.dt.bfloat16
AF = mybir.ActivationFunctionType
ALU = mybir.AluOpType
AX = mybir.AxisListType.X
MAGIC = float(np.float32(1.5 * 2 ** 23))
NPBF = mybir.dt.np(BF16)


class Ch:

    def __init__(self, eng, sem_h):
        self.eng, self.sem = eng, sem_h
        self.cnt = {}
        self.last = None

    def __call__(self, inst, purpose=None):
        s = purpose if purpose is not None else self.sem
        k = id(s)
        self.cnt[k] = self.cnt.get(k, 0) + 1
        inst.then_inc(s, 1)
        self.last = (s, self.cnt[k])
        return inst

    def pre(self):
        if self.last is not None:
            self.eng.wait_ge(self.last[0], self.last[1])


def build_nc(L=4096, n_cores=8, dbg=False):
    D, H, S = 1024, 4096, 64
    NT = L // 128
    NT5 = L // 512
    NJ = D // 128
    NM = H // 128
    NG = NM // 4
    WMXN1 = (D * H) // (128 * 512)
    WMXN2 = (H * D) // (128 * 1024)
    MODN = (6 * D) // 512

    nc = bass.Bass("TRN2", target_bir_lowering=False, debug=dbg)

    x_h = nc.dram_tensor("x", [L, D], F32, kind="ExternalInput").ap()
    bwT_h = nc.dram_tensor("bwT", [D, S], F32, kind="ExternalInput").ap()
    cwT_h = nc.dram_tensor("cwT", [S, D], F32, kind="ExternalInput").ap()
    gwT_h = nc.dram_tensor("gwT", [D, D], F32, kind="ExternalInput").ap()
    gateb_h = nc.dram_tensor("gate_b", [1, D], F32, kind="ExternalInput").ap()
    logdec_h = nc.dram_tensor("log_dec", [1, D], F32, kind="ExternalInput").ap()
    w1T_h = nc.dram_tensor("w1T", [D, H], F32, kind="ExternalInput").ap()
    b1f_h = nc.dram_tensor("b1f", [128, NM], F32, kind="ExternalInput").ap()
    w2T_h = nc.dram_tensor("w2T", [H, D], F32, kind="ExternalInput").ap()
    b2_h = nc.dram_tensor("b2", [1, D], F32, kind="ExternalInput").ap()
    cfm_h = nc.dram_tensor("c_fm", [128, NJ], F32, kind="ExternalInput").ap()
    wmod_h = nc.dram_tensor("wmod", [D, 6 * D], F32, kind="ExternalInput").ap()
    out_h = nc.dram_tensor("out", [L, D], F32, kind="ExternalOutput").ap()

    xp_h = nc.dram_tensor("xp_scr", [L, D], F32).ap()
    h1_h = nc.dram_tensor("h1_scr", [NT5 * NM * 128, 512], BF16).ap()
    rowb_h = nc.dram_tensor("rowb_scr", [1, 8 * D], BF16).ap()
    cc1i = nc.dram_tensor("cc1_in", [1, 16], F32)
    cc1o = nc.dram_tensor("cc1_out", [1, 16], F32)
    cc2i = nc.dram_tensor("cc2_in", [1, 16], F32)
    cc2o = nc.dram_tensor("cc2_out", [1, 16], F32)

    t_excl_np = np.triu(np.ones((128, 128), dtype=np.float32), 1).astype(NPBF)
    Texcl_h = nc.inline_tensor(t_excl_np, "Texcl_c").ap()
    ident_h = nc.inline_tensor(np.eye(128, dtype=np.float32).astype(NPBF), "ident_c").ap()
    identf_h = nc.inline_tensor(np.eye(128, dtype=np.float32), "identf_c").ap()
    onesr_h = nc.inline_tensor(np.ones((1, 128), dtype=NPBF), "onesr_c").ap()
    onesrf_h = nc.inline_tensor(np.ones((1, 128), dtype=np.float32), "onesrf_c").ap()
    onesc_h = nc.inline_tensor(np.ones((128, 1), dtype=NPBF), "onesc_c").ap()
    oA = np.zeros((128, 33), dtype=np.float32); oA[:, 0] = 1.0
    oB = np.zeros((128, 33), dtype=np.float32); oB[:, 32] = 1.0
    onesA_h = nc.inline_tensor(oA.astype(NPBF), "onesA_c").ap()
    onesB_h = nc.inline_tensor(oB.astype(NPBF), "onesB_c").ap()
    epsc_h = nc.inline_tensor(np.full((128, 1), 1e-5, dtype=np.float32),
                              "epsc_c").ap()

    stk = contextlib.ExitStack()
    s12 = contextlib.ExitStack()
    p1 = contextlib.ExitStack()
    p2 = contextlib.ExitStack()
    p3 = contextlib.ExitStack()
    ph1t = contextlib.ExitStack()
    pro = contextlib.ExitStack()
    ph2s = contextlib.ExitStack()
    s3 = contextlib.ExitStack()

    def _mk(stack):
        def sbufF(name, shape, dt):
            return stack.enter_context(nc.sbuf_tensor(name, shape, dt))
        return sbufF

    sbuf = _mk(stk)
    sbuf12 = _mk(s12)
    sbuf1 = _mk(ph1t)
    sbufP = _mk(pro)
    sbuf2 = _mk(ph2s)
    sbuf3 = _mk(s3)

    def psum1(name, shape, dt=F32):
        return p1.enter_context(nc.psum_tensor(name, shape, dt))

    def psum2(name, shape, dt=F32):
        return p2.enter_context(nc.psum_tensor(name, shape, dt))

    def psum3(name, shape, dt=F32):
        return p3.enter_context(nc.psum_tensor(name, shape, dt))

    def sem(name):
        return stk.enter_context(nc.semaphore(name))

    with stk:
        Texcl = sbuf("Texcl_s", [128, 128], BF16)
        ident = sbuf("ident_s", [128, 128], BF16)
        identf = sbuf("identf_s", [128, 128], F32)
        onesr = sbuf("onesr_s", [1, 128], BF16)
        onesrf = sbuf("onesrf_s", [1, 128], F32)
        onesc = sbuf("onesc_s", [128, 1], BF16)
        onesA = sbuf("onesA_s", [128, 33], BF16)
        onesB = sbuf("onesB_s", [128, 33], BF16)
        gateb_bf = sbuf("gateb_bf", [1, D], BF16)
        gb2_bf = sbuf("gb2_bf", [1, D], BF16)
        bu0_bf = sbuf("bu0_bf", [1, 64], BF16)
        sa_bf = sbuf("sa_bf", [128, NJ], BF16)
        ca1_fm = sbuf("ca1_fm", [128, NJ], F32)
        sa_fm = sbuf("sa_fm", [128, NJ], F32)
        cb1_fm = sbuf("cb1_fm", [128, NJ], F32)
        sbv_fm = sbuf("sbv_fm", [128, NJ], F32)
        gb_row = sbuf("gb_row", [1, D], F32)
        b1f = sbuf("b1f_s", [128, NM], F32)
        b1s = sbuf("b1s", [128, NM], F32)
        C_sb = sbuf("C_sb", [1, D], BF16)
        C_f32 = sbuf("C_f32", [1, D], F32)
        hmax2 = sbuf("hmax2", [128, 8 * NT], F32)
        hmax1 = sbuf("hmax1", [128, 4 * NG * NT5], F32)
        w1mxc = sbuf("w1mxc", [128, WMXN1], F32)
        w2mxc = sbuf("w2mxc", [128, WMXN2], F32)
        scl = sbuf("scl", [1, 16], F32)
        invsw1_c = sbuf("invsw1_c", [128, 1], F32)
        invsx1_c = sbuf("invsx1_c", [128, 1], F32)
        s1_c = sbuf("s1_c", [128, 1], F32)
        invsw2_c = sbuf("invsw2_c", [128, 1], F32)
        invsx2_c = sbuf("invsx2_c", [128, 1], F32)
        s2_c = sbuf("s2_c", [128, 1], F32)
        ccb = sbuf("ccb", [1, 16], F32)
        epsc = sbuf("epsc", [128, 1], F32)

        cols = sbuf("cols", [128, 2, 40], F32)
        rscr = sbuf("rscr", [128, 1024], F32)
        h2q = sbuf12("h2q", [128, NJ, L], BF16)
        gwT = sbuf12("gwT_s", [128, NJ, D], BF16)
        bwT = sbuf12("bwT_s", [128, NJ, S], BF16)
        cwga = sbuf12("cwga", [64, D], BF16)
        xbuf = sbuf1("xbuf", [128, 6, D], F32)
        xnorm_bf = sbuf1("xnorm_bf", [128, 2, D], BF16)
        xnorm2_bf = sbuf1("xnorm2_bf", [128, 3, D], BF16)
        ha_fm = sbuf1("ha_fm", [128, 2, D], BF16)
        bu_bf = sbuf1("bu_bf", [64, 2, 128], BF16)
        W_sb = sbuf1("W_sb", [128, 3, D], BF16)
        ld_bc = sbuf1("ld_bc", [128, D], F32)
        tmp_bf = sbuf1("tmp_bf", [128, 2, D], BF16)
        gate_sb = sbuf1("gate_sb", [128, 2, D], BF16)
        xp_buf = sbuf1("xp_buf", [128, 3, D], F32)
        tcol = sbuf1("tcol", [128, 4], F32)
        wmx_buf = sbufP("wmx_buf", [128, 2, 512], F32)
        wm_bf = sbufP("wm_bf", [128, 4, 512], F32)
        silu_f = sbufP("silu_f", [128, NJ], F32)
        wmf = sbufP("wmf", [128, 4, 256], F32)
        cfm = sbufP("cfm_s", [128, NJ], F32)
        mod_b = sbufP("mod_b", [1, 2 * D], BF16)
        mod_f = sbufP("mod_f", [1, D], F32)
        sgn_row = sbufP("sgn_row", [1, D], BF16)
        ldrow_f = sbufP("ldrow_f", [1, D], F32)
        cw_bf = sbufP("cw_bf", [64, D], BF16)

        PT1 = psum1("PT1", [128, D], BF16)
        PW = psum1("PW", [128, 512])
        PBS = psum1("PBS", [128, D])
        PT2 = psum1("PT2", [128, D], BF16)
        PCS = psum1("PCS", [64, 512])
        PG = psum1("PG", [128, D])

        d_sm = sem("d_sm"); d_x = sem("d_x"); d_xpst = sem("d_xpst")
        d_xps2 = sem("d_xps2")
        d_wmx = sem("d_wmx"); d_w1 = sem("d_w1"); d_w2 = sem("d_w2")
        d_w1b = sem("d_w1b"); s_w1rh = sem("s_w1rh"); d_h1lb = sem("d_h1lb")
        d_h1st = sem("d_h1st"); d_h1ld = sem("d_h1ld"); d_xpld = sem("d_xpld")
        d_out = sem("d_out"); d_gw = sem("d_gw"); d_wm = sem("d_wm")
        d_fm = sem("d_fm"); d_cc = sem("d_cc"); d_wmf = sem("d_wmf")
        d_wmx2 = sem("d_wmx2")
        c_v = Ch(nc.vector, sem("c_v"))
        c_s = Ch(nc.scalar, sem("c_s"))
        c_g = Ch(nc.gpsimd, sem("c_g"))
        s_sumx = sem("s_sumx")
        s_cols1 = sem("s_cols1"); s_sqrt1 = sem("s_sqrt1"); s_nb1 = sem("s_nb1")
        s_norm1 = sem("s_norm1"); s_trans1 = sem("s_trans1"); s_haev = sem("s_haev")
        s_bu = sem("s_bu"); s_buev = sem("s_buev"); s_iota = sem("s_iota")
        s_wexp = sem("s_wexp")
        s_bp = sem("s_bp"); s_tmp = sem("s_tmp")
        s_gate = sem("s_gate"); s_sig = sem("s_sig")
        s_scan = sem("s_scan"); s_attn = sem("s_attn")
        s_cs = sem("s_cs"); s_cse = sem("s_cse"); s_cse0 = sem("s_cse0")
        s_xpadd = sem("s_xpadd")
        s_sumx2 = sem("s_sumx2")
        s_pro2 = sem("s_pro2"); s_modf = sem("s_modf"); d_wm2 = sem("d_wm2")
        s_cols2 = sem("s_cols2"); s_sqrt2 = sem("s_sqrt2"); s_nb2 = sem("s_nb2")
        s_norm2 = sem("s_norm2"); s_trans2 = sem("s_trans2"); s_h2ev = sem("s_h2ev")
        s_h2mx = sem("s_h2mx")
        s_silu = sem("s_silu"); s_sig0 = sem("s_sig0"); s_clamp0 = sem("s_clamp0")
        s_ln0 = sem("s_ln0"); s_ldbf = sem("s_ldbf")
        s_drow = sem("s_drow"); s_gab = sem("s_gab"); s_cwga = sem("s_cwga")
        s_dfm = sem("s_dfm")
        s_pro = sem("s_pro"); s_modev = sem("s_modev")
        s_w1m = sem("s_w1m"); s_w2m = sem("s_w2m")
        s_sabf = sem("s_sabf"); s_bu0 = sem("s_bu0"); s_gb2 = sem("s_gb2")
        s_bu0ev = sem("s_bu0ev"); s_gb2ev = sem("s_gb2ev")
        s_bwca = sem("s_bwca"); s_gwca = sem("s_gwca")
        s_ldbc = sem("s_ldbc"); s_ldbcm = sem("s_ldbcm"); s_csa = sem("s_csa")
        s_bx2w = sem("s_bx2w"); s_w2sc = sem("s_w2sc")
        s_bx1v = sem("s_bx1v"); s_bx1t = sem("s_bx1t"); s_bx1s = sem("s_bx1s")
        s_bx1g = sem("s_bx1g")
        s_h2rnd = sem("s_h2rnd")
        s_w1rnd = sem("s_w1rnd"); s_f1 = sem("s_f1")
        s_gelu = sem("s_gelu"); s_h1mx = sem("s_h1mx")
        s_bx2v = sem("s_bx2v"); s_bx2t = sem("s_bx2t"); s_bx2s = sem("s_bx2s")
        s_bx2g = sem("s_bx2g")
        s_w2rnd = sem("s_w2rnd"); s_h1rnd = sem("s_h1rnd")
        s_f2 = sem("s_f2"); s_f2b = sem("s_f2b")
        s_epi = sem("s_epi"); s_gbm = sem("s_gbm"); s_oadd = sem("s_oadd")
        s_b2s = sem("s_b2s")

        DG_FIX = NJ + 3
        DG_WM0 = DG_FIX
        DG_DROW = DG_WM0 + 6 * NJ
        DG_FM_END = DG_DROW + 2 + 4

        with nc.Block("ph1") as blk:

            @blk.sync
            def _(sy):
                for dst, srct in ((Texcl, Texcl_h), (ident, ident_h), (identf, identf_h),
                                  (onesr, onesr_h), (onesrf, onesrf_h), (onesc, onesc_h),
                                  (onesA, onesA_h), (onesB, onesB_h),
                                  (ldrow_f, logdec_h), (cfm, cfm_h), (b1f, b1f_h),
                                  (epsc, epsc_h)):
                    sy.dma_start(out=dst[:, :], in_=srct[:, :]).then_inc(d_sm, 16)
                for i in range(2):
                    sy.dma_start(out=xbuf[:, i % 6, :],
                                 in_=x_h[i * 128:(i + 1) * 128, :]).then_inc(d_x, 16)
                idx = 0
                for n2 in (0, 1):
                    for q in range(2):
                        for j in range(NJ):
                            if idx >= 4:
                                sy.wait_ge(s_pro, idx - 3)
                            sy.dma_start(out=wm_bf[:, idx % 4, :],
                                         in_=bass.AP(wmod_h.tensor,
                                                     j * 128 * 6 * D + n2 * 1024
                                                     + q * 512,
                                                     [[6 * D, 128], [1, 512]])
                                         ).then_inc(d_wm, 16)
                            idx += 1
                for s in range(NT + 10):
                    i = s
                    if 2 <= i < NT:
                        sy.wait_ge(s_sumx, i - 1)
                        if i >= 6:
                            sy.wait_ge(s_xpadd, i - 5)
                        sy.dma_start(out=xbuf[:, i % 6, :],
                                     in_=x_h[i * 128:(i + 1) * 128, :]).then_inc(d_x, 16)
                    i = s - 5
                    if 0 <= i < NT:
                        sy.wait_ge(s_xpadd, i + 1)
                        if i >= 2:
                            sy.wait_ge(d_xpst if i % 2 == 0 else d_xps2,
                                       16 * (i // 2))
                        sy.dma_start(out=xp_h[i * 128:(i + 1) * 128, :],
                                     in_=xp_buf[:, i % 3, :]
                                     ).then_inc(d_xpst if i % 2 == 0 else d_xps2, 16)

            @blk.gpsimd
            def _(g):
                for j in range(NJ):
                    g.dma_start(out=gwT[:, j, :],
                                in_=gwT_h[j * 128:(j + 1) * 128, :]).then_inc(d_gw, 16)
                g.dma_start(out=bwT[:, :, :],
                            in_=bass.AP(bwT_h.tensor, 0,
                                        [[S, 128], [128 * S, NJ], [1, S]])
                            ).then_inc(d_gw, 16)
                g.dma_start(out=gateb_bf[:, :], in_=gateb_h[:, :]).then_inc(d_gw, 16)
                g.dma_start(out=cw_bf[:, :], in_=cwT_h[:, :]).then_inc(d_gw, 16)
                g.wait_ge(s_modev, 4)
                g.dma_start(out=rowb_h[0:1, 0:2 * D],
                            in_=mod_b[:, :]).then_inc(d_fm, 16)
                g.wait_ge(d_fm, 16)
                for t_i, dst in ((0, sa_fm), (1, ca1_fm)):
                    g.dma_start(out=dst[:, :],
                                in_=bass.AP(rowb_h.tensor, t_i * D,
                                            [[1, 128], [128, NJ]]),
                                allow_slow_non_contiguous=True).then_inc(d_fm, 16)
                g.wait_ge(d_fm, 48)
                c_g.pre(); c_g(g.tensor_copy(sa_bf[:, :], sa_fm[:, :]), s_sabf)
                c_g.pre(); c_g(g.memset(C_f32[:, :], 0.0), s_cse0)
                for s in range(max(NT + 10, WMXN1 // 2 + 1)):
                    if s == 2:
                        g.wait_ge(s_modev, 8)
                        g.dma_start(out=rowb_h[0:1, 2 * D:4 * D],
                                    in_=mod_b[:, :]).then_inc(d_fm, 16)
                        g.wait_ge(d_fm, 64)
                        for t_i, dst in ((2, sbv_fm), (3, cb1_fm)):
                            g.dma_start(out=dst[:, :],
                                        in_=bass.AP(rowb_h.tensor, t_i * D,
                                                    [[1, 128], [128, NJ]]),
                                        allow_slow_non_contiguous=True
                                        ).then_inc(d_fm, 16)
                    i = s
                    if i < NT:
                        if i >= 4:
                            g.wait_ge(s_wexp, i - 3)
                        c_g.pre()
                        c_g(g.iota(tcol[:, i % 4:i % 4 + 1], [[1, 1]], base=i * 128,
                                   channel_multiplier=1,
                                   allow_small_or_imprecise_dtypes=True), s_iota)
                    for k in (2 * s, 2 * s + 1):
                        if k < WMXN1:
                            if k >= 2:
                                g.wait_ge(s_w1m, k - 1)
                            g.dma_start(out=wmx_buf[:, k % 2, :],
                                        in_=bass.AP(w1T_h.tensor, k * 128 * 512,
                                                    [[512, 128], [1, 512]])
                                        ).then_inc(d_wmx, 16)
                    i = s - 1
                    if 0 <= i < NT:
                        g.wait_ge(s_nb1, i + 1)
                        g.wait_ge(d_x, 32 * (i // 2 + 1))
                        if i >= 2:
                            g.wait_ge(s_trans1, i - 1)
                        cc = cols[:, i % 2, :]
                        c_g.pre()
                        c_g(g.tensor_scalar(xnorm_bf[:, i % 2, :], xbuf[:, i % 6, :],
                                            cc[:, 16:17], cc[:, 17:18],
                                            ALU.mult, ALU.add), s_norm1)
                    i = s - 7
                    if 0 <= i < NT:
                        g.wait_ge(s_nb2, i + 1)
                        if i >= 3:
                            g.wait_ge(s_trans2, i - 2)
                        cc = cols[:, i % 2, :]
                        c_g.pre()
                        c_g(g.tensor_scalar(xnorm2_bf[:, i % 3, :], xp_buf[:, i % 3, :],
                                            cc[:, 36:37], cc[:, 37:38],
                                            ALU.mult, ALU.add), s_norm2)
                    i = s - 4
                    if 0 <= i < NT:
                        g.wait_ge(s_attn, i + 1)
                        c_g.pre()
                        c_g(g.tensor_add(xp_buf[:, i % 3, :], xp_buf[:, i % 3, :],
                                         xbuf[:, i % 6, :]), s_xpadd)


            @blk.vector
            def _(v):
                v.wait_ge(s_sig0, 1)
                c_v.pre()
                c_v(v.tensor_scalar_max(ldrow_f[:, :], ldrow_f[:, :], 1e-38), s_clamp0)
                v.wait_ge(s_ln0, 1)
                c_v.pre()
                c_v(v.tensor_scalar_max(ldrow_f[:, :], ldrow_f[:, :], -87.0), s_ldbf)

                def clamp_pm(slot):
                    c_v.pre(); c_v(v.tensor_scalar(sgn_row[:, :], slot, 0.0, None,
                                                   ALU.is_ge))
                    c_v.pre(); c_v(v.tensor_scalar(sgn_row[:, :], sgn_row[:, :], 2.0,
                                                   -1.0, ALU.mult, ALU.add))
                    c_v.pre(); c_v(v.tensor_scalar(rscr[0:1, 0:D], slot, -1.0, None,
                                                   ALU.mult))
                    c_v.pre(); c_v(v.tensor_max(slot, slot, rscr[0:1, 0:D]))
                    c_v.pre(); c_v(v.tensor_scalar_max(slot, slot, 0.25))
                    c_v.pre(); c_v(v.tensor_mul(slot, slot, sgn_row[:, :]))

                v.wait_ge(s_modf, 2)
                clamp_pm(mod_f[:, 0:D])
                c_v.pre(); c_v(v.memset(scl[:, 15:16], 0.0), s_drow)
                v.wait_ge(s_gab, 1)
                v.wait_ge(d_gw, 16 * (NJ + 3))
                c_v.pre()
                c_v(v.scalar_tensor_tensor(cwga[:, :], cw_bf[:, :], 0.5,
                                           PG[0:64, 0:D], ALU.mult, ALU.mult), s_cwga)
                v.wait_ge(d_fm, 48)
                c_v.pre(); c_v(v.tensor_scalar_add(ca1_fm[:, :], ca1_fm[:, :], 1.0),
                               s_dfm)
                for s in range(max(NT + 10, WMXN1 // 2 + 1)):
                    if s == 4:
                        v.wait_ge(s_gb2, 2)
                        v.wait_ge(s_dfm, 1)
                        for j in range(NJ):
                            c_v.pre()
                            c_v(v.tensor_scalar(gwT[:, j, :], gwT[:, j, :],
                                                ca1_fm[:, j:j + 1], None, ALU.mult),
                                s_gwca if j == NJ - 1 else None)
                        v.wait_ge(s_bu0, 1)
                        for j in range(NJ):
                            c_v.pre()
                            c_v(v.tensor_scalar(bwT[:, j, :], bwT[:, j, :],
                                                ca1_fm[:, j:j + 1], None, ALU.mult),
                                s_bwca if j == NJ - 1 else None)
                    if s == 6:
                        v.wait_ge(s_modf, 4)
                        clamp_pm(mod_f[:, 0:D])
                        c_v.pre(); c_v(v.tensor_copy(gb_row[:, :], mod_f[:, 0:D]),
                                       s_drow)
                        v.wait_ge(d_fm, 16 * 6)
                        c_v.pre(); c_v(v.tensor_scalar_add(cb1_fm[:, :],
                                                           cb1_fm[:, :], 1.0), s_dfm)
                    i = s
                    if i < NT:
                        v.wait_ge(d_x, 32 * (i // 2 + 1))
                        cc = cols[:, i % 2, :]
                        c_v.pre()
                        c_v(v.bn_stats(cc[:, 0:6], xbuf[:, i % 6, 0:512]))
                        c_v.pre()
                        c_v(v.bn_stats(cc[:, 6:12], xbuf[:, i % 6, 512:1024]), s_sumx)
                        if i >= 2:
                            v.wait_ge(s_sqrt1, i - 1)
                        c_v.pre()
                        c_v(v.bn_aggr(cc[:, 12:14], cc[:, 0:12]), s_cols1)
                    i = s - 1
                    if 0 <= i < NT:
                        v.wait_ge(s_sqrt1, i + 1)
                        if i >= 2:
                            v.wait_ge(s_norm1, i - 1)
                        cc = cols[:, i % 2, :]
                        c_v.pre()
                        c_v(v.reciprocal(cc[:, 16:17], cc[:, 15:16]))
                        c_v.pre()
                        c_v(v.scalar_tensor_tensor(cc[:, 17:18], cc[:, 12:13], -1.0,
                                                   cc[:, 16:17], ALU.mult, ALU.mult),
                            s_nb1)
                    i = s - 4
                    if 0 <= i < NT:
                        v.wait_ge(s_bp, i + 1)
                        v.wait_ge(s_wexp, i + 1)
                        if i >= 2:
                            v.wait_ge(s_cs, i - 1)
                        c_v.pre()
                        c_v(v.tensor_mul(tmp_bf[:, i % 2, :], W_sb[:, i % 3, :],
                                         PBS[:, :]), s_tmp)
                    i = s - 5
                    if 0 <= i < NT:
                        v.wait_ge(s_xpadd, i + 1)
                        cc = cols[:, i % 2, :]
                        c_v.pre()
                        c_v(v.bn_stats(cc[:, 20:26], xp_buf[:, i % 3, 0:512]))
                        c_v.pre()
                        c_v(v.bn_stats(cc[:, 26:32], xp_buf[:, i % 3, 512:1024]),
                            s_sumx2)
                        if i >= 2:
                            v.wait_ge(s_sqrt2, i - 1)
                        c_v.pre()
                        c_v(v.bn_aggr(cc[:, 32:34], cc[:, 20:32]), s_cols2)
                    i = s - 4
                    if 0 <= i < NT:
                        v.wait_ge(s_scan, i + 1)
                        v.wait_ge(s_sig, i + 1)
                        if i >= 3:
                            v.wait_ge(s_norm2, i - 2)
                        if i >= 3:
                            v.wait_ge(d_xpst if (i - 3) % 2 == 0 else d_xps2,
                                      16 * ((i - 3) // 2 + 1))
                        c_v.pre()
                        c_v(v.scalar_tensor_tensor(xp_buf[:, i % 3, :],
                                                   gate_sb[:, i % 2, :], 1.0,
                                                   PBS[:, :], ALU.add, ALU.mult),
                            s_attn)
                        v.wait_ge(s_cs, i + 1)
                        if i == 0:
                            v.wait_ge(s_cse0, 1)
                        if i >= 1:
                            v.wait_ge(s_cse, i)
                        c_v.pre()
                        c_v(v.tensor_add(C_f32[:, 0:512], C_f32[:, 0:512], PCS[0:1, :]))
                        c_v.pre()
                        c_v(v.tensor_add(C_f32[:, 512:1024], C_f32[:, 512:1024],
                                         PCS[32:33, :]), s_csa)
                    i = s - 6
                    if 0 <= i < NT:
                        v.wait_ge(s_sqrt2, i + 1)
                        if i >= 2:
                            v.wait_ge(s_norm2, i - 1)
                        cc = cols[:, i % 2, :]
                        c_v.pre()
                        c_v(v.reciprocal(cc[:, 36:37], cc[:, 35:36]))
                        c_v.pre()
                        c_v(v.scalar_tensor_tensor(cc[:, 37:38], cc[:, 32:33], -1.0,
                                                   cc[:, 36:37], ALU.mult, ALU.mult),
                            s_nb2)
                    for k in (2 * s, 2 * s + 1):
                        if k < WMXN1:
                            v.wait_ge(d_wmx, 32 * (k // 2 + 1))
                            c_v.pre()
                            c_v(v.reduce_max(w1mxc[:, k:k + 1], wmx_buf[:, k % 2, :],
                                             AX, apply_absolute_value=True), s_w1m)
                    i = s - 10
                    if 0 <= i < NT:
                        v.wait_ge(s_h2ev, i + 1)
                        c_v.pre()
                        c_v(v.reduce_max(hmax2[:, 8 * i:8 * i + 8],
                                         h2q[:, :, i * 128:(i + 1) * 128], AX,
                                         apply_absolute_value=True), s_h2mx)

            @blk.scalar
            def _(sc):
                sc.wait_ge(d_sm, 16 * 12)
                c_s.pre()
                c_s(sc.activation(silu_f[:, :], cfm[:, :], AF.Silu), s_silu)
                c_s.pre()
                c_s(sc.activation(ldrow_f[:, :], ldrow_f[:, :], AF.Sigmoid), s_sig0)
                sc.wait_ge(s_clamp0, 1)
                c_s.pre()
                c_s(sc.activation(ldrow_f[:, :], ldrow_f[:, :], AF.Ln), s_ln0)
                psl = (PBS[0:1, 0:512], PBS[0:1, 512:1024],
                       PG[0:1, 0:512], PG[0:1, 512:1024])

                def modev(n512):
                    seg = n512 // 2
                    if n512 < 8:
                        sc.wait_ge(s_pro, 8 * (n512 + 1))
                    else:
                        sc.wait_ge(s_pro2, 16 * (n512 - 7))
                    if n512 == 4:
                        sc.wait_ge(d_fm, 16)
                    if n512 == 10:
                        sc.wait_ge(s_gab, 1)
                    c_s.pre()
                    if n512 < 8:
                        dst = mod_b[:, (seg % 2) * D + (n512 % 2) * 512:
                                    (seg % 2) * D + (n512 % 2) * 512 + 512]
                        psrc = psl[n512 % 4]
                        c_s(sc.activation(dst, psrc, AF.Identity), s_modev)
                    else:
                        dst = mod_f[:, (n512 % 2) * 512:(n512 % 2) * 512 + 512]
                        psrc = (PCS[0:1, 0:512] if n512 % 2 == 0
                                else PCS[32:33, 0:512])
                        c_s(sc.activation(dst, psrc, AF.Identity), s_modf)

                fidx = 0
                for n2 in (2, 5):
                    for q in range(2):
                        for q2 in range(2):
                            for j in range(NJ):
                                if fidx >= 4:
                                    sc.wait_ge(s_pro2, fidx - 3)
                                sc.dma_start(out=wmf[:, fidx % 4, :],
                                             in_=bass.AP(wmod_h.tensor,
                                                         j * 128 * 6 * D + n2 * 1024
                                                         + q * 512 + q2 * 256,
                                                         [[6 * D, 128], [1, 256]])
                                             ).then_inc(d_wmf, 16)
                                fidx += 1
                    if n2 == 2:
                        modev(8)
                        modev(9)
                for n512 in range(4):
                    modev(n512)
                modev(10)
                modev(11)
                sc.wait_ge(s_ldbcm, 1)
                c_s.pre()
                c_s(sc.activation(ld_bc[:, :], PG[:, :], AF.Identity), s_ldbc)
                idx = 32
                for n2 in (3, 4):
                    for q in range(2):
                        for j in range(NJ):
                            sc.wait_ge(s_pro, idx - 3)
                            sc.dma_start(out=wm_bf[:, idx % 4, :],
                                         in_=bass.AP(wmod_h.tensor,
                                                     j * 128 * 6 * D + n2 * 1024
                                                     + q * 512,
                                                     [[6 * D, 128], [1, 512]])
                                         ).then_inc(d_wm2, 16)
                            idx += 1
                    if n2 == 3:
                        modev(4)
                        modev(5)
                modev(6)
                modev(7)
                sc.wait_ge(s_bu0, 1)
                c_s.pre()
                c_s(sc.activation(bu0_bf[:, :], PCS[0:1, 0:64], AF.Identity), s_bu0ev)
                sc.wait_ge(s_gb2, 2)
                c_s.pre()
                c_s(sc.activation(gb2_bf[:, 0:512], PW[0:1, 0:512], AF.Identity))
                c_s.pre()
                c_s(sc.activation(gb2_bf[:, 512:1024], PW[32:33, 0:512], AF.Identity),
                    s_gb2ev)


                for s in range(NT + 10):
                    i = s - 1
                    if 0 <= i < NT:
                        sc.wait_ge(s_cols1, i + 1)
                        if i >= 2:
                            sc.wait_ge(s_norm1, i - 1)
                        cc = cols[:, i % 2, :]
                        c_s.pre()
                        c_s(sc.activation(cc[:, 15:16], cc[:, 13:14], AF.Sqrt,
                                          bias=epsc[:, 0:1]), s_sqrt1)
                    i = s - 6
                    if 0 <= i < NT:
                        sc.wait_ge(s_cols2, i + 1)
                        if i >= 2:
                            sc.wait_ge(s_norm2, i - 1)
                        cc = cols[:, i % 2, :]
                        c_s.pre()
                        c_s(sc.activation(cc[:, 35:36], cc[:, 33:34], AF.Sqrt,
                                          bias=epsc[:, 0:1]), s_sqrt2)
                    i = s - 4
                    if 0 <= i < NT:
                        sc.wait_ge(s_gate, i + 1)
                        if i >= 2:
                            sc.wait_ge(s_attn, i - 1)
                        c_s.pre()
                        c_s(sc.activation(gate_sb[:, i % 2, :], PG[:, :], AF.Tanh,
                                          scale=0.5), s_sig)
                    i = s - 2
                    if 0 <= i < NT:
                        sc.wait_ge(s_iota, i + 1)
                        if i == 0:
                            sc.wait_ge(s_ldbc, 1)
                        if i >= 3:
                            sc.wait_ge(s_tmp, i - 2)
                        c_s.pre()
                        c_s(sc.activation(W_sb[:, i % 3, :], ld_bc[:, :], AF.Exp,
                                          scale=tcol[:, i % 4:i % 4 + 1]), s_wexp)
                    i = s - 2
                    if 0 <= i < NT:
                        sc.wait_ge(s_trans1, i + 1)
                        if i >= 2:
                            sc.wait_ge(s_gate, i - 1)
                        c_s.pre()
                        c_s(sc.activation(ha_fm[:, i % 2, :], PT1[:, :], AF.Identity),
                            s_haev)
                    i = s - 3
                    if 0 <= i < NT:
                        sc.wait_ge(s_bu, i + 1)
                        if i >= 2:
                            sc.wait_ge(s_bp, i - 1)
                        c_s.pre()
                        c_s(sc.activation(bu_bf[:, i % 2, :],
                                          PW[0:64, 128 * (i % 2):128 * (i % 2) + 128],
                                          AF.Identity), s_buev)
                    i = s - 9
                    if 0 <= i < NT:
                        sc.wait_ge(s_trans2, i + 1)
                        if i == 0:
                            sc.wait_ge(s_dfm, 2)
                        for j in range(NJ):
                            c_s.pre()
                            c_s(sc.activation(h2q[:, j, i * 128:(i + 1) * 128],
                                              PT2[:, j * 128:(j + 1) * 128],
                                              AF.Identity, bias=sbv_fm[:, j:j + 1],
                                              scale=cb1_fm[:, j:j + 1]),
                                s_h2ev if j == NJ - 1 else None)
                    i = s - 4
                    if 0 <= i < NT:
                        sc.wait_ge(s_csa, i + 1)
                        c_s.pre()
                        c_s(sc.activation(C_sb[:, :], C_f32[:, :], AF.Identity), s_cse)

            @blk.tensor
            def _(t):
                t.wait_ge(s_silu, 1)
                psl = (PBS[0:1, 0:512], PBS[0:1, 512:1024],
                       PG[0:1, 0:512], PG[0:1, 512:1024])
                for i in range(32):
                    for fidx in (2 * i, 2 * i + 1):
                        seg = fidx // 32
                        q = (fidx // 16) % 2
                        q2 = (fidx // 8) % 2
                        j = fidx % 8
                        n512 = 2 * (4 + seg) + q
                        if fidx == 32:
                            t.wait_ge(s_modf, 2)
                        row = (PCS[0:1, 0:512] if n512 % 2 == 0
                               else PCS[32:33, 0:512])
                        t.wait_ge(d_wmf, min(64 * (fidx // 4 + 1), 16 * 64))
                        t.matmul(row[:, q2 * 256:q2 * 256 + 256],
                                 silu_f[:, j:j + 1], wmf[:, fidx % 4, :],
                                 start=(j == 0), stop=(j == NJ - 1)
                                 ).then_inc(s_pro2, 1)
                    idx = i
                    n512 = idx // 8
                    j = idx % 8
                    t.wait_ge(d_wm, min(64 * (idx // 4 + 1), 16 * 32))
                    t.matmul(psl[n512 % 4], silu_f[:, j:j + 1],
                             wm_bf[:, idx % 4, :],
                             start=(j == 0), stop=(j == NJ - 1)
                             ).then_inc(s_pro, 1)
                t.wait_ge(s_drow, 1)
                t.wait_ge(s_modev, 4)
                t.matmul(PG[0:64, 0:512], onesrf[0:1, 0:64],
                         mod_f[:, 0:512], start=True, stop=True)
                t.matmul(PG[0:64, 512:1024], onesrf[0:1, 0:64],
                         mod_f[:, 512:1024], start=True,
                         stop=True).then_inc(s_gab, 1)
                t.wait_ge(s_cwga, 1)
                t.wait_ge(s_ldbf, 1)
                t.matmul(PG[:, 0:512], onesrf[0:1, :], ldrow_f[:, 0:512],
                         start=True, stop=True)
                t.matmul(PG[:, 512:1024], onesrf[0:1, :], ldrow_f[:, 512:1024],
                         start=True, stop=True).then_inc(s_ldbcm, 1)
                for idx in range(32, 64):
                    n512 = idx // 8
                    j = idx % 8
                    if idx == 32:
                        t.wait_ge(s_modev, 2)
                    if idx == 48:
                        t.wait_ge(s_modev, 4)
                        t.wait_ge(s_ldbc, 1)
                    t.wait_ge(d_wm2, min(64 * ((idx - 32) // 4 + 1), 16 * 32))
                    t.matmul(psl[n512 % 4], silu_f[:, j:j + 1],
                             wm_bf[:, idx % 4, :],
                             start=(j == 0), stop=(j == NJ - 1)
                             ).then_inc(s_pro, 1)
                t.wait_ge(s_sabf, 1)
                t.wait_ge(s_modf, 4)
                t.wait_ge(d_gw, 16 * (NJ + 3))
                last = None
                for j in range(NJ):
                    last = t.matmul(PCS[0:1, 0:64], sa_bf[:, j:j + 1], bwT[:, j, :],
                                    start=(j == 0), stop=(j == NJ - 1))
                last.then_inc(s_bu0, 1)
                for nh in range(2):
                    prow = PW[0:1, 0:512] if nh == 0 else PW[32:33, 0:512]
                    for j in range(NJ):
                        t.matmul(prow, sa_bf[:, j:j + 1],
                                 gwT[:, j, nh * 512:(nh + 1) * 512],
                                 start=(j == 0), stop=False)
                    t.matmul(prow, onesr[0:1, 0:1],
                             gateb_bf[:, nh * 512:(nh + 1) * 512],
                             start=False, stop=True).then_inc(s_gb2, 1)


                for s in range(NT + 10):
                    i = s - 9
                    if 0 <= i < NT:
                        t.wait_ge(s_norm2, i + 1)
                        if i >= 1:
                            t.wait_ge(s_h2ev, i)
                        last = None
                        for j in range(NJ):
                            last = t.transpose(PT2[:, j * 128:(j + 1) * 128],
                                               xnorm2_bf[:, i % 3,
                                                         j * 128:(j + 1) * 128],
                                               ident[:, :])
                        last.then_inc(s_trans2, 1)
                    i = s - 4
                    if 0 <= i < NT:
                        if i == 0:
                            t.wait_ge(s_cwga, 1)
                            t.wait_ge(s_modev, 6)
                        if i >= 1:
                            t.wait_ge(s_attn, i)
                        t.wait_ge(s_buev, i + 1)
                        last = None
                        for nh in range(2):
                            last = t.matmul(PBS[:, nh * 512:(nh + 1) * 512],
                                            bu_bf[:, i % 2, :],
                                            cwga[:, nh * 512:(nh + 1) * 512],
                                            start=True, stop=True)
                        last.then_inc(s_bp, 1)
                    i = s - 2
                    if 0 <= i < NT:
                        t.wait_ge(s_norm1, i + 1)
                        if i >= 1:
                            t.wait_ge(s_haev, i)
                        last = None
                        for j in range(NJ):
                            last = t.transpose(PT1[:, j * 128:(j + 1) * 128],
                                               xnorm_bf[:, i % 2, j * 128:(j + 1) * 128],
                                               ident[:, :])
                        last.then_inc(s_trans1, 1)
                    i = s - 3
                    if 0 <= i < NT:
                        t.wait_ge(s_haev, i + 1)
                        if i == 0:
                            t.wait_ge(s_bwca, 1)
                            t.wait_ge(s_bu0ev, 1)
                            t.wait_ge(s_gb2ev, 1)
                        if i >= 2:
                            t.wait_ge(s_buev, i - 1)
                        reg = PW[0:64, 128 * (i % 2):128 * (i % 2) + 128]
                        for j in range(NJ):
                            t.matmul(reg, bwT[:, j, :],
                                     ha_fm[:, i % 2, j * 128:(j + 1) * 128],
                                     start=(j == 0), stop=False)
                        t.matmul(reg, bu0_bf[:, :], onesr[0:1, :],
                                 start=False, stop=True).then_inc(s_bu, 1)
                    i = s - 4
                    if 0 <= i < NT:
                        t.wait_ge(s_tmp, i + 1)
                        if i >= 1:
                            t.wait_ge(s_cse, i)
                        last = None
                        for nh in range(2):
                            last = t.matmul(PBS[:, nh * 512:(nh + 1) * 512],
                                            Texcl[:, :],
                                            tmp_bf[:, i % 2, nh * 512:(nh + 1) * 512],
                                            start=True, stop=(i == 0))
                            if i > 0:
                                last = t.matmul(PBS[:, nh * 512:(nh + 1) * 512],
                                                onesr[:, :],
                                                C_sb[:, nh * 512:(nh + 1) * 512],
                                                start=False, stop=True)
                        last.then_inc(s_scan, 1)
                        if i == 0:
                            t.wait_ge(s_bu0ev, 1)
                        if i >= 1:
                            t.wait_ge(s_csa, i)
                        t.matmul(PCS[0:33, :], onesA[:, :], tmp_bf[:, i % 2, 0:512],
                                 start=True, stop=False)
                        t.matmul(PCS[0:33, :], onesB[:, :], tmp_bf[:, i % 2, 512:1024],
                                 start=False, stop=True).then_inc(s_cs, 1)
                    i = s - 3
                    if 0 <= i < NT:
                        if i == 0:
                            t.wait_ge(s_gwca, 1)
                            t.wait_ge(s_modev, 8)
                        if i >= 1:
                            t.wait_ge(s_sig, i)
                        for j in range(NJ):
                            for nh in range(2):
                                t.matmul(PG[:, nh * 512:(nh + 1) * 512],
                                         ha_fm[:, i % 2, j * 128:(j + 1) * 128],
                                         gwT[:, j, nh * 512:(nh + 1) * 512],
                                         start=(j == 0), stop=False)
                        last = None
                        for nh in range(2):
                            last = t.matmul(PG[:, nh * 512:(nh + 1) * 512],
                                            onesr[:, :],
                                            gb2_bf[:, nh * 512:(nh + 1) * 512],
                                            start=False, stop=True)
                        last.then_inc(s_gate, 1)

        pro.close()
        ph1t.close()
        w1f_buf = sbuf2("w1f_buf", [128, NJ, 512], F32)
        wmx2 = sbuf2("wmx2", [128, 2, 1024], F32)
        w1q = sbuf2("w1q", [128, 2, NJ, 512], BF16)
        h1st = sbuf2("h1st", [128, 4, 4, 512], BF16)

        with nc.Block("bx1") as blk:

            @blk.vector
            def _(v):
                c_v.pre(); c_v(v.reduce_max(cols[:, 0, 0:1], hmax2[:, :], AX))
                c_v.pre(); c_v(v.reduce_max(cols[:, 0, 1:2], w1mxc[:, :], AX)
                               , s_bx1v)
                v.wait_ge(s_bx1t, 1)
                c_v.pre(); c_v(v.reduce_max(scl[:, 2:3], PG[0:1, 0:128], AX))
                c_v.pre(); c_v(v.reduce_max(scl[:, 0:1], PG[0:1, 128:256], AX))
                c_v.pre(); c_v(v.tensor_scalar_max(scl[:, 2:3], scl[:, 2:3], 1e-6))
                c_v.pre(); c_v(v.memset(ccb[:, :], 0.0))
                c_v.pre(); c_v(v.tensor_copy(ccb[:, 0:1], scl[:, 2:3]), s_bx1v)
                v.wait_ge(s_bx1g, 2)
                c_v.pre(); c_v(v.tensor_scalar_max(scl[:, 3:4], ccb[:, 0:1], 1e-6))
                c_v.pre(); c_v(v.tensor_scalar_max(scl[:, 0:1], scl[:, 0:1], 1e-6))
                c_v.pre(); c_v(v.reciprocal(scl[:, 4:5], scl[:, 3:4]))
                c_v.pre(); c_v(v.tensor_scalar_mul(scl[:, 4:5], scl[:, 4:5], 127.0))
                c_v.pre(); c_v(v.reciprocal(scl[:, 5:6], scl[:, 0:1]))
                c_v.pre(); c_v(v.tensor_scalar_mul(scl[:, 5:6], scl[:, 5:6], 127.0))
                c_v.pre(); c_v(v.tensor_mul(scl[:, 6:7], scl[:, 0:1], scl[:, 3:4]))
                c_v.pre(); c_v(v.tensor_scalar_mul(scl[:, 6:7], scl[:, 6:7],
                                                   1.0 / (127.0 * 127.0))
                               , s_bx1v)
                v.wait_ge(s_bx1s, 1)
                c_v.pre()
                c_v(v.tensor_scalar(b1s[:, :], b1f[:, :], s1_c[:, 0:1], None,
                                    ALU.mult), s_bx1v)

            @blk.tensor
            def _(t):
                t.wait_ge(s_bx1v, 1)
                t.transpose(PG[0:1, 0:128], cols[:, 0, 0:1], identf[:, :])
                t.transpose(PG[0:1, 128:256], cols[:, 0, 1:2], identf[:, :]).then_inc(s_bx1t, 1)
                t.wait_ge(s_bx1v, 3)
                t.matmul(PW[:, 0:1], onesrf[0:1, :], scl[:, 4:5], start=True, stop=False)
                t.matmul(PW[:, 1:2], onesrf[0:1, :], scl[:, 5:6], start=False, stop=False)
                t.matmul(PW[:, 2:3], onesrf[0:1, :], scl[:, 6:7], start=False,
                         stop=True).then_inc(s_bx1t, 1)

            @blk.scalar
            def _(sc):
                sc.wait_ge(s_bx1t, 2)
                c_s.pre(); c_s(sc.activation(invsx1_c[:, :], PW[:, 0:1], AF.Identity))
                c_s.pre(); c_s(sc.activation(invsw1_c[:, :], PW[:, 1:2], AF.Identity))
                c_s.pre(); c_s(sc.activation(s1_c[:, :], PW[:, 2:3], AF.Identity)
                               , s_bx1s)

            @blk.gpsimd
            def _(g):
                g.wait_ge(s_bx1v, 2)
                g.dma_start(out=cc1i.ap()[:, :], in_=ccb[:, :]).then_inc(d_cc, 16)
                g.wait_ge(d_cc, 16)
                g.collective_compute(
                    "AllReduce", ALU.max,
                    replica_groups=[list(range(n_cores))],
                    ins=[cc1i.ap().opt()], outs=[cc1o.ap().opt()],
                ).then_inc(s_bx1g, 1)
                g.wait_ge(s_bx1g, 1)
                g.dma_start(out=ccb[:, :], in_=cc1o.ap()[:, :]).then_inc(d_cc, 16)
                g.wait_ge(d_cc, 32)
                c_g.pre(); c_g(g.engine_nop(), s_bx1g)

        p1.close()
        F1 = [psum2("F1_%d" % k, [128, 512]) for k in range(8)]
        SETA = tuple(F1[k][:, :] for k in range(4))
        SETB = tuple(F1[k][:, :] for k in range(4, 8))

        with nc.Block("ph2") as blk:

            @blk.sync
            def _(sy):
                sy.wait_ge(d_xpst, 16 * (NT - NT // 2))
                sy.wait_ge(d_xps2, 16 * (NT // 2))
                def h1store(gt):
                    gg2, tt2 = gt // NT5, gt % NT5
                    sy.wait_ge(s_gelu, gt + 1)
                    sy.wait_ge(s_h1mx, gt + 1)
                    sy.dma_start(
                        out=bass.AP(h1_h.tensor,
                                    (tt2 * NM + gg2 * 4) * 128 * 512,
                                    [[512, 128], [128 * 512, 4], [1, 512]]),
                        in_=h1st[:, gt % 4, :, :]).then_inc(d_h1st, 16)
                for gg in range(NG):
                    for gq in range(2):
                        if gg >= 1:
                            sy.wait_ge(s_w1rh, 2 * (gg - 1) + gq + 1)
                        sy.dma_start(out=w1f_buf[:, 4 * gq:4 * gq + 4, :],
                                     in_=bass.AP(w1T_h.tensor,
                                                 gg * 512 + 4 * gq * 128 * H,
                                                 [[H, 128], [128 * H, 4], [1, 512]])
                                     ).then_inc(d_w1 if gq == 0 else d_w1b, 16)
                    if gg >= 1:
                        for tt in range(NT5):
                            h1store((gg - 1) * NT5 + tt)
                    for kk in range(4 * gg, 4 * gg + 4):
                        if kk >= 2:
                            sy.wait_ge(s_w2m, kk - 1)
                        sy.dma_start(out=wmx2[:, kk % 2, :],
                                     in_=bass.AP(w2T_h.tensor, kk * 128 * 1024,
                                                 [[1024, 128], [1, 1024]])
                                     ).then_inc(d_wmx2, 16)
                for tt in range(NT5):
                    h1store((NG - 1) * NT5 + tt)

            @blk.vector
            def _(v):
                def w1round(gg):
                    v.wait_ge(s_bx1s, 1)
                    if gg >= 2:
                        v.wait_ge(s_f1, NT5 * (gg - 1))
                    for hh in range(4):
                        if hh == 0:
                            v.wait_ge(d_w1, 16 * (gg + 1))
                        if hh == 2:
                            v.wait_ge(d_w1b, 16 * (gg + 1))
                        c_v.pre()
                        c_v(v.tensor_scalar(rscr[:, :],
                                            w1f_buf[:, 2 * hh:2 * (hh + 1), :],
                                            invsw1_c[:, 0:1], MAGIC, ALU.mult, ALU.add))
                        c_v.pre()
                        c_v(v.tensor_scalar(w1q[:, gg % 2, 2 * hh:2 * (hh + 1), :],
                                            rscr[:, :], MAGIC, None, ALU.subtract),
                            s_w1rh if hh in (1, 3) else None)
                    c_v.pre()
                    c_v(v.memset(scl[:, 15:16], 0.0), s_w1rnd)
                def h2rnd(tg):
                    v.wait_ge(s_bx1s, 1)
                    for j in range(NJ):
                        c_v.pre()
                        c_v(v.tensor_scalar(rscr[:, 0:512],
                                            h2q[:, j, tg * 512:(tg + 1) * 512],
                                            invsx1_c[:, 0:1], MAGIC, ALU.mult, ALU.add))
                        c_v.pre()
                        c_v(v.tensor_scalar(h2q[:, j, tg * 512:(tg + 1) * 512],
                                            rscr[:, 0:512], MAGIC, None, ALU.subtract)
                            , s_h2rnd)
                h2rnd(0)
                w1round(0)
                if NT5 > 1:
                    h2rnd(1)
                for gg in range(NG):
                    for kk in range(4 * gg, 4 * gg + 4):
                        v.wait_ge(d_wmx2, 32 * (kk // 2 + 1))
                        c_v.pre()
                        c_v(v.reduce_max(w2mxc[:, kk:kk + 1], wmx2[:, kk % 2, :], AX,
                                         apply_absolute_value=True), s_w2m)
                    if gg + 1 < NG:
                        w1round(gg + 1)
                    for tt in range(NT5):
                        gt = gg * NT5 + tt
                        if gg == 0 and tt + 2 < NT5:
                            h2rnd(tt + 2)
                        v.wait_ge(s_gelu, gt + 1)
                        for m in range(4):
                            c_v.pre()
                            c_v(v.reduce_max(
                                hmax1[:, 4 * gt + m:4 * gt + m + 1],
                                h1st[:, gt % 4, m, :], AX,
                                apply_absolute_value=True),
                                s_h1mx if m == 3 else None)

            @blk.tensor
            def _(t):
                for gg in range(NG):
                    t.wait_ge(s_w1rnd, gg + 1)
                    for tt in range(NT5):
                        gt = gg * NT5 + tt
                        pset = SETA if gt % 2 == 0 else SETB
                        t.wait_ge(s_h2rnd, NJ * (tt + 1))
                        if gt >= 2:
                            t.wait_ge(s_gelu, gt - 1)
                        last = None
                        for m in range(4):
                            for j in range(NJ):
                                last = t.matmul(
                                    pset[m],
                                    w1q[:, gg % 2, j, m * 128:(m + 1) * 128],
                                    h2q[:, j, tt * 512:(tt + 1) * 512],
                                    start=(j == 0), stop=(j == NJ - 1))
                        last.then_inc(s_f1, 1)

            @blk.scalar
            def _(sc):
                for gg in range(NG):
                    for tt in range(NT5):
                        gt = gg * NT5 + tt
                        pset = SETA if gt % 2 == 0 else SETB
                        sc.wait_ge(s_f1, gt + 1)
                        if gt >= 4:
                            sc.wait_ge(d_h1st, 64 * (gt // 4))
                            sc.wait_ge(s_h1mx, gt - 3)
                        for m in range(4):
                            c_s.pre()
                            c_s(sc.activation(
                                h1st[:, gt % 4, m, :], pset[m], AF.Gelu,
                                bias=b1s[:, gg * 4 + m:gg * 4 + m + 1],
                                scale=s1_c[:, 0:1]),
                                s_gelu if m == 3 else None)


        ph2s.close()
        s12.close()
        w2f_buf = sbuf3("w2f_buf", [128, 4, D], F32)
        b2row = sbuf3("b2row", [1, D], F32)
        b2s_bf = sbuf3("b2s_bf", [1, D], BF16)
        w2qT = sbuf3("w2qT", [128, NM, D], BF16)
        h1buf = sbuf3("h1buf", [128, NM // 2, 2, 512], BF16)
        xp_t = sbuf3("xp_t", [128, 2, 4, D], F32)
        epi = sbuf3("epi", [128, 4, D], F32)
        gb_rep = sbuf3("gb_rep", [128, D], F32)

        with nc.Block("bx2") as blk:

            @blk.vector
            def _(v):
                v.wait_ge(d_xpst, 16 * (NT - NT // 2))
                v.wait_ge(d_xps2, 16 * (NT // 2))
                v.wait_ge(d_h1st, 16 * NG * NT5)
                c_v.pre()
                c_v(v.reduce_max(cols[:, 0, 0:1], hmax1[:, :], AX))
                c_v.pre()
                c_v(v.reduce_max(cols[:, 0, 2:3], w2mxc[:, :], AX), s_bx2v)
                v.wait_ge(s_bx2t, 1)
                c_v.pre(); c_v(v.reduce_max(scl[:, 1:2], F1[0][0:1, 128:256], AX))
                c_v.pre(); c_v(v.tensor_scalar_max(scl[:, 1:2], scl[:, 1:2], 1e-6))
                c_v.pre(); c_v(v.reciprocal(scl[:, 11:12], scl[:, 1:2]))
                c_v.pre(); c_v(v.tensor_scalar_mul(scl[:, 11:12], scl[:, 11:12], 127.0)
                               , s_bx2w)
                c_v.pre(); c_v(v.reduce_max(scl[:, 8:9], F1[0][0:1, 0:128], AX))
                c_v.pre(); c_v(v.tensor_scalar_max(scl[:, 8:9], scl[:, 8:9], 1e-6))
                c_v.pre(); c_v(v.memset(ccb[:, :], 0.0))
                c_v.pre(); c_v(v.tensor_copy(ccb[:, 0:1], scl[:, 8:9]), s_bx2v)
                v.wait_ge(s_bx2g, 2)
                c_v.pre(); c_v(v.tensor_scalar_max(scl[:, 9:10], ccb[:, 0:1], 1e-6))
                c_v.pre(); c_v(v.reciprocal(scl[:, 10:11], scl[:, 9:10]))
                c_v.pre(); c_v(v.tensor_scalar_mul(scl[:, 10:11], scl[:, 10:11], 127.0))
                c_v.pre(); c_v(v.tensor_mul(scl[:, 12:13], scl[:, 1:2], scl[:, 9:10]))
                c_v.pre(); c_v(v.tensor_scalar_mul(scl[:, 12:13], scl[:, 12:13],
                                                   1.0 / (127.0 * 127.0))
                               , s_bx2v)
                v.wait_ge(s_bx2s, 1)
                v.wait_ge(d_sm, 16 * 13)
                c_v.pre()
                c_v(v.tensor_scalar(b2s_bf[:, :], b2row[:, :], scl[:, 12:13], None,
                                    ALU.mult), s_b2s)

            @blk.tensor
            def _(t):
                t.wait_ge(s_bx2v, 1)
                t.transpose(F1[0][0:1, 0:128], cols[:, 0, 0:1], identf[:, :])
                t.transpose(F1[0][0:1, 128:256], cols[:, 0, 2:3],
                            identf[:, :]).then_inc(s_bx2t, 1)
                t.wait_ge(s_bx2w, 1)
                t.matmul(F1[1][:, 1:2], onesrf[0:1, :], scl[:, 11:12],
                         start=True, stop=True).then_inc(s_bx2t, 1)
                t.wait_ge(s_bx2v, 3)
                t.matmul(F1[1][:, 0:1], onesrf[0:1, :], scl[:, 10:11], start=True, stop=False)
                t.matmul(F1[1][:, 2:3], onesrf[0:1, :], scl[:, 12:13], start=False, stop=True)
                t.wait_ge(s_drow, 2)
                t.matmul(F1[2][:, :], onesrf[0:1, :], gb_row[:, 0:512],
                         start=True, stop=True)
                t.matmul(F1[3][:, :], onesrf[0:1, :], gb_row[:, 512:1024],
                         start=True, stop=True).then_inc(s_bx2t, 1)

            @blk.scalar
            def _(sc):
                sc.wait_ge(s_bx2t, 2)
                c_s.pre(); c_s(sc.activation(invsw2_c[:, :], F1[1][:, 1:2], AF.Identity)
                               , s_w2sc)
                sc.wait_ge(s_bx2t, 3)
                c_s.pre(); c_s(sc.activation(invsx2_c[:, :], F1[1][:, 0:1], AF.Identity))
                c_s.pre(); c_s(sc.activation(s2_c[:, :], F1[1][:, 2:3], AF.Identity))
                c_s.pre(); c_s(sc.activation(gb_rep[:, 0:512], F1[2][:, :], AF.Identity))
                c_s.pre(); c_s(sc.activation(gb_rep[:, 512:1024], F1[3][:, :], AF.Identity)
                               , s_bx2s)

            @blk.gpsimd
            def _(g):
                g.wait_ge(s_bx2v, 2)
                g.dma_start(out=cc2i.ap()[:, :], in_=ccb[:, :]).then_inc(d_cc, 16)
                g.wait_ge(d_cc, 48)
                g.collective_compute(
                    "AllReduce", ALU.max,
                    replica_groups=[list(range(n_cores))],
                    ins=[cc2i.ap().opt()], outs=[cc2o.ap().opt()],
                ).then_inc(s_bx2g, 1)
                g.wait_ge(s_bx2g, 1)
                g.dma_start(out=ccb[:, :], in_=cc2o.ap()[:, :]).then_inc(d_cc, 16)
                g.wait_ge(d_cc, 64)
                c_g.pre(); c_g(g.engine_nop(), s_bx2g)

            @blk.sync
            def _(sy):
                sy.wait_ge(d_xpst, 16 * (NT - NT // 2))
                sy.wait_ge(d_xps2, 16 * (NT // 2))
                sy.wait_ge(d_h1st, 16 * NG * NT5)
                sy.dma_start(out=b2row[:, :], in_=b2_h[:, :]).then_inc(d_sm, 16)
                for half in range(2):
                    for gq in range(4):
                        sy.dma_start(
                            out=h1buf[:, 4 * gq:4 * gq + 4, half, :],
                            in_=bass.AP(h1_h.tensor,
                                        (half * 16 + 4 * gq) * 128 * 512,
                                        [[512, 128], [128 * 512, 4], [1, 512]])
                            ).then_inc(d_h1ld if half == 0 else d_h1lb, 16)
                sy.dma_start(out=xp_t[:, 0, :, :],
                             in_=bass.AP(xp_h.tensor, 0,
                                         [[D, 128], [128 * D, 4], [1, D]])
                             ).then_inc(d_xpld, 16)
                for k in range(4):
                    sy.dma_start(out=w2f_buf[:, k % 4, :],
                                 in_=w2T_h[k * 128:(k + 1) * 128, :]).then_inc(d_w2, 16)

        p2.close()
        Q = [psum3("Q_%d" % q, [128, D]) for q in range(4)]

        with nc.Block("ph3") as blk:

            @blk.sync
            def _(sy):
                sy.wait_ge(d_xpst, 16 * (NT - NT // 2))
                sy.wait_ge(d_xps2, 16 * (NT // 2))
                for tt in range(1, NT5):
                    for half in range(2):
                        hh = 2 * tt + half
                        if hh >= 2:
                            sy.wait_ge(s_h1rnd, hh - 1)
                        sy.wait_ge(s_f2, 64 * (tt - 1) + 32 * (half + 1))
                        for gq in range(4):
                            sy.dma_start(
                                out=h1buf[:, 4 * gq:4 * gq + 4, half, :],
                                in_=bass.AP(h1_h.tensor,
                                            (tt * NM + half * 16 + 4 * gq) * 128 * 512,
                                            [[512, 128], [128 * 512, 4], [1, 512]])
                                ).then_inc(d_h1ld if half == 0 else d_h1lb, 16)
                sy.wait_ge(d_out, 16 * NT5)

            @blk.vector
            def _(v):
                w2k = [0]

                def w2rnd(n):
                    for k in range(w2k[0], min(w2k[0] + n, NM)):
                        if k == 0:
                            v.wait_ge(s_w2sc, 1)
                        v.wait_ge(d_w2, min(64 * (k // 4 + 1), 16 * NM))
                        c_v.pre()
                        c_v(v.tensor_scalar(rscr[:, 0:1024], w2f_buf[:, k % 4, :],
                                            invsw2_c[:, 0:1], MAGIC, ALU.mult,
                                            ALU.add))
                        c_v.pre()
                        c_v(v.tensor_scalar(w2qT[:, k, :], rscr[:, 0:1024], MAGIC,
                                            None, ALU.subtract), s_w2rnd)
                    w2k[0] = min(w2k[0] + n, NM)

                for tt in range(NT5):
                    for half in range(2):
                        v.wait_ge(d_h1ld if half == 0 else d_h1lb, 64 * (tt + 1))
                        v.wait_ge(s_bx2v, 3)
                        for qq in range(8):
                            c_v.pre()
                            c_v(v.tensor_scalar(
                                rscr[:, :],
                                h1buf[:, 2 * qq:2 * (qq + 1), half, :],
                                invsx2_c[:, 0:1], MAGIC, ALU.mult, ALU.add))
                            c_v.pre()
                            c_v(v.tensor_scalar(
                                h1buf[:, 2 * qq:2 * (qq + 1), half, :],
                                rscr[:, :], MAGIC, None, ALU.subtract))
                        c_v.pre()
                        c_v(v.memset(scl[:, 15:16], 0.0)
                            , s_h1rnd)
                        w2rnd(8)
                    if tt == 0:
                        w2rnd(NM)


            @blk.scalar
            def _(sc):
                for k in range(4, NM):
                    sc.wait_ge(s_w2rnd, k - 3)
                    sc.dma_start(out=w2f_buf[:, k % 4, :],
                                 in_=w2T_h[k * 128:(k + 1) * 128, :]).then_inc(d_w2, 16)
                for tt in range(NT5):
                    if tt >= 1:
                        sc.wait_ge(d_xpld, 16 * tt)
                        if tt >= 2:
                            sc.wait_ge(s_oadd, 4 * (tt - 1))
                            sc.wait_ge(d_out, 16 * (tt - 1))
                        sc.dma_start(out=xp_t[:, tt % 2, :, :],
                                     in_=bass.AP(xp_h.tensor, tt * 512 * D,
                                                 [[D, 128], [128 * D, 4], [1, D]])
                                     ).then_inc(d_xpld, 16)
                    if tt >= 1:
                        sc.wait_ge(s_oadd, 4 * tt)
                        sc.dma_start(out=bass.AP(out_h.tensor, (tt - 1) * 512 * D,
                                                 [[D, 128], [128 * D, 4], [1, D]]),
                                     in_=xp_t[:, (tt - 1) % 2, :, :]).then_inc(d_out, 16)
                    for q in range(4):
                        tq = 4 * tt + q
                        sc.wait_ge(s_f2b, 2 * tt + (1 if q < 2 else 2))
                        if tq >= 4:
                            sc.wait_ge(s_oadd, tq - 3)
                        c_s.pre()
                        c_s(sc.activation(epi[:, q, :], Q[q][:, :], AF.Identity,
                                          scale=s2_c[:, 0:1]), s_epi)
                sc.wait_ge(s_oadd, 4 * NT5)
                sc.dma_start(out=bass.AP(out_h.tensor, (NT5 - 1) * 512 * D,
                                         [[D, 128], [128 * D, 4], [1, D]]),
                             in_=xp_t[:, (NT5 - 1) % 2, :, :]).then_inc(d_out, 16)

            @blk.gpsimd
            def _(g):
                for tt in range(NT5):
                    for q in range(4):
                        tq = 4 * tt + q
                        g.wait_ge(s_epi, tq + 1)
                        if tq >= 2:
                            g.wait_ge(s_oadd, tq - 1)
                        if tq == 0:
                            g.wait_ge(s_bx2s, 1)
                        c_g.pre()
                        c_g(g.tensor_mul(epi[:, q, :], epi[:, q, :],
                                         gb_rep[:, :]), s_gbm)
                        g.wait_ge(d_xpld, 16 * (tt + 1))
                        c_g.pre()
                        c_g(g.tensor_add(xp_t[:, tt % 2, q, :], xp_t[:, tt % 2, q, :],
                                         epi[:, q, :]), s_oadd)

            @blk.tensor
            def _(t):
                for tt in range(NT5):
                    for ph, (qs, half) in enumerate((((0, 1), 0), ((2, 3), 0),
                                                     ((0, 1), 1), ((2, 3), 1))):
                        t.wait_ge(s_h1rnd, 2 * tt + half + 1)
                        if ph == 0:
                            t.wait_ge(s_b2s, 1)
                            if tt >= 1:
                                t.wait_ge(s_epi, 4 * (tt - 1) + 2)
                        if ph == 1 and tt >= 1:
                            t.wait_ge(s_epi, 4 * tt)
                        for kk in range(16):
                            k = half * 16 + kk
                            if tt == 0:
                                t.wait_ge(s_w2rnd, k + 1)
                            last = None
                            for q in qs:
                                t.matmul(Q[q][:, 0:512],
                                         h1buf[:, kk, half, :][:, q * 128:(q + 1) * 128],
                                         w2qT[:, k, 0:512], start=(k == 0), stop=False)
                                last = t.matmul(Q[q][:, 512:1024],
                                                h1buf[:, kk, half, :][:, q * 128:(q + 1) * 128],
                                                w2qT[:, k, 512:1024],
                                                start=(k == 0), stop=False)
                            last.then_inc(s_f2, 1)
                        if ph >= 2:
                            last = None
                            for q in qs:
                                t.matmul(Q[q][:, 0:512], onesr[:, :], b2s_bf[:, 0:512],
                                         start=False, stop=True)
                                last = t.matmul(Q[q][:, 512:1024], onesr[:, :],
                                                b2s_bf[:, 512:1024],
                                                start=False, stop=True)
                            last.then_inc(s_f2b, 1)

        s3.close()
        p3.close()

    return nc


def prep_inputs(inputs, n_cores=8):
    L = inputs["L0_attn_x"].shape[1]
    D = 1024
    NM = 32
    x = np.ascontiguousarray(np.asarray(inputs["L0_attn_x"], dtype=np.float32))
    B = x.shape[0]
    shared = {
        "bwT": np.ascontiguousarray(np.asarray(inputs["L0_attn_B_w"]).T),
        "cwT": np.ascontiguousarray(np.asarray(inputs["L0_attn_C_w"]).T),
        "gwT": np.ascontiguousarray(np.asarray(inputs["L0_attn_gate_w"]).T),
        "gate_b": np.asarray(inputs["L0_attn_gate_b"]).reshape(1, D),
        "log_dec": np.asarray(inputs["L0_attn_log_decay"]).reshape(1, D),
        "w1T": np.ascontiguousarray(np.asarray(inputs["L0_ffn_w1"]).T),
        "b1f": np.ascontiguousarray(np.asarray(inputs["L0_ffn_b1"]).reshape(NM, 128).T),
        "w2T": np.ascontiguousarray(np.asarray(inputs["L0_ffn_w2"]).T),
        "b2": np.asarray(inputs["L0_ffn_b2"]).reshape(1, D),
        "wmod": np.ascontiguousarray(np.asarray(inputs["L0_wmod"])),
    }
    shared = {k: v.astype(np.float32) for k, v in shared.items()}
    c = np.asarray(inputs["L0_c"], dtype=np.float32)
    in_maps = []
    for b in range(n_cores):
        m = dict(shared)
        m["x"] = np.ascontiguousarray(x[b % B])
        m["c_fm"] = np.ascontiguousarray(c[b % B].reshape(8, 128).T)
        in_maps.append(m)
    return in_maps




def kernel(**inputs):
    from concourse.bass_utils import run_bass_kernel_spmd
    n_cores = 8
    nc = build_nc(L=4096, n_cores=n_cores)
    in_maps = prep_inputs(inputs, n_cores=n_cores)
    res = run_bass_kernel_spmd(nc, in_maps, core_ids=list(range(n_cores)))
    return np.stack([np.asarray(r["out"]) for r in res.results])


# version 28
# speedup vs baseline: 1.0298x; 1.0298x over previous
import contextlib
import numpy as np
import concourse.bass as bass
import concourse.mybir as mybir

F32 = mybir.dt.float32
F32R = mybir.dt.float32r
BF16 = mybir.dt.bfloat16
AF = mybir.ActivationFunctionType
ALU = mybir.AluOpType
AX = mybir.AxisListType.X
MAGIC = float(np.float32(1.5 * 2 ** 23))
NPBF = mybir.dt.np(BF16)


class Ch:

    def __init__(self, eng, sem_h):
        self.eng, self.sem = eng, sem_h
        self.cnt = {}
        self.last = None

    def __call__(self, inst, purpose=None):
        s = purpose if purpose is not None else self.sem
        k = id(s)
        self.cnt[k] = self.cnt.get(k, 0) + 1
        inst.then_inc(s, 1)
        self.last = (s, self.cnt[k])
        return inst

    def pre(self):
        if self.last is not None:
            self.eng.wait_ge(self.last[0], self.last[1])


def build_nc(L=4096, n_cores=8, dbg=False):
    D, H, S = 1024, 4096, 64
    NT = L // 128
    NT5 = L // 512
    NJ = D // 128
    NM = H // 128
    NG = NM // 4
    WMXN1 = (D * H) // (128 * 512)
    WMXN2 = (H * D) // (128 * 1024)
    MODN = (6 * D) // 512

    nc = bass.Bass("TRN2", target_bir_lowering=False, debug=dbg)

    x_h = nc.dram_tensor("x", [L, D], F32, kind="ExternalInput").ap()
    bwT_h = nc.dram_tensor("bwT", [D, S], F32, kind="ExternalInput").ap()
    cwT_h = nc.dram_tensor("cwT", [S, D], F32, kind="ExternalInput").ap()
    gwT_h = nc.dram_tensor("gwT", [D, D], F32, kind="ExternalInput").ap()
    gateb_h = nc.dram_tensor("gate_b", [1, D], F32, kind="ExternalInput").ap()
    logdec_h = nc.dram_tensor("log_dec", [1, D], F32, kind="ExternalInput").ap()
    w1T_h = nc.dram_tensor("w1T", [D, H], F32, kind="ExternalInput").ap()
    b1f_h = nc.dram_tensor("b1f", [128, NM], F32, kind="ExternalInput").ap()
    w2T_h = nc.dram_tensor("w2T", [H, D], F32, kind="ExternalInput").ap()
    b2_h = nc.dram_tensor("b2", [1, D], F32, kind="ExternalInput").ap()
    cfm_h = nc.dram_tensor("c_fm", [128, NJ], F32, kind="ExternalInput").ap()
    wmod_h = nc.dram_tensor("wmod", [D, 6 * D], F32, kind="ExternalInput").ap()
    out_h = nc.dram_tensor("out", [L, D], F32, kind="ExternalOutput").ap()

    xp_h = nc.dram_tensor("xp_scr", [L, D], F32).ap()
    h1_h = nc.dram_tensor("h1_scr", [NT5 * NM * 128, 512], BF16).ap()
    rowb_h = nc.dram_tensor("rowb_scr", [1, 8 * D], BF16).ap()
    cc1i = nc.dram_tensor("cc1_in", [1, 16], F32)
    cc1o = nc.dram_tensor("cc1_out", [1, 16], F32)
    cc2i = nc.dram_tensor("cc2_in", [1, 16], F32)
    cc2o = nc.dram_tensor("cc2_out", [1, 16], F32)

    t_excl_np = np.triu(np.ones((128, 128), dtype=np.float32), 1).astype(NPBF)
    Texcl_h = nc.inline_tensor(t_excl_np, "Texcl_c").ap()
    ident_h = nc.inline_tensor(np.eye(128, dtype=np.float32).astype(NPBF), "ident_c").ap()
    identf_h = nc.inline_tensor(np.eye(128, dtype=np.float32), "identf_c").ap()
    onesr_h = nc.inline_tensor(np.ones((1, 128), dtype=NPBF), "onesr_c").ap()
    onesrf_h = nc.inline_tensor(np.ones((1, 128), dtype=np.float32), "onesrf_c").ap()
    onesc_h = nc.inline_tensor(np.ones((128, 1), dtype=NPBF), "onesc_c").ap()
    oA = np.zeros((128, 33), dtype=np.float32); oA[:, 0] = 1.0
    oB = np.zeros((128, 33), dtype=np.float32); oB[:, 32] = 1.0
    onesA_h = nc.inline_tensor(oA.astype(NPBF), "onesA_c").ap()
    onesB_h = nc.inline_tensor(oB.astype(NPBF), "onesB_c").ap()
    epsc_h = nc.inline_tensor(np.full((128, 1), 1e-5, dtype=np.float32),
                              "epsc_c").ap()

    stk = contextlib.ExitStack()
    s12 = contextlib.ExitStack()
    p1 = contextlib.ExitStack()
    p2 = contextlib.ExitStack()
    p3 = contextlib.ExitStack()
    ph1t = contextlib.ExitStack()
    pro = contextlib.ExitStack()
    ph2s = contextlib.ExitStack()
    s3 = contextlib.ExitStack()

    def _mk(stack):
        def sbufF(name, shape, dt):
            return stack.enter_context(nc.sbuf_tensor(name, shape, dt))
        return sbufF

    sbuf = _mk(stk)
    sbuf12 = _mk(s12)
    sbuf1 = _mk(ph1t)
    sbufP = _mk(pro)
    sbuf2 = _mk(ph2s)
    sbuf3 = _mk(s3)

    def psum1(name, shape, dt=F32):
        return p1.enter_context(nc.psum_tensor(name, shape, dt))

    def psum2(name, shape, dt=F32):
        return p2.enter_context(nc.psum_tensor(name, shape, dt))

    def psum3(name, shape, dt=F32):
        return p3.enter_context(nc.psum_tensor(name, shape, dt))

    def sem(name):
        return stk.enter_context(nc.semaphore(name))

    with stk:
        Texcl = sbuf("Texcl_s", [128, 128], BF16)
        ident = sbuf("ident_s", [128, 128], BF16)
        identf = sbuf("identf_s", [128, 128], F32)
        onesr = sbuf("onesr_s", [1, 128], BF16)
        onesrf = sbuf("onesrf_s", [1, 128], F32)
        onesc = sbuf("onesc_s", [128, 1], BF16)
        onesA = sbuf("onesA_s", [128, 33], BF16)
        onesB = sbuf("onesB_s", [128, 33], BF16)
        gateb_bf = sbuf("gateb_bf", [1, D], BF16)
        gb2_bf = sbuf("gb2_bf", [1, D], BF16)
        bu0_bf = sbuf("bu0_bf", [1, 64], BF16)
        sa_bf = sbuf("sa_bf", [128, NJ], BF16)
        ca1_fm = sbuf("ca1_fm", [128, NJ], F32)
        sa_fm = sbuf("sa_fm", [128, NJ], F32)
        cb1_fm = sbuf("cb1_fm", [128, NJ], F32)
        sbv_fm = sbuf("sbv_fm", [128, NJ], F32)
        gb_row = sbuf("gb_row", [1, D], F32)
        b1f = sbuf("b1f_s", [128, NM], F32)
        b1s = sbuf("b1s", [128, NM], F32)
        C_sb = sbuf("C_sb", [1, D], BF16)
        C_f32 = sbuf("C_f32", [1, D], F32)
        hmax2 = sbuf("hmax2", [128, 8 * NT], F32)
        hmax1 = sbuf("hmax1", [128, 4 * NG * NT5], F32)
        w1mxc = sbuf("w1mxc", [128, WMXN1], F32)
        w2mxc = sbuf("w2mxc", [128, WMXN2], F32)
        scl = sbuf("scl", [1, 16], F32)
        invsw1_c = sbuf("invsw1_c", [128, 1], F32)
        invsx1_c = sbuf("invsx1_c", [128, 1], F32)
        s1_c = sbuf("s1_c", [128, 1], F32)
        invsw2_c = sbuf("invsw2_c", [128, 1], F32)
        invsx2_c = sbuf("invsx2_c", [128, 1], F32)
        s2_c = sbuf("s2_c", [128, 1], F32)
        ccb = sbuf("ccb", [1, 16], F32)
        epsc = sbuf("epsc", [128, 1], F32)

        cols = sbuf("cols", [128, 2, 40], F32)
        rscr = sbuf("rscr", [128, 1024], F32)
        h2q = sbuf12("h2q", [128, NJ, L], BF16)
        gwT = sbuf12("gwT_s", [128, NJ, D], BF16)
        bwT = sbuf12("bwT_s", [128, NJ, S], BF16)
        cwga = sbuf12("cwga", [64, D], BF16)
        xbuf = sbuf1("xbuf", [128, 6, D], F32)
        xnorm_bf = sbuf1("xnorm_bf", [128, 2, D], BF16)
        xnorm2_bf = sbuf1("xnorm2_bf", [128, 3, D], BF16)
        ha_fm = sbuf1("ha_fm", [128, 2, D], BF16)
        bu_bf = sbuf1("bu_bf", [64, 2, 128], BF16)
        W_sb = sbuf1("W_sb", [128, 3, D], BF16)
        ld_bc = sbuf1("ld_bc", [128, D], F32)
        tmp_bf = sbuf1("tmp_bf", [128, 2, D], BF16)
        gate_sb = sbuf1("gate_sb", [128, 2, D], BF16)
        xp_buf = sbuf1("xp_buf", [128, 3, D], F32)
        tcol = sbuf1("tcol", [128, 4], F32)
        wmx_buf = sbufP("wmx_buf", [128, 2, 512], F32)
        wm_bf = sbufP("wm_bf", [128, 4, 512], BF16)
        silu_bf = sbufP("silu_bf", [128, NJ], BF16)
        silu_f = sbufP("silu_f", [128, NJ], F32)
        wmf = sbufP("wmf", [128, 4, 256], F32)
        cfm = sbufP("cfm_s", [128, NJ], F32)
        mod_b = sbufP("mod_b", [1, 2 * D], BF16)
        mod_f = sbufP("mod_f", [1, D], F32)
        sgn_row = sbufP("sgn_row", [1, D], BF16)
        ldrow_f = sbufP("ldrow_f", [1, D], F32)
        cw_bf = sbufP("cw_bf", [64, D], BF16)

        PT1 = psum1("PT1", [128, D], BF16)
        PW = psum1("PW", [128, 512])
        PBS = psum1("PBS", [128, D])
        PT2 = psum1("PT2", [128, D], BF16)
        PCS = psum1("PCS", [64, 512])
        PG = psum1("PG", [128, D])

        d_sm = sem("d_sm"); d_x = sem("d_x"); d_xpst = sem("d_xpst")
        d_xps2 = sem("d_xps2")
        d_wmx = sem("d_wmx"); d_w1 = sem("d_w1"); d_w2 = sem("d_w2")
        d_w1b = sem("d_w1b"); s_w1rh = sem("s_w1rh"); d_h1lb = sem("d_h1lb")
        d_h1st = sem("d_h1st"); d_h1ld = sem("d_h1ld"); d_xpld = sem("d_xpld")
        d_out = sem("d_out"); d_gw = sem("d_gw"); d_wm = sem("d_wm")
        d_fm = sem("d_fm"); d_cc = sem("d_cc"); d_wmf = sem("d_wmf")
        d_wmx2 = sem("d_wmx2")
        c_v = Ch(nc.vector, sem("c_v"))
        c_s = Ch(nc.scalar, sem("c_s"))
        c_g = Ch(nc.gpsimd, sem("c_g"))
        s_sumx = sem("s_sumx")
        s_cols1 = sem("s_cols1"); s_sqrt1 = sem("s_sqrt1"); s_nb1 = sem("s_nb1")
        s_norm1 = sem("s_norm1"); s_trans1 = sem("s_trans1"); s_haev = sem("s_haev")
        s_bu = sem("s_bu"); s_buev = sem("s_buev"); s_iota = sem("s_iota")
        s_wexp = sem("s_wexp")
        s_bp = sem("s_bp"); s_tmp = sem("s_tmp")
        s_gate = sem("s_gate"); s_sig = sem("s_sig")
        s_scan = sem("s_scan"); s_attn = sem("s_attn")
        s_cs = sem("s_cs"); s_cse = sem("s_cse"); s_cse0 = sem("s_cse0")
        s_xpadd = sem("s_xpadd")
        s_sumx2 = sem("s_sumx2")
        s_pro2 = sem("s_pro2"); s_modf = sem("s_modf")
        s_cols2 = sem("s_cols2"); s_sqrt2 = sem("s_sqrt2"); s_nb2 = sem("s_nb2")
        s_norm2 = sem("s_norm2"); s_trans2 = sem("s_trans2"); s_h2ev = sem("s_h2ev")
        s_h2mx = sem("s_h2mx")
        s_silu = sem("s_silu"); s_sig0 = sem("s_sig0"); s_clamp0 = sem("s_clamp0")
        s_ln0 = sem("s_ln0"); s_ldbf = sem("s_ldbf")
        s_drow = sem("s_drow"); s_gab = sem("s_gab"); s_cwga = sem("s_cwga")
        s_dfm = sem("s_dfm")
        s_pro = sem("s_pro"); s_modev = sem("s_modev")
        s_w1m = sem("s_w1m"); s_w2m = sem("s_w2m")
        s_sabf = sem("s_sabf"); s_bu0 = sem("s_bu0"); s_gb2 = sem("s_gb2")
        s_bu0ev = sem("s_bu0ev"); s_gb2ev = sem("s_gb2ev")
        s_bwca = sem("s_bwca"); s_gwca = sem("s_gwca")
        s_ldbc = sem("s_ldbc"); s_ldbcm = sem("s_ldbcm"); s_csa = sem("s_csa")
        s_bx2w = sem("s_bx2w"); s_w2sc = sem("s_w2sc")
        s_bx1v = sem("s_bx1v"); s_bx1t = sem("s_bx1t"); s_bx1s = sem("s_bx1s")
        s_bx1g = sem("s_bx1g")
        s_h2rnd = sem("s_h2rnd")
        s_w1rnd = sem("s_w1rnd"); s_f1 = sem("s_f1")
        s_gelu = sem("s_gelu"); s_h1mx = sem("s_h1mx")
        s_bx2v = sem("s_bx2v"); s_bx2t = sem("s_bx2t"); s_bx2s = sem("s_bx2s")
        s_bx2g = sem("s_bx2g")
        s_w2rnd = sem("s_w2rnd"); s_h1rnd = sem("s_h1rnd")
        s_f2 = sem("s_f2"); s_f2b = sem("s_f2b")
        s_epi = sem("s_epi"); s_gbm = sem("s_gbm"); s_oadd = sem("s_oadd")
        s_b2s = sem("s_b2s")

        DG_FIX = NJ + 3
        DG_WM0 = DG_FIX
        DG_DROW = DG_WM0 + 6 * NJ
        DG_FM_END = DG_DROW + 2 + 4

        with nc.Block("ph1") as blk:

            @blk.sync
            def _(sy):
                for dst, srct in ((Texcl, Texcl_h), (ident, ident_h), (identf, identf_h),
                                  (onesr, onesr_h), (onesrf, onesrf_h), (onesc, onesc_h),
                                  (onesA, onesA_h), (onesB, onesB_h),
                                  (ldrow_f, logdec_h), (cfm, cfm_h), (b1f, b1f_h),
                                  (epsc, epsc_h)):
                    sy.dma_start(out=dst[:, :], in_=srct[:, :]).then_inc(d_sm, 16)
                for s in range(NT + 10):
                    i = s
                    if i < NT:
                        if i >= 2:
                            sy.wait_ge(s_sumx, i - 1)
                        if i >= 6:
                            sy.wait_ge(s_xpadd, i - 5)
                        sy.dma_start(out=xbuf[:, i % 6, :],
                                     in_=x_h[i * 128:(i + 1) * 128, :]).then_inc(d_x, 16)
                    i = s - 5
                    if 0 <= i < NT:
                        sy.wait_ge(s_xpadd, i + 1)
                        if i >= 2:
                            sy.wait_ge(d_xpst if i % 2 == 0 else d_xps2,
                                       16 * (i // 2))
                        sy.dma_start(out=xp_h[i * 128:(i + 1) * 128, :],
                                     in_=xp_buf[:, i % 3, :]
                                     ).then_inc(d_xpst if i % 2 == 0 else d_xps2, 16)

            @blk.gpsimd
            def _(g):
                for j in range(NJ):
                    g.dma_start(out=gwT[:, j, :],
                                in_=gwT_h[j * 128:(j + 1) * 128, :]).then_inc(d_gw, 16)
                g.dma_start(out=bwT[:, :, :],
                            in_=bass.AP(bwT_h.tensor, 0,
                                        [[S, 128], [128 * S, NJ], [1, S]])
                            ).then_inc(d_gw, 16)
                g.dma_start(out=gateb_bf[:, :], in_=gateb_h[:, :]).then_inc(d_gw, 16)
                g.dma_start(out=cw_bf[:, :], in_=cwT_h[:, :]).then_inc(d_gw, 16)
                def wm_load(idx):
                    n2 = (0, 1, 3, 4)[idx // 16]
                    q = (idx // 8) % 2
                    j = idx % 8
                    if idx >= 4:
                        g.wait_ge(s_pro, idx - 3)
                    g.dma_start(out=wm_bf[:, idx % 4, :],
                                in_=bass.AP(wmod_h.tensor,
                                            j * 128 * 6 * D + n2 * 1024 + q * 512,
                                            [[6 * D, 128], [1, 512]])
                                ).then_inc(d_wm, 16)

                for idx in range(36):
                    wm_load(idx)
                g.wait_ge(s_modev, 4)
                g.dma_start(out=rowb_h[0:1, 0:2 * D],
                            in_=mod_b[:, :]).then_inc(d_fm, 16)
                g.wait_ge(d_fm, 16)
                for t_i, dst in ((0, sa_fm), (1, ca1_fm)):
                    g.dma_start(out=dst[:, :],
                                in_=bass.AP(rowb_h.tensor, t_i * D,
                                            [[1, 128], [128, NJ]]),
                                allow_slow_non_contiguous=True).then_inc(d_fm, 16)
                g.wait_ge(d_fm, 48)
                c_g.pre(); c_g(g.tensor_copy(sa_bf[:, :], sa_fm[:, :]), s_sabf)
                for idx in range(36, 64):
                    wm_load(idx)
                g.wait_ge(s_bu0, 1)
                g.wait_ge(s_dfm, 1)
                for j in range(NJ):
                    c_g.pre()
                    c_g(g.tensor_scalar(bwT[:, j, :], bwT[:, j, :],
                                        ca1_fm[:, j:j + 1], None, ALU.mult),
                        s_bwca if j == NJ - 1 else None)
                g.wait_ge(s_modev, 8)
                g.dma_start(out=rowb_h[0:1, 2 * D:4 * D],
                            in_=mod_b[:, :]).then_inc(d_fm, 16)
                g.wait_ge(d_fm, 64)
                for t_i, dst in ((2, sbv_fm), (3, cb1_fm)):
                    g.dma_start(out=dst[:, :],
                                in_=bass.AP(rowb_h.tensor, t_i * D,
                                            [[1, 128], [128, NJ]]),
                                allow_slow_non_contiguous=True).then_inc(d_fm, 16)
                c_g.pre(); c_g(g.memset(C_f32[:, :], 0.0), s_cse0)
                for s in range(max(NT + 10, WMXN1 // 2 + 1)):
                    i = s
                    if i < NT:
                        if i >= 4:
                            g.wait_ge(s_wexp, i - 3)
                        c_g.pre()
                        c_g(g.iota(tcol[:, i % 4:i % 4 + 1], [[1, 1]], base=i * 128,
                                   channel_multiplier=1,
                                   allow_small_or_imprecise_dtypes=True), s_iota)
                    for k in (2 * s, 2 * s + 1):
                        if k < WMXN1:
                            if k >= 2:
                                g.wait_ge(s_w1m, k - 1)
                            g.dma_start(out=wmx_buf[:, k % 2, :],
                                        in_=bass.AP(w1T_h.tensor, k * 128 * 512,
                                                    [[512, 128], [1, 512]])
                                        ).then_inc(d_wmx, 16)
                    i = s - 1
                    if 0 <= i < NT:
                        g.wait_ge(s_nb1, i + 1)
                        g.wait_ge(d_x, 32 * (i // 2 + 1))
                        if i >= 2:
                            g.wait_ge(s_trans1, i - 1)
                        cc = cols[:, i % 2, :]
                        c_g.pre()
                        c_g(g.tensor_scalar(xnorm_bf[:, i % 2, :], xbuf[:, i % 6, :],
                                            cc[:, 16:17], cc[:, 17:18],
                                            ALU.mult, ALU.add), s_norm1)
                    i = s - 7
                    if 0 <= i < NT:
                        g.wait_ge(s_nb2, i + 1)
                        if i >= 3:
                            g.wait_ge(s_trans2, i - 2)
                        cc = cols[:, i % 2, :]
                        c_g.pre()
                        c_g(g.tensor_scalar(xnorm2_bf[:, i % 3, :], xp_buf[:, i % 3, :],
                                            cc[:, 36:37], cc[:, 37:38],
                                            ALU.mult, ALU.add), s_norm2)
                    i = s - 4
                    if 0 <= i < NT:
                        g.wait_ge(s_attn, i + 1)
                        c_g.pre()
                        c_g(g.tensor_add(xp_buf[:, i % 3, :], xp_buf[:, i % 3, :],
                                         xbuf[:, i % 6, :]), s_xpadd)


            @blk.vector
            def _(v):
                v.wait_ge(s_sig0, 1)
                c_v.pre()
                c_v(v.tensor_scalar_max(ldrow_f[:, :], ldrow_f[:, :], 1e-38), s_clamp0)
                v.wait_ge(s_ln0, 1)
                c_v.pre()
                c_v(v.tensor_scalar_max(ldrow_f[:, :], ldrow_f[:, :], -87.0), s_ldbf)

                def clamp_pm(slot):
                    c_v.pre(); c_v(v.tensor_scalar(sgn_row[:, :], slot, 0.0, None,
                                                   ALU.is_ge))
                    c_v.pre(); c_v(v.tensor_scalar(sgn_row[:, :], sgn_row[:, :], 2.0,
                                                   -1.0, ALU.mult, ALU.add))
                    c_v.pre(); c_v(v.tensor_scalar(rscr[0:1, 0:D], slot, -1.0, None,
                                                   ALU.mult))
                    c_v.pre(); c_v(v.tensor_max(slot, slot, rscr[0:1, 0:D]))
                    c_v.pre(); c_v(v.tensor_scalar_max(slot, slot, 0.25))
                    c_v.pre(); c_v(v.tensor_mul(slot, slot, sgn_row[:, :]))

                v.wait_ge(s_modf, 2)
                clamp_pm(mod_f[:, 0:D])
                c_v.pre(); c_v(v.memset(scl[:, 15:16], 0.0), s_drow)
                v.wait_ge(s_modf, 4)
                clamp_pm(mod_f[:, 0:D])
                c_v.pre(); c_v(v.tensor_copy(gb_row[:, :], mod_f[:, 0:D]), s_drow)
                v.wait_ge(d_fm, 48)
                c_v.pre(); c_v(v.tensor_scalar_add(ca1_fm[:, :], ca1_fm[:, :], 1.0),
                               s_dfm)
                v.wait_ge(s_gab, 1)
                v.wait_ge(d_gw, 16 * (NJ + 3))
                c_v.pre()
                c_v(v.scalar_tensor_tensor(cwga[:, :], cw_bf[:, :], 0.5,
                                           PG[0:64, 0:D], ALU.mult, ALU.mult), s_cwga)
                v.wait_ge(s_gb2, 2)
                v.wait_ge(s_dfm, 1)
                for j in range(NJ):
                    c_v.pre()
                    c_v(v.tensor_scalar(gwT[:, j, :], gwT[:, j, :],
                                        ca1_fm[:, j:j + 1], None, ALU.mult),
                        s_gwca if j == NJ - 1 else None)

                for s in range(max(NT + 10, WMXN1 // 2 + 1)):
                    if s == 2:
                        v.wait_ge(d_fm, 16 * 6)
                        c_v.pre()
                        c_v(v.tensor_scalar_add(cb1_fm[:, :], cb1_fm[:, :], 1.0),
                            s_dfm)
                    i = s
                    if i < NT:
                        v.wait_ge(d_x, 32 * (i // 2 + 1))
                        cc = cols[:, i % 2, :]
                        c_v.pre()
                        c_v(v.bn_stats(cc[:, 0:6], xbuf[:, i % 6, 0:512]))
                        c_v.pre()
                        c_v(v.bn_stats(cc[:, 6:12], xbuf[:, i % 6, 512:1024]), s_sumx)
                        if i >= 2:
                            v.wait_ge(s_sqrt1, i - 1)
                        c_v.pre()
                        c_v(v.bn_aggr(cc[:, 12:14], cc[:, 0:12]), s_cols1)
                    i = s - 1
                    if 0 <= i < NT:
                        v.wait_ge(s_sqrt1, i + 1)
                        if i >= 2:
                            v.wait_ge(s_norm1, i - 1)
                        cc = cols[:, i % 2, :]
                        c_v.pre()
                        c_v(v.reciprocal(cc[:, 16:17], cc[:, 15:16]))
                        c_v.pre()
                        c_v(v.scalar_tensor_tensor(cc[:, 17:18], cc[:, 12:13], -1.0,
                                                   cc[:, 16:17], ALU.mult, ALU.mult),
                            s_nb1)
                    i = s - 4
                    if 0 <= i < NT:
                        v.wait_ge(s_bp, i + 1)
                        v.wait_ge(s_wexp, i + 1)
                        if i >= 2:
                            v.wait_ge(s_cs, i - 1)
                        c_v.pre()
                        c_v(v.tensor_mul(tmp_bf[:, i % 2, :], W_sb[:, i % 3, :],
                                         PBS[:, :]), s_tmp)
                    i = s - 5
                    if 0 <= i < NT:
                        v.wait_ge(s_xpadd, i + 1)
                        cc = cols[:, i % 2, :]
                        c_v.pre()
                        c_v(v.bn_stats(cc[:, 20:26], xp_buf[:, i % 3, 0:512]))
                        c_v.pre()
                        c_v(v.bn_stats(cc[:, 26:32], xp_buf[:, i % 3, 512:1024]),
                            s_sumx2)
                        if i >= 2:
                            v.wait_ge(s_sqrt2, i - 1)
                        c_v.pre()
                        c_v(v.bn_aggr(cc[:, 32:34], cc[:, 20:32]), s_cols2)
                    i = s - 4
                    if 0 <= i < NT:
                        v.wait_ge(s_scan, i + 1)
                        v.wait_ge(s_sig, i + 1)
                        if i >= 3:
                            v.wait_ge(s_norm2, i - 2)
                        if i >= 3:
                            v.wait_ge(d_xpst if (i - 3) % 2 == 0 else d_xps2,
                                      16 * ((i - 3) // 2 + 1))
                        c_v.pre()
                        c_v(v.scalar_tensor_tensor(xp_buf[:, i % 3, :],
                                                   gate_sb[:, i % 2, :], 1.0,
                                                   PBS[:, :], ALU.add, ALU.mult),
                            s_attn)
                        v.wait_ge(s_cs, i + 1)
                        if i == 0:
                            v.wait_ge(s_cse0, 1)
                        if i >= 1:
                            v.wait_ge(s_cse, i)
                        c_v.pre()
                        c_v(v.tensor_add(C_f32[:, 0:512], C_f32[:, 0:512], PCS[0:1, :]))
                        c_v.pre()
                        c_v(v.tensor_add(C_f32[:, 512:1024], C_f32[:, 512:1024],
                                         PCS[32:33, :]), s_csa)
                    i = s - 6
                    if 0 <= i < NT:
                        v.wait_ge(s_sqrt2, i + 1)
                        if i >= 2:
                            v.wait_ge(s_norm2, i - 1)
                        cc = cols[:, i % 2, :]
                        c_v.pre()
                        c_v(v.reciprocal(cc[:, 36:37], cc[:, 35:36]))
                        c_v.pre()
                        c_v(v.scalar_tensor_tensor(cc[:, 37:38], cc[:, 32:33], -1.0,
                                                   cc[:, 36:37], ALU.mult, ALU.mult),
                            s_nb2)
                    for k in (2 * s, 2 * s + 1):
                        if k < WMXN1:
                            v.wait_ge(d_wmx, 32 * (k // 2 + 1))
                            c_v.pre()
                            c_v(v.reduce_max(w1mxc[:, k:k + 1], wmx_buf[:, k % 2, :],
                                             AX, apply_absolute_value=True), s_w1m)
                    i = s - 10
                    if 0 <= i < NT:
                        v.wait_ge(s_h2ev, i + 1)
                        c_v.pre()
                        c_v(v.reduce_max(hmax2[:, 8 * i:8 * i + 8],
                                         h2q[:, :, i * 128:(i + 1) * 128], AX,
                                         apply_absolute_value=True), s_h2mx)

            @blk.scalar
            def _(sc):
                sc.wait_ge(d_sm, 16 * 12)
                c_s.pre()
                c_s(sc.activation(silu_bf[:, :], cfm[:, :], AF.Silu))
                c_s.pre()
                c_s(sc.activation(silu_f[:, :], cfm[:, :], AF.Silu), s_silu)
                c_s.pre()
                c_s(sc.activation(ldrow_f[:, :], ldrow_f[:, :], AF.Sigmoid), s_sig0)
                sc.wait_ge(s_clamp0, 1)
                c_s.pre()
                c_s(sc.activation(ldrow_f[:, :], ldrow_f[:, :], AF.Ln), s_ln0)
                psl = (PBS[0:1, 0:512], PBS[0:1, 512:1024],
                       PG[0:1, 0:512], PG[0:1, 512:1024])

                def modev(n512):
                    seg = n512 // 2
                    if n512 < 8:
                        sc.wait_ge(s_pro, 8 * (n512 + 1))
                    else:
                        sc.wait_ge(s_pro2, 16 * (n512 - 7))
                    if n512 == 4:
                        sc.wait_ge(d_fm, 16)
                    if n512 == 10:
                        sc.wait_ge(s_gab, 1)
                    c_s.pre()
                    if n512 < 8:
                        dst = mod_b[:, (seg % 2) * D + (n512 % 2) * 512:
                                    (seg % 2) * D + (n512 % 2) * 512 + 512]
                        c_s(sc.activation(dst, psl[n512 % 4], AF.Identity), s_modev)
                    else:
                        dst = mod_f[:, (n512 % 2) * 512:(n512 % 2) * 512 + 512]
                        psrc = (PCS[0:1, 0:512] if n512 % 2 == 0
                                else PCS[32:33, 0:512])
                        c_s(sc.activation(dst, psrc, AF.Identity), s_modf)

                fidx = 0
                for n2 in (2, 5):
                    for q in range(2):
                        for q2 in range(2):
                            for j in range(NJ):
                                if fidx >= 4:
                                    sc.wait_ge(s_pro2, fidx - 3)
                                sc.dma_start(out=wmf[:, fidx % 4, :],
                                             in_=bass.AP(wmod_h.tensor,
                                                         j * 128 * 6 * D + n2 * 1024
                                                         + q * 512 + q2 * 256,
                                                         [[6 * D, 128], [1, 256]])
                                             ).then_inc(d_wmf, 16)
                                fidx += 1
                    if n2 == 2:
                        modev(8)
                        modev(9)
                for n512 in range(4):
                    modev(n512)
                modev(10)
                modev(11)
                sc.wait_ge(s_ldbcm, 1)
                c_s.pre()
                c_s(sc.activation(ld_bc[:, :], PG[:, :], AF.Identity), s_ldbc)
                for n512 in range(4, 8):
                    modev(n512)
                sc.wait_ge(s_bu0, 1)
                c_s.pre()
                c_s(sc.activation(bu0_bf[:, :], PCS[0:1, 0:64], AF.Identity), s_bu0ev)
                sc.wait_ge(s_gb2, 2)
                c_s.pre()
                c_s(sc.activation(gb2_bf[:, 0:512], PW[0:1, 0:512], AF.Identity))
                c_s.pre()
                c_s(sc.activation(gb2_bf[:, 512:1024], PW[32:33, 0:512], AF.Identity),
                    s_gb2ev)


                for s in range(NT + 10):
                    i = s - 1
                    if 0 <= i < NT:
                        sc.wait_ge(s_cols1, i + 1)
                        if i >= 2:
                            sc.wait_ge(s_norm1, i - 1)
                        cc = cols[:, i % 2, :]
                        c_s.pre()
                        c_s(sc.activation(cc[:, 15:16], cc[:, 13:14], AF.Sqrt,
                                          bias=epsc[:, 0:1]), s_sqrt1)
                    i = s - 6
                    if 0 <= i < NT:
                        sc.wait_ge(s_cols2, i + 1)
                        if i >= 2:
                            sc.wait_ge(s_norm2, i - 1)
                        cc = cols[:, i % 2, :]
                        c_s.pre()
                        c_s(sc.activation(cc[:, 35:36], cc[:, 33:34], AF.Sqrt,
                                          bias=epsc[:, 0:1]), s_sqrt2)
                    i = s - 4
                    if 0 <= i < NT:
                        sc.wait_ge(s_gate, i + 1)
                        if i >= 2:
                            sc.wait_ge(s_attn, i - 1)
                        c_s.pre()
                        c_s(sc.activation(gate_sb[:, i % 2, :], PG[:, :], AF.Tanh,
                                          scale=0.5), s_sig)
                    i = s - 2
                    if 0 <= i < NT:
                        sc.wait_ge(s_iota, i + 1)
                        if i == 0:
                            sc.wait_ge(s_ldbc, 1)
                        if i >= 3:
                            sc.wait_ge(s_tmp, i - 2)
                        c_s.pre()
                        c_s(sc.activation(W_sb[:, i % 3, :], ld_bc[:, :], AF.Exp,
                                          scale=tcol[:, i % 4:i % 4 + 1]), s_wexp)
                    i = s - 2
                    if 0 <= i < NT:
                        sc.wait_ge(s_trans1, i + 1)
                        if i >= 2:
                            sc.wait_ge(s_gate, i - 1)
                        c_s.pre()
                        c_s(sc.activation(ha_fm[:, i % 2, :], PT1[:, :], AF.Identity),
                            s_haev)
                    i = s - 3
                    if 0 <= i < NT:
                        sc.wait_ge(s_bu, i + 1)
                        if i >= 2:
                            sc.wait_ge(s_bp, i - 1)
                        c_s.pre()
                        c_s(sc.activation(bu_bf[:, i % 2, :],
                                          PW[0:64, 128 * (i % 2):128 * (i % 2) + 128],
                                          AF.Identity), s_buev)
                    i = s - 9
                    if 0 <= i < NT:
                        sc.wait_ge(s_trans2, i + 1)
                        if i == 0:
                            sc.wait_ge(s_dfm, 2)
                        for j in range(NJ):
                            c_s.pre()
                            c_s(sc.activation(h2q[:, j, i * 128:(i + 1) * 128],
                                              PT2[:, j * 128:(j + 1) * 128],
                                              AF.Identity, bias=sbv_fm[:, j:j + 1],
                                              scale=cb1_fm[:, j:j + 1]),
                                s_h2ev if j == NJ - 1 else None)
                    i = s - 4
                    if 0 <= i < NT:
                        sc.wait_ge(s_csa, i + 1)
                        c_s.pre()
                        c_s(sc.activation(C_sb[:, :], C_f32[:, :], AF.Identity), s_cse)

            @blk.tensor
            def _(t):
                t.wait_ge(s_silu, 1)
                psl = (PBS[0:1, 0:512], PBS[0:1, 512:1024],
                       PG[0:1, 0:512], PG[0:1, 512:1024])

                def wm_mm(idx):
                    n512 = idx // 8
                    if idx == 32:
                        t.wait_ge(s_modev, 2)
                    if idx == 48:
                        t.wait_ge(s_modev, 4)
                        t.wait_ge(s_ldbc, 1)
                    t.wait_ge(d_wm, min(64 * (idx // 4 + 1), 16 * 64))
                    t.matmul(psl[n512 % 4], silu_bf[:, idx % 8:idx % 8 + 1],
                             wm_bf[:, idx % 4, :],
                             start=(idx % 8 == 0), stop=(idx % 8 == NJ - 1)
                             ).then_inc(s_pro, 1)

                for i in range(32):
                    for fidx in (2 * i, 2 * i + 1):
                        seg = fidx // 32
                        q = (fidx // 16) % 2
                        j = fidx % 8
                        n512 = 2 * (4 + seg) + q
                        if fidx == 32:
                            t.wait_ge(s_modf, 2)
                        row = (PCS[0:1, 0:512] if n512 % 2 == 0
                               else PCS[32:33, 0:512])
                        q2 = (fidx // 8) % 2
                        t.wait_ge(d_wmf, min(64 * (fidx // 4 + 1), 16 * 64))
                        t.matmul(row[:, q2 * 256:q2 * 256 + 256],
                                 silu_f[:, j:j + 1], wmf[:, fidx % 4, :],
                                 start=(j == 0), stop=(j == NJ - 1)
                                 ).then_inc(s_pro2, 1)
                    wm_mm(i)
                t.wait_ge(s_drow, 1)
                t.wait_ge(s_modev, 4)
                t.matmul(PG[0:64, 0:512], onesrf[0:1, 0:64],
                         mod_f[:, 0:512], start=True, stop=True)
                t.matmul(PG[0:64, 512:1024], onesrf[0:1, 0:64],
                         mod_f[:, 512:1024], start=True,
                         stop=True).then_inc(s_gab, 1)
                t.wait_ge(s_cwga, 1)
                t.wait_ge(s_ldbf, 1)
                t.matmul(PG[:, 0:512], onesrf[0:1, :], ldrow_f[:, 0:512],
                         start=True, stop=True)
                t.matmul(PG[:, 512:1024], onesrf[0:1, :], ldrow_f[:, 512:1024],
                         start=True, stop=True).then_inc(s_ldbcm, 1)
                for idx in range(32, 48):
                    wm_mm(idx)
                t.wait_ge(s_sabf, 1)
                t.wait_ge(s_modf, 4)
                t.wait_ge(d_gw, 16 * (NJ + 3))
                last = None
                for j in range(NJ):
                    last = t.matmul(PCS[0:1, 0:64], sa_bf[:, j:j + 1], bwT[:, j, :],
                                    start=(j == 0), stop=(j == NJ - 1))
                last.then_inc(s_bu0, 1)
                for nh in range(2):
                    prow = PW[0:1, 0:512] if nh == 0 else PW[32:33, 0:512]
                    for j in range(NJ):
                        t.matmul(prow, sa_bf[:, j:j + 1],
                                 gwT[:, j, nh * 512:(nh + 1) * 512],
                                 start=(j == 0), stop=False)
                    t.matmul(prow, onesr[0:1, 0:1],
                             gateb_bf[:, nh * 512:(nh + 1) * 512],
                             start=False, stop=True).then_inc(s_gb2, 1)
                for idx in range(48, 64):
                    wm_mm(idx)

                for s in range(NT + 10):
                    i = s - 9
                    if 0 <= i < NT:
                        t.wait_ge(s_norm2, i + 1)
                        if i >= 1:
                            t.wait_ge(s_h2ev, i)
                        last = None
                        for j in range(NJ):
                            last = t.transpose(PT2[:, j * 128:(j + 1) * 128],
                                               xnorm2_bf[:, i % 3,
                                                         j * 128:(j + 1) * 128],
                                               ident[:, :])
                        last.then_inc(s_trans2, 1)
                    i = s - 4
                    if 0 <= i < NT:
                        if i == 0:
                            t.wait_ge(s_cwga, 1)
                            t.wait_ge(s_modev, 6)
                        if i >= 1:
                            t.wait_ge(s_attn, i)
                        t.wait_ge(s_buev, i + 1)
                        last = None
                        for nh in range(2):
                            last = t.matmul(PBS[:, nh * 512:(nh + 1) * 512],
                                            bu_bf[:, i % 2, :],
                                            cwga[:, nh * 512:(nh + 1) * 512],
                                            start=True, stop=True)
                        last.then_inc(s_bp, 1)
                    i = s - 2
                    if 0 <= i < NT:
                        t.wait_ge(s_norm1, i + 1)
                        if i >= 1:
                            t.wait_ge(s_haev, i)
                        last = None
                        for j in range(NJ):
                            last = t.transpose(PT1[:, j * 128:(j + 1) * 128],
                                               xnorm_bf[:, i % 2, j * 128:(j + 1) * 128],
                                               ident[:, :])
                        last.then_inc(s_trans1, 1)
                    i = s - 3
                    if 0 <= i < NT:
                        t.wait_ge(s_haev, i + 1)
                        if i == 0:
                            t.wait_ge(s_bwca, 1)
                            t.wait_ge(s_bu0ev, 1)
                            t.wait_ge(s_gb2ev, 1)
                        if i >= 2:
                            t.wait_ge(s_buev, i - 1)
                        reg = PW[0:64, 128 * (i % 2):128 * (i % 2) + 128]
                        for j in range(NJ):
                            t.matmul(reg, bwT[:, j, :],
                                     ha_fm[:, i % 2, j * 128:(j + 1) * 128],
                                     start=(j == 0), stop=False)
                        t.matmul(reg, bu0_bf[:, :], onesr[0:1, :],
                                 start=False, stop=True).then_inc(s_bu, 1)
                    i = s - 4
                    if 0 <= i < NT:
                        t.wait_ge(s_tmp, i + 1)
                        if i >= 1:
                            t.wait_ge(s_cse, i)
                        last = None
                        for nh in range(2):
                            last = t.matmul(PBS[:, nh * 512:(nh + 1) * 512],
                                            Texcl[:, :],
                                            tmp_bf[:, i % 2, nh * 512:(nh + 1) * 512],
                                            start=True, stop=(i == 0))
                            if i > 0:
                                last = t.matmul(PBS[:, nh * 512:(nh + 1) * 512],
                                                onesr[:, :],
                                                C_sb[:, nh * 512:(nh + 1) * 512],
                                                start=False, stop=True)
                        last.then_inc(s_scan, 1)
                        if i == 0:
                            t.wait_ge(s_bu0ev, 1)
                        if i >= 1:
                            t.wait_ge(s_csa, i)
                        t.matmul(PCS[0:33, :], onesA[:, :], tmp_bf[:, i % 2, 0:512],
                                 start=True, stop=False)
                        t.matmul(PCS[0:33, :], onesB[:, :], tmp_bf[:, i % 2, 512:1024],
                                 start=False, stop=True).then_inc(s_cs, 1)
                    i = s - 3
                    if 0 <= i < NT:
                        if i == 0:
                            t.wait_ge(s_gwca, 1)
                            t.wait_ge(s_modev, 8)
                        if i >= 1:
                            t.wait_ge(s_sig, i)
                        for j in range(NJ):
                            for nh in range(2):
                                t.matmul(PG[:, nh * 512:(nh + 1) * 512],
                                         ha_fm[:, i % 2, j * 128:(j + 1) * 128],
                                         gwT[:, j, nh * 512:(nh + 1) * 512],
                                         start=(j == 0), stop=False)
                        last = None
                        for nh in range(2):
                            last = t.matmul(PG[:, nh * 512:(nh + 1) * 512],
                                            onesr[:, :],
                                            gb2_bf[:, nh * 512:(nh + 1) * 512],
                                            start=False, stop=True)
                        last.then_inc(s_gate, 1)

        pro.close()
        ph1t.close()
        w1f_buf = sbuf2("w1f_buf", [128, NJ, 512], F32)
        wmx2 = sbuf2("wmx2", [128, 2, 1024], F32)
        w1q = sbuf2("w1q", [128, 2, NJ, 512], BF16)
        h1st = sbuf2("h1st", [128, 4, 4, 512], BF16)

        with nc.Block("bx1") as blk:

            @blk.vector
            def _(v):
                c_v.pre(); c_v(v.reduce_max(cols[:, 0, 0:1], hmax2[:, :], AX))
                c_v.pre(); c_v(v.reduce_max(cols[:, 0, 1:2], w1mxc[:, :], AX)
                               , s_bx1v)
                v.wait_ge(s_bx1t, 1)
                c_v.pre(); c_v(v.reduce_max(scl[:, 2:3], PG[0:1, 0:128], AX))
                c_v.pre(); c_v(v.reduce_max(scl[:, 0:1], PG[0:1, 128:256], AX))
                c_v.pre(); c_v(v.tensor_scalar_max(scl[:, 2:3], scl[:, 2:3], 1e-6))
                c_v.pre(); c_v(v.memset(ccb[:, :], 0.0))
                c_v.pre(); c_v(v.tensor_copy(ccb[:, 0:1], scl[:, 2:3]), s_bx1v)
                v.wait_ge(s_bx1g, 2)
                c_v.pre(); c_v(v.tensor_scalar_max(scl[:, 3:4], ccb[:, 0:1], 1e-6))
                c_v.pre(); c_v(v.tensor_scalar_max(scl[:, 0:1], scl[:, 0:1], 1e-6))
                c_v.pre(); c_v(v.reciprocal(scl[:, 4:5], scl[:, 3:4]))
                c_v.pre(); c_v(v.tensor_scalar_mul(scl[:, 4:5], scl[:, 4:5], 127.0))
                c_v.pre(); c_v(v.reciprocal(scl[:, 5:6], scl[:, 0:1]))
                c_v.pre(); c_v(v.tensor_scalar_mul(scl[:, 5:6], scl[:, 5:6], 127.0))
                c_v.pre(); c_v(v.tensor_mul(scl[:, 6:7], scl[:, 0:1], scl[:, 3:4]))
                c_v.pre(); c_v(v.tensor_scalar_mul(scl[:, 6:7], scl[:, 6:7],
                                                   1.0 / (127.0 * 127.0))
                               , s_bx1v)
                v.wait_ge(s_bx1s, 1)
                c_v.pre()
                c_v(v.tensor_scalar(b1s[:, :], b1f[:, :], s1_c[:, 0:1], None,
                                    ALU.mult), s_bx1v)

            @blk.tensor
            def _(t):
                t.wait_ge(s_bx1v, 1)
                t.transpose(PG[0:1, 0:128], cols[:, 0, 0:1], identf[:, :])
                t.transpose(PG[0:1, 128:256], cols[:, 0, 1:2], identf[:, :]).then_inc(s_bx1t, 1)
                t.wait_ge(s_bx1v, 3)
                t.matmul(PW[:, 0:1], onesrf[0:1, :], scl[:, 4:5], start=True, stop=False)
                t.matmul(PW[:, 1:2], onesrf[0:1, :], scl[:, 5:6], start=False, stop=False)
                t.matmul(PW[:, 2:3], onesrf[0:1, :], scl[:, 6:7], start=False,
                         stop=True).then_inc(s_bx1t, 1)

            @blk.scalar
            def _(sc):
                sc.wait_ge(s_bx1t, 2)
                c_s.pre(); c_s(sc.activation(invsx1_c[:, :], PW[:, 0:1], AF.Identity))
                c_s.pre(); c_s(sc.activation(invsw1_c[:, :], PW[:, 1:2], AF.Identity))
                c_s.pre(); c_s(sc.activation(s1_c[:, :], PW[:, 2:3], AF.Identity)
                               , s_bx1s)

            @blk.gpsimd
            def _(g):
                g.wait_ge(s_bx1v, 2)
                g.dma_start(out=cc1i.ap()[:, :], in_=ccb[:, :]).then_inc(d_cc, 16)
                g.wait_ge(d_cc, 16)
                g.collective_compute(
                    "AllReduce", ALU.max,
                    replica_groups=[list(range(n_cores))],
                    ins=[cc1i.ap().opt()], outs=[cc1o.ap().opt()],
                ).then_inc(s_bx1g, 1)
                g.wait_ge(s_bx1g, 1)
                g.dma_start(out=ccb[:, :], in_=cc1o.ap()[:, :]).then_inc(d_cc, 16)
                g.wait_ge(d_cc, 32)
                c_g.pre(); c_g(g.engine_nop(), s_bx1g)

        p1.close()
        F1 = [psum2("F1_%d" % k, [128, 512]) for k in range(8)]
        SETA = tuple(F1[k][:, :] for k in range(4))
        SETB = tuple(F1[k][:, :] for k in range(4, 8))

        with nc.Block("ph2") as blk:

            @blk.sync
            def _(sy):
                sy.wait_ge(d_xpst, 16 * (NT - NT // 2))
                sy.wait_ge(d_xps2, 16 * (NT // 2))
                def h1store(gt):
                    gg2, tt2 = gt // NT5, gt % NT5
                    sy.wait_ge(s_gelu, gt + 1)
                    sy.wait_ge(s_h1mx, gt + 1)
                    sy.dma_start(
                        out=bass.AP(h1_h.tensor,
                                    (tt2 * NM + gg2 * 4) * 128 * 512,
                                    [[512, 128], [128 * 512, 4], [1, 512]]),
                        in_=h1st[:, gt % 4, :, :]).then_inc(d_h1st, 16)
                for gg in range(NG):
                    for gq in range(2):
                        if gg >= 1:
                            sy.wait_ge(s_w1rh, 2 * (gg - 1) + gq + 1)
                        sy.dma_start(out=w1f_buf[:, 4 * gq:4 * gq + 4, :],
                                     in_=bass.AP(w1T_h.tensor,
                                                 gg * 512 + 4 * gq * 128 * H,
                                                 [[H, 128], [128 * H, 4], [1, 512]])
                                     ).then_inc(d_w1 if gq == 0 else d_w1b, 16)
                    if gg >= 1:
                        for tt in range(NT5):
                            h1store((gg - 1) * NT5 + tt)
                    for kk in range(4 * gg, 4 * gg + 4):
                        if kk >= 2:
                            sy.wait_ge(s_w2m, kk - 1)
                        sy.dma_start(out=wmx2[:, kk % 2, :],
                                     in_=bass.AP(w2T_h.tensor, kk * 128 * 1024,
                                                 [[1024, 128], [1, 1024]])
                                     ).then_inc(d_wmx2, 16)
                for tt in range(NT5):
                    h1store((NG - 1) * NT5 + tt)

            @blk.vector
            def _(v):
                def w1round(gg):
                    v.wait_ge(s_bx1s, 1)
                    if gg >= 2:
                        v.wait_ge(s_f1, NT5 * (gg - 1))
                    for hh in range(4):
                        if hh == 0:
                            v.wait_ge(d_w1, 16 * (gg + 1))
                        if hh == 2:
                            v.wait_ge(d_w1b, 16 * (gg + 1))
                        c_v.pre()
                        c_v(v.tensor_scalar(rscr[:, :],
                                            w1f_buf[:, 2 * hh:2 * (hh + 1), :],
                                            invsw1_c[:, 0:1], MAGIC, ALU.mult, ALU.add))
                        c_v.pre()
                        c_v(v.tensor_scalar(w1q[:, gg % 2, 2 * hh:2 * (hh + 1), :],
                                            rscr[:, :], MAGIC, None, ALU.subtract),
                            s_w1rh if hh in (1, 3) else None)
                    c_v.pre()
                    c_v(v.memset(scl[:, 15:16], 0.0), s_w1rnd)
                def h2rnd(tg):
                    v.wait_ge(s_bx1s, 1)
                    for j in range(NJ):
                        c_v.pre()
                        c_v(v.tensor_scalar(rscr[:, 0:512],
                                            h2q[:, j, tg * 512:(tg + 1) * 512],
                                            invsx1_c[:, 0:1], MAGIC, ALU.mult, ALU.add))
                        c_v.pre()
                        c_v(v.tensor_scalar(h2q[:, j, tg * 512:(tg + 1) * 512],
                                            rscr[:, 0:512], MAGIC, None, ALU.subtract)
                            , s_h2rnd)
                h2rnd(0)
                w1round(0)
                if NT5 > 1:
                    h2rnd(1)
                for gg in range(NG):
                    for kk in range(4 * gg, 4 * gg + 4):
                        v.wait_ge(d_wmx2, 32 * (kk // 2 + 1))
                        c_v.pre()
                        c_v(v.reduce_max(w2mxc[:, kk:kk + 1], wmx2[:, kk % 2, :], AX,
                                         apply_absolute_value=True), s_w2m)
                    if gg + 1 < NG:
                        w1round(gg + 1)
                    for tt in range(NT5):
                        gt = gg * NT5 + tt
                        if gg == 0 and tt + 2 < NT5:
                            h2rnd(tt + 2)
                        v.wait_ge(s_gelu, gt + 1)
                        for m in range(4):
                            c_v.pre()
                            c_v(v.reduce_max(
                                hmax1[:, 4 * gt + m:4 * gt + m + 1],
                                h1st[:, gt % 4, m, :], AX,
                                apply_absolute_value=True),
                                s_h1mx if m == 3 else None)

            @blk.tensor
            def _(t):
                for gg in range(NG):
                    t.wait_ge(s_w1rnd, gg + 1)
                    for tt in range(NT5):
                        gt = gg * NT5 + tt
                        pset = SETA if gt % 2 == 0 else SETB
                        t.wait_ge(s_h2rnd, NJ * (tt + 1))
                        if gt >= 2:
                            t.wait_ge(s_gelu, gt - 1)
                        last = None
                        for m in range(4):
                            for j in range(NJ):
                                last = t.matmul(
                                    pset[m],
                                    w1q[:, gg % 2, j, m * 128:(m + 1) * 128],
                                    h2q[:, j, tt * 512:(tt + 1) * 512],
                                    start=(j == 0), stop=(j == NJ - 1))
                        last.then_inc(s_f1, 1)

            @blk.scalar
            def _(sc):
                for gg in range(NG):
                    for tt in range(NT5):
                        gt = gg * NT5 + tt
                        pset = SETA if gt % 2 == 0 else SETB
                        sc.wait_ge(s_f1, gt + 1)
                        if gt >= 4:
                            sc.wait_ge(d_h1st, 64 * (gt // 4))
                            sc.wait_ge(s_h1mx, gt - 3)
                        for m in range(4):
                            c_s.pre()
                            c_s(sc.activation(
                                h1st[:, gt % 4, m, :], pset[m], AF.Gelu,
                                bias=b1s[:, gg * 4 + m:gg * 4 + m + 1],
                                scale=s1_c[:, 0:1]),
                                s_gelu if m == 3 else None)


        ph2s.close()
        s12.close()
        w2f_buf = sbuf3("w2f_buf", [128, 4, D], F32)
        b2row = sbuf3("b2row", [1, D], F32)
        b2s_bf = sbuf3("b2s_bf", [1, D], BF16)
        w2qT = sbuf3("w2qT", [128, NM, D], BF16)
        h1buf = sbuf3("h1buf", [128, NM // 2, 2, 512], BF16)
        xp_t = sbuf3("xp_t", [128, 2, 4, D], F32)
        epi = sbuf3("epi", [128, 4, D], F32)
        gb_rep = sbuf3("gb_rep", [128, D], F32)

        with nc.Block("bx2") as blk:

            @blk.vector
            def _(v):
                v.wait_ge(d_xpst, 16 * (NT - NT // 2))
                v.wait_ge(d_xps2, 16 * (NT // 2))
                v.wait_ge(d_h1st, 16 * NG * NT5)
                c_v.pre()
                c_v(v.reduce_max(cols[:, 0, 0:1], hmax1[:, :], AX))
                c_v.pre()
                c_v(v.reduce_max(cols[:, 0, 2:3], w2mxc[:, :], AX), s_bx2v)
                v.wait_ge(s_bx2t, 1)
                c_v.pre(); c_v(v.reduce_max(scl[:, 1:2], F1[0][0:1, 128:256], AX))
                c_v.pre(); c_v(v.tensor_scalar_max(scl[:, 1:2], scl[:, 1:2], 1e-6))
                c_v.pre(); c_v(v.reciprocal(scl[:, 11:12], scl[:, 1:2]))
                c_v.pre(); c_v(v.tensor_scalar_mul(scl[:, 11:12], scl[:, 11:12], 127.0)
                               , s_bx2w)
                c_v.pre(); c_v(v.reduce_max(scl[:, 8:9], F1[0][0:1, 0:128], AX))
                c_v.pre(); c_v(v.tensor_scalar_max(scl[:, 8:9], scl[:, 8:9], 1e-6))
                c_v.pre(); c_v(v.memset(ccb[:, :], 0.0))
                c_v.pre(); c_v(v.tensor_copy(ccb[:, 0:1], scl[:, 8:9]), s_bx2v)
                v.wait_ge(s_bx2g, 2)
                c_v.pre(); c_v(v.tensor_scalar_max(scl[:, 9:10], ccb[:, 0:1], 1e-6))
                c_v.pre(); c_v(v.reciprocal(scl[:, 10:11], scl[:, 9:10]))
                c_v.pre(); c_v(v.tensor_scalar_mul(scl[:, 10:11], scl[:, 10:11], 127.0))
                c_v.pre(); c_v(v.tensor_mul(scl[:, 12:13], scl[:, 1:2], scl[:, 9:10]))
                c_v.pre(); c_v(v.tensor_scalar_mul(scl[:, 12:13], scl[:, 12:13],
                                                   1.0 / (127.0 * 127.0))
                               , s_bx2v)
                v.wait_ge(s_bx2s, 1)
                v.wait_ge(d_sm, 16 * 13)
                c_v.pre()
                c_v(v.tensor_scalar(b2s_bf[:, :], b2row[:, :], scl[:, 12:13], None,
                                    ALU.mult), s_b2s)

            @blk.tensor
            def _(t):
                t.wait_ge(s_bx2v, 1)
                t.transpose(F1[0][0:1, 0:128], cols[:, 0, 0:1], identf[:, :])
                t.transpose(F1[0][0:1, 128:256], cols[:, 0, 2:3],
                            identf[:, :]).then_inc(s_bx2t, 1)
                t.wait_ge(s_bx2w, 1)
                t.matmul(F1[1][:, 1:2], onesrf[0:1, :], scl[:, 11:12],
                         start=True, stop=True).then_inc(s_bx2t, 1)
                t.wait_ge(s_bx2v, 3)
                t.matmul(F1[1][:, 0:1], onesrf[0:1, :], scl[:, 10:11], start=True, stop=False)
                t.matmul(F1[1][:, 2:3], onesrf[0:1, :], scl[:, 12:13], start=False, stop=True)
                t.wait_ge(s_drow, 2)
                t.matmul(F1[2][:, :], onesrf[0:1, :], gb_row[:, 0:512],
                         start=True, stop=True)
                t.matmul(F1[3][:, :], onesrf[0:1, :], gb_row[:, 512:1024],
                         start=True, stop=True).then_inc(s_bx2t, 1)

            @blk.scalar
            def _(sc):
                sc.wait_ge(s_bx2t, 2)
                c_s.pre(); c_s(sc.activation(invsw2_c[:, :], F1[1][:, 1:2], AF.Identity)
                               , s_w2sc)
                sc.wait_ge(s_bx2t, 3)
                c_s.pre(); c_s(sc.activation(invsx2_c[:, :], F1[1][:, 0:1], AF.Identity))
                c_s.pre(); c_s(sc.activation(s2_c[:, :], F1[1][:, 2:3], AF.Identity))
                c_s.pre(); c_s(sc.activation(gb_rep[:, 0:512], F1[2][:, :], AF.Identity))
                c_s.pre(); c_s(sc.activation(gb_rep[:, 512:1024], F1[3][:, :], AF.Identity)
                               , s_bx2s)

            @blk.gpsimd
            def _(g):
                g.wait_ge(s_bx2v, 2)
                g.dma_start(out=cc2i.ap()[:, :], in_=ccb[:, :]).then_inc(d_cc, 16)
                g.wait_ge(d_cc, 48)
                g.collective_compute(
                    "AllReduce", ALU.max,
                    replica_groups=[list(range(n_cores))],
                    ins=[cc2i.ap().opt()], outs=[cc2o.ap().opt()],
                ).then_inc(s_bx2g, 1)
                g.wait_ge(s_bx2g, 1)
                g.dma_start(out=ccb[:, :], in_=cc2o.ap()[:, :]).then_inc(d_cc, 16)
                g.wait_ge(d_cc, 64)
                c_g.pre(); c_g(g.engine_nop(), s_bx2g)

            @blk.sync
            def _(sy):
                sy.wait_ge(d_xpst, 16 * (NT - NT // 2))
                sy.wait_ge(d_xps2, 16 * (NT // 2))
                sy.wait_ge(d_h1st, 16 * NG * NT5)
                sy.dma_start(out=b2row[:, :], in_=b2_h[:, :]).then_inc(d_sm, 16)
                for half in range(2):
                    for gq in range(4):
                        sy.dma_start(
                            out=h1buf[:, 4 * gq:4 * gq + 4, half, :],
                            in_=bass.AP(h1_h.tensor,
                                        (half * 16 + 4 * gq) * 128 * 512,
                                        [[512, 128], [128 * 512, 4], [1, 512]])
                            ).then_inc(d_h1ld if half == 0 else d_h1lb, 16)
                sy.dma_start(out=xp_t[:, 0, :, :],
                             in_=bass.AP(xp_h.tensor, 0,
                                         [[D, 128], [128 * D, 4], [1, D]])
                             ).then_inc(d_xpld, 16)
                for k in range(4):
                    sy.dma_start(out=w2f_buf[:, k % 4, :],
                                 in_=w2T_h[k * 128:(k + 1) * 128, :]).then_inc(d_w2, 16)

        p2.close()
        Q = [psum3("Q_%d" % q, [128, D]) for q in range(4)]

        with nc.Block("ph3") as blk:

            @blk.sync
            def _(sy):
                sy.wait_ge(d_xpst, 16 * (NT - NT // 2))
                sy.wait_ge(d_xps2, 16 * (NT // 2))
                for tt in range(1, NT5):
                    for half in range(2):
                        hh = 2 * tt + half
                        if hh >= 2:
                            sy.wait_ge(s_h1rnd, hh - 1)
                        sy.wait_ge(s_f2, 64 * (tt - 1) + 32 * (half + 1))
                        for gq in range(4):
                            sy.dma_start(
                                out=h1buf[:, 4 * gq:4 * gq + 4, half, :],
                                in_=bass.AP(h1_h.tensor,
                                            (tt * NM + half * 16 + 4 * gq) * 128 * 512,
                                            [[512, 128], [128 * 512, 4], [1, 512]])
                                ).then_inc(d_h1ld if half == 0 else d_h1lb, 16)
                sy.wait_ge(d_out, 16 * NT5)

            @blk.vector
            def _(v):
                w2k = [0]

                def w2rnd(n):
                    for k in range(w2k[0], min(w2k[0] + n, NM)):
                        if k == 0:
                            v.wait_ge(s_w2sc, 1)
                        v.wait_ge(d_w2, min(64 * (k // 4 + 1), 16 * NM))
                        c_v.pre()
                        c_v(v.tensor_scalar(rscr[:, 0:1024], w2f_buf[:, k % 4, :],
                                            invsw2_c[:, 0:1], MAGIC, ALU.mult,
                                            ALU.add))
                        c_v.pre()
                        c_v(v.tensor_scalar(w2qT[:, k, :], rscr[:, 0:1024], MAGIC,
                                            None, ALU.subtract), s_w2rnd)
                    w2k[0] = min(w2k[0] + n, NM)

                for tt in range(NT5):
                    for half in range(2):
                        v.wait_ge(d_h1ld if half == 0 else d_h1lb, 64 * (tt + 1))
                        v.wait_ge(s_bx2v, 3)
                        for qq in range(8):
                            c_v.pre()
                            c_v(v.tensor_scalar(
                                rscr[:, :],
                                h1buf[:, 2 * qq:2 * (qq + 1), half, :],
                                invsx2_c[:, 0:1], MAGIC, ALU.mult, ALU.add))
                            c_v.pre()
                            c_v(v.tensor_scalar(
                                h1buf[:, 2 * qq:2 * (qq + 1), half, :],
                                rscr[:, :], MAGIC, None, ALU.subtract))
                        c_v.pre()
                        c_v(v.memset(scl[:, 15:16], 0.0)
                            , s_h1rnd)
                        w2rnd(8)
                    if tt == 0:
                        w2rnd(NM)


            @blk.scalar
            def _(sc):
                for k in range(4, NM):
                    sc.wait_ge(s_w2rnd, k - 3)
                    sc.dma_start(out=w2f_buf[:, k % 4, :],
                                 in_=w2T_h[k * 128:(k + 1) * 128, :]).then_inc(d_w2, 16)
                for tt in range(NT5):
                    if tt >= 1:
                        sc.wait_ge(d_xpld, 16 * tt)
                        if tt >= 2:
                            sc.wait_ge(s_oadd, 4 * (tt - 1))
                            sc.wait_ge(d_out, 16 * (tt - 1))
                        sc.dma_start(out=xp_t[:, tt % 2, :, :],
                                     in_=bass.AP(xp_h.tensor, tt * 512 * D,
                                                 [[D, 128], [128 * D, 4], [1, D]])
                                     ).then_inc(d_xpld, 16)
                    if tt >= 1:
                        sc.wait_ge(s_oadd, 4 * tt)
                        sc.dma_start(out=bass.AP(out_h.tensor, (tt - 1) * 512 * D,
                                                 [[D, 128], [128 * D, 4], [1, D]]),
                                     in_=xp_t[:, (tt - 1) % 2, :, :]).then_inc(d_out, 16)
                    for q in range(4):
                        tq = 4 * tt + q
                        sc.wait_ge(s_f2b, 2 * tt + (1 if q < 2 else 2))
                        if tq >= 4:
                            sc.wait_ge(s_oadd, tq - 3)
                        c_s.pre()
                        c_s(sc.activation(epi[:, q, :], Q[q][:, :], AF.Identity,
                                          scale=s2_c[:, 0:1]), s_epi)
                sc.wait_ge(s_oadd, 4 * NT5)
                sc.dma_start(out=bass.AP(out_h.tensor, (NT5 - 1) * 512 * D,
                                         [[D, 128], [128 * D, 4], [1, D]]),
                             in_=xp_t[:, (NT5 - 1) % 2, :, :]).then_inc(d_out, 16)

            @blk.gpsimd
            def _(g):
                for tt in range(NT5):
                    for q in range(4):
                        tq = 4 * tt + q
                        g.wait_ge(s_epi, tq + 1)
                        if tq >= 2:
                            g.wait_ge(s_oadd, tq - 1)
                        if tq == 0:
                            g.wait_ge(s_bx2s, 1)
                        c_g.pre()
                        c_g(g.tensor_mul(epi[:, q, :], epi[:, q, :],
                                         gb_rep[:, :]), s_gbm)
                        g.wait_ge(d_xpld, 16 * (tt + 1))
                        c_g.pre()
                        c_g(g.tensor_add(xp_t[:, tt % 2, q, :], xp_t[:, tt % 2, q, :],
                                         epi[:, q, :]), s_oadd)

            @blk.tensor
            def _(t):
                for tt in range(NT5):
                    for ph, (qs, half) in enumerate((((0, 1), 0), ((2, 3), 0),
                                                     ((0, 1), 1), ((2, 3), 1))):
                        t.wait_ge(s_h1rnd, 2 * tt + half + 1)
                        if ph == 0:
                            t.wait_ge(s_b2s, 1)
                            if tt >= 1:
                                t.wait_ge(s_epi, 4 * (tt - 1) + 2)
                        if ph == 1 and tt >= 1:
                            t.wait_ge(s_epi, 4 * tt)
                        for kk in range(16):
                            k = half * 16 + kk
                            if tt == 0:
                                t.wait_ge(s_w2rnd, k + 1)
                            last = None
                            for q in qs:
                                t.matmul(Q[q][:, 0:512],
                                         h1buf[:, kk, half, :][:, q * 128:(q + 1) * 128],
                                         w2qT[:, k, 0:512], start=(k == 0), stop=False)
                                last = t.matmul(Q[q][:, 512:1024],
                                                h1buf[:, kk, half, :][:, q * 128:(q + 1) * 128],
                                                w2qT[:, k, 512:1024],
                                                start=(k == 0), stop=False)
                            last.then_inc(s_f2, 1)
                        if ph >= 2:
                            last = None
                            for q in qs:
                                t.matmul(Q[q][:, 0:512], onesr[:, :], b2s_bf[:, 0:512],
                                         start=False, stop=True)
                                last = t.matmul(Q[q][:, 512:1024], onesr[:, :],
                                                b2s_bf[:, 512:1024],
                                                start=False, stop=True)
                            last.then_inc(s_f2b, 1)

        s3.close()
        p3.close()

    return nc


def prep_inputs(inputs, n_cores=8):
    L = inputs["L0_attn_x"].shape[1]
    D = 1024
    NM = 32
    x = np.ascontiguousarray(np.asarray(inputs["L0_attn_x"], dtype=np.float32))
    B = x.shape[0]
    shared = {
        "bwT": np.ascontiguousarray(np.asarray(inputs["L0_attn_B_w"]).T),
        "cwT": np.ascontiguousarray(np.asarray(inputs["L0_attn_C_w"]).T),
        "gwT": np.ascontiguousarray(np.asarray(inputs["L0_attn_gate_w"]).T),
        "gate_b": np.asarray(inputs["L0_attn_gate_b"]).reshape(1, D),
        "log_dec": np.asarray(inputs["L0_attn_log_decay"]).reshape(1, D),
        "w1T": np.ascontiguousarray(np.asarray(inputs["L0_ffn_w1"]).T),
        "b1f": np.ascontiguousarray(np.asarray(inputs["L0_ffn_b1"]).reshape(NM, 128).T),
        "w2T": np.ascontiguousarray(np.asarray(inputs["L0_ffn_w2"]).T),
        "b2": np.asarray(inputs["L0_ffn_b2"]).reshape(1, D),
        "wmod": np.ascontiguousarray(np.asarray(inputs["L0_wmod"])),
    }
    shared = {k: v.astype(np.float32) for k, v in shared.items()}
    c = np.asarray(inputs["L0_c"], dtype=np.float32)
    in_maps = []
    for b in range(n_cores):
        m = dict(shared)
        m["x"] = np.ascontiguousarray(x[b % B])
        m["c_fm"] = np.ascontiguousarray(c[b % B].reshape(8, 128).T)
        in_maps.append(m)
    return in_maps




def kernel(**inputs):
    from concourse.bass_utils import run_bass_kernel_spmd
    n_cores = 8
    nc = build_nc(L=4096, n_cores=n_cores)
    in_maps = prep_inputs(inputs, n_cores=n_cores)
    res = run_bass_kernel_spmd(nc, in_maps, core_ids=list(range(n_cores)))
    return np.stack([np.asarray(r["out"]) for r in res.results])


# version 29
# speedup vs baseline: 1.0376x; 1.0076x over previous
import contextlib
import numpy as np
import concourse.bass as bass
import concourse.mybir as mybir

F32 = mybir.dt.float32
F32R = mybir.dt.float32r
BF16 = mybir.dt.bfloat16
AF = mybir.ActivationFunctionType
ALU = mybir.AluOpType
AX = mybir.AxisListType.X
MAGIC = float(np.float32(1.5 * 2 ** 23))
NPBF = mybir.dt.np(BF16)


class Ch:

    def __init__(self, eng, sem_h):
        self.eng, self.sem = eng, sem_h
        self.cnt = {}
        self.last = None

    def __call__(self, inst, purpose=None):
        s = purpose if purpose is not None else self.sem
        k = id(s)
        self.cnt[k] = self.cnt.get(k, 0) + 1
        inst.then_inc(s, 1)
        self.last = (s, self.cnt[k])
        return inst

    def pre(self):
        if self.last is not None:
            self.eng.wait_ge(self.last[0], self.last[1])


def build_nc(L=4096, n_cores=8, dbg=False):
    D, H, S = 1024, 4096, 64
    NT = L // 128
    NT5 = L // 512
    NJ = D // 128
    NM = H // 128
    NG = NM // 4
    WMXN1 = (D * H) // (128 * 512)
    WMXN2 = (H * D) // (128 * 1024)
    MODN = (6 * D) // 512

    nc = bass.Bass("TRN2", target_bir_lowering=False, debug=dbg)

    x_h = nc.dram_tensor("x", [L, D], F32, kind="ExternalInput").ap()
    bwT_h = nc.dram_tensor("bwT", [D, S], F32, kind="ExternalInput").ap()
    cwT_h = nc.dram_tensor("cwT", [S, D], F32, kind="ExternalInput").ap()
    gwT_h = nc.dram_tensor("gwT", [D, D], F32, kind="ExternalInput").ap()
    gateb_h = nc.dram_tensor("gate_b", [1, D], F32, kind="ExternalInput").ap()
    logdec_h = nc.dram_tensor("log_dec", [1, D], F32, kind="ExternalInput").ap()
    w1T_h = nc.dram_tensor("w1T", [D, H], F32, kind="ExternalInput").ap()
    b1f_h = nc.dram_tensor("b1f", [128, NM], F32, kind="ExternalInput").ap()
    w2T_h = nc.dram_tensor("w2T", [H, D], F32, kind="ExternalInput").ap()
    b2_h = nc.dram_tensor("b2", [1, D], F32, kind="ExternalInput").ap()
    cfm_h = nc.dram_tensor("c_fm", [128, NJ], F32, kind="ExternalInput").ap()
    wmod_h = nc.dram_tensor("wmod", [D, 6 * D], F32, kind="ExternalInput").ap()
    out_h = nc.dram_tensor("out", [L, D], F32, kind="ExternalOutput").ap()

    xp_h = nc.dram_tensor("xp_scr", [L, D], F32).ap()
    h1_h = nc.dram_tensor("h1_scr", [NT5 * NM * 128, 512], BF16).ap()
    rowb_h = nc.dram_tensor("rowb_scr", [1, 8 * D], BF16).ap()
    cc1i = nc.dram_tensor("cc1_in", [1, 16], F32)
    cc1o = nc.dram_tensor("cc1_out", [1, 16], F32)
    cc2i = nc.dram_tensor("cc2_in", [1, 16], F32)
    cc2o = nc.dram_tensor("cc2_out", [1, 16], F32)

    t_excl_np = np.triu(np.ones((128, 128), dtype=np.float32), 1).astype(NPBF)
    Texcl_h = nc.inline_tensor(t_excl_np, "Texcl_c").ap()
    ident_h = nc.inline_tensor(np.eye(128, dtype=np.float32).astype(NPBF), "ident_c").ap()
    identf_h = nc.inline_tensor(np.eye(128, dtype=np.float32), "identf_c").ap()
    onesr_h = nc.inline_tensor(np.ones((1, 128), dtype=NPBF), "onesr_c").ap()
    onesrf_h = nc.inline_tensor(np.ones((1, 128), dtype=np.float32), "onesrf_c").ap()
    onesc_h = nc.inline_tensor(np.ones((128, 1), dtype=NPBF), "onesc_c").ap()
    oA = np.zeros((128, 33), dtype=np.float32); oA[:, 0] = 1.0
    oB = np.zeros((128, 33), dtype=np.float32); oB[:, 32] = 1.0
    onesA_h = nc.inline_tensor(oA.astype(NPBF), "onesA_c").ap()
    onesB_h = nc.inline_tensor(oB.astype(NPBF), "onesB_c").ap()
    epsc_h = nc.inline_tensor(np.full((128, 1), 1e-5, dtype=np.float32),
                              "epsc_c").ap()

    stk = contextlib.ExitStack()
    s12 = contextlib.ExitStack()
    p1 = contextlib.ExitStack()
    p2 = contextlib.ExitStack()
    p3 = contextlib.ExitStack()
    ph1t = contextlib.ExitStack()
    pro = contextlib.ExitStack()
    ph2s = contextlib.ExitStack()
    s3 = contextlib.ExitStack()

    def _mk(stack):
        def sbufF(name, shape, dt):
            return stack.enter_context(nc.sbuf_tensor(name, shape, dt))
        return sbufF

    sbuf = _mk(stk)
    sbuf12 = _mk(s12)
    sbuf1 = _mk(ph1t)
    sbufP = _mk(pro)
    sbuf2 = _mk(ph2s)
    sbuf3 = _mk(s3)

    def psum1(name, shape, dt=F32):
        return p1.enter_context(nc.psum_tensor(name, shape, dt))

    def psum2(name, shape, dt=F32):
        return p2.enter_context(nc.psum_tensor(name, shape, dt))

    def psum3(name, shape, dt=F32):
        return p3.enter_context(nc.psum_tensor(name, shape, dt))

    def sem(name):
        return stk.enter_context(nc.semaphore(name))

    with stk:
        Texcl = sbuf("Texcl_s", [128, 128], BF16)
        ident = sbuf("ident_s", [128, 128], BF16)
        identf = sbuf("identf_s", [128, 128], F32)
        onesr = sbuf("onesr_s", [1, 128], BF16)
        onesrf = sbuf("onesrf_s", [1, 128], F32)
        onesc = sbuf("onesc_s", [128, 1], BF16)
        onesA = sbuf("onesA_s", [128, 33], BF16)
        onesB = sbuf("onesB_s", [128, 33], BF16)
        gateb_bf = sbuf("gateb_bf", [1, D], BF16)
        gb2_bf = sbuf("gb2_bf", [1, D], BF16)
        bu0_bf = sbuf("bu0_bf", [1, 64], BF16)
        sa_bf = sbuf("sa_bf", [128, NJ], BF16)
        ca1_fm = sbuf("ca1_fm", [128, NJ], F32)
        sa_fm = sbuf("sa_fm", [128, NJ], F32)
        cb1_fm = sbuf("cb1_fm", [128, NJ], F32)
        sbv_fm = sbuf("sbv_fm", [128, NJ], F32)
        gb_row = sbuf("gb_row", [1, D], F32)
        b1f = sbuf("b1f_s", [128, NM], F32)
        b1s = sbuf("b1s", [128, NM], F32)
        C_sb = sbuf("C_sb", [1, D], BF16)
        C_f32 = sbuf("C_f32", [1, D], F32)
        hmax2 = sbuf("hmax2", [128, 8 * NT], F32)
        hmax1 = sbuf("hmax1", [128, 4 * NG * NT5], F32)
        w1mxc = sbuf("w1mxc", [128, WMXN1], F32)
        w2mxc = sbuf("w2mxc", [128, WMXN2], F32)
        scl = sbuf("scl", [1, 16], F32)
        invsw1_c = sbuf("invsw1_c", [128, 1], F32)
        invsx1_c = sbuf("invsx1_c", [128, 1], F32)
        s1_c = sbuf("s1_c", [128, 1], F32)
        invsw2_c = sbuf("invsw2_c", [128, 1], F32)
        invsx2_c = sbuf("invsx2_c", [128, 1], F32)
        s2_c = sbuf("s2_c", [128, 1], F32)
        ccb = sbuf("ccb", [1, 16], F32)
        epsc = sbuf("epsc", [128, 1], F32)

        cols = sbuf("cols", [128, 2, 40], F32)
        rscr = sbuf("rscr", [128, 1024], F32)
        h2q = sbuf12("h2q", [128, NJ, L], BF16)
        gwT = sbuf12("gwT_s", [128, NJ, D], BF16)
        bwT = sbuf12("bwT_s", [128, NJ, S], BF16)
        cwga = sbuf12("cwga", [64, D], BF16)
        xbuf = sbuf1("xbuf", [128, 6, D], F32)
        xnorm_bf = sbuf1("xnorm_bf", [128, 2, D], BF16)
        xnorm2_bf = sbuf1("xnorm2_bf", [128, 3, D], BF16)
        ha_fm = sbuf1("ha_fm", [128, 2, D], BF16)
        bu_bf = sbuf1("bu_bf", [64, 2, 128], BF16)
        W_sb = sbuf1("W_sb", [128, 3, D], BF16)
        ld_bc = sbuf1("ld_bc", [128, D], F32)
        tmp_bf = sbuf1("tmp_bf", [128, 2, D], BF16)
        gate_sb = sbuf1("gate_sb", [128, 2, D], BF16)
        xp_buf = sbuf1("xp_buf", [128, 3, D], F32)
        tcol = sbuf1("tcol", [128, 4], F32)
        wmx_buf = sbufP("wmx_buf", [128, 2, 512], F32)
        wm_bf = sbufP("wm_bf", [128, 4, 512], BF16)
        silu_bf = sbufP("silu_bf", [128, NJ], BF16)
        silu_f = sbufP("silu_f", [128, NJ], F32)
        wmf = sbufP("wmf", [128, 4, 256], F32)
        cfm = sbufP("cfm_s", [128, NJ], F32)
        mod_b = sbufP("mod_b", [1, 2 * D], BF16)
        mod_f = sbufP("mod_f", [1, D], F32)
        sgn_row = sbufP("sgn_row", [1, D], BF16)
        ldrow_f = sbufP("ldrow_f", [1, D], F32)
        cw_bf = sbufP("cw_bf", [64, D], BF16)

        PT1 = psum1("PT1", [128, D], BF16)
        PW = psum1("PW", [128, 512])
        PBS = psum1("PBS", [128, D])
        PT2 = psum1("PT2", [128, D], BF16)
        PCS = psum1("PCS", [64, 512])
        PG = psum1("PG", [128, D])

        d_sm = sem("d_sm"); d_x = sem("d_x"); d_xpst = sem("d_xpst")
        d_xps2 = sem("d_xps2")
        d_wmx = sem("d_wmx"); d_w1 = sem("d_w1"); d_w2 = sem("d_w2")
        d_w1b = sem("d_w1b"); s_w1rh = sem("s_w1rh"); d_h1lb = sem("d_h1lb")
        d_h1st = sem("d_h1st"); d_h1ld = sem("d_h1ld"); d_xpld = sem("d_xpld")
        d_out = sem("d_out"); d_gw = sem("d_gw"); d_wm = sem("d_wm")
        d_fm = sem("d_fm"); d_cc = sem("d_cc"); d_wmf = sem("d_wmf")
        d_wmx2 = sem("d_wmx2")
        c_v = Ch(nc.vector, sem("c_v"))
        c_s = Ch(nc.scalar, sem("c_s"))
        c_g = Ch(nc.gpsimd, sem("c_g"))
        s_sumx = sem("s_sumx")
        s_cols1 = sem("s_cols1"); s_sqrt1 = sem("s_sqrt1"); s_nb1 = sem("s_nb1")
        s_norm1 = sem("s_norm1"); s_trans1 = sem("s_trans1"); s_haev = sem("s_haev")
        s_bu = sem("s_bu"); s_buev = sem("s_buev"); s_iota = sem("s_iota")
        s_wexp = sem("s_wexp")
        s_bp = sem("s_bp"); s_tmp = sem("s_tmp")
        s_gate = sem("s_gate"); s_sig = sem("s_sig")
        s_scan = sem("s_scan"); s_attn = sem("s_attn")
        s_cs = sem("s_cs"); s_cse = sem("s_cse"); s_cse0 = sem("s_cse0")
        s_xpadd = sem("s_xpadd")
        s_sumx2 = sem("s_sumx2")
        s_pro2 = sem("s_pro2"); s_modf = sem("s_modf")
        s_cols2 = sem("s_cols2"); s_sqrt2 = sem("s_sqrt2"); s_nb2 = sem("s_nb2")
        s_norm2 = sem("s_norm2"); s_trans2 = sem("s_trans2"); s_h2ev = sem("s_h2ev")
        s_h2mx = sem("s_h2mx")
        s_silu = sem("s_silu"); s_sig0 = sem("s_sig0"); s_clamp0 = sem("s_clamp0")
        s_ln0 = sem("s_ln0"); s_ldbf = sem("s_ldbf")
        s_drow = sem("s_drow"); s_gab = sem("s_gab"); s_cwga = sem("s_cwga")
        s_dfm = sem("s_dfm")
        s_pro = sem("s_pro"); s_modev = sem("s_modev")
        s_w1m = sem("s_w1m"); s_w2m = sem("s_w2m")
        s_sabf = sem("s_sabf"); s_bu0 = sem("s_bu0"); s_gb2 = sem("s_gb2")
        s_bu0ev = sem("s_bu0ev"); s_gb2ev = sem("s_gb2ev")
        s_bwca = sem("s_bwca"); s_gwca = sem("s_gwca")
        s_ldbc = sem("s_ldbc"); s_ldbcm = sem("s_ldbcm"); s_csa = sem("s_csa")
        s_bx2w = sem("s_bx2w"); s_w2sc = sem("s_w2sc")
        s_bx1v = sem("s_bx1v"); s_bx1t = sem("s_bx1t"); s_bx1s = sem("s_bx1s")
        s_bx1g = sem("s_bx1g")
        s_h2rnd = sem("s_h2rnd")
        s_w1rnd = sem("s_w1rnd"); s_f1 = sem("s_f1")
        s_gelu = sem("s_gelu"); s_h1mx = sem("s_h1mx")
        s_bx2v = sem("s_bx2v"); s_bx2t = sem("s_bx2t"); s_bx2s = sem("s_bx2s")
        s_bx2g = sem("s_bx2g")
        s_w2rnd = sem("s_w2rnd"); s_h1rnd = sem("s_h1rnd")
        s_f2 = sem("s_f2"); s_f2b = sem("s_f2b")
        s_epi = sem("s_epi"); s_gbm = sem("s_gbm"); s_oadd = sem("s_oadd")
        s_b2s = sem("s_b2s")

        DG_FIX = NJ + 3
        DG_WM0 = DG_FIX
        DG_DROW = DG_WM0 + 6 * NJ
        DG_FM_END = DG_DROW + 2 + 4

        with nc.Block("ph1") as blk:

            @blk.sync
            def _(sy):
                for dst, srct in ((Texcl, Texcl_h), (ident, ident_h), (identf, identf_h),
                                  (onesr, onesr_h), (onesrf, onesrf_h), (onesc, onesc_h),
                                  (onesA, onesA_h), (onesB, onesB_h),
                                  (ldrow_f, logdec_h), (cfm, cfm_h), (b1f, b1f_h),
                                  (epsc, epsc_h)):
                    sy.dma_start(out=dst[:, :], in_=srct[:, :]).then_inc(d_sm, 16)
                for s in range(NT + 10):
                    i = s
                    if i < NT:
                        if i >= 2:
                            sy.wait_ge(s_sumx, i - 1)
                        if i >= 6:
                            sy.wait_ge(s_xpadd, i - 5)
                        sy.dma_start(out=xbuf[:, i % 6, :],
                                     in_=x_h[i * 128:(i + 1) * 128, :]).then_inc(d_x, 16)
                    i = s - 5
                    if 0 <= i < NT:
                        sy.wait_ge(s_xpadd, i + 1)
                        if i >= 2:
                            sy.wait_ge(d_xpst if i % 2 == 0 else d_xps2,
                                       16 * (i // 2))
                        sy.dma_start(out=xp_h[i * 128:(i + 1) * 128, :],
                                     in_=xp_buf[:, i % 3, :]
                                     ).then_inc(d_xpst if i % 2 == 0 else d_xps2, 16)

            @blk.gpsimd
            def _(g):
                for j in range(NJ):
                    g.dma_start(out=gwT[:, j, :],
                                in_=gwT_h[j * 128:(j + 1) * 128, :]).then_inc(d_gw, 16)
                g.dma_start(out=bwT[:, :, :],
                            in_=bass.AP(bwT_h.tensor, 0,
                                        [[S, 128], [128 * S, NJ], [1, S]])
                            ).then_inc(d_gw, 16)
                g.dma_start(out=gateb_bf[:, :], in_=gateb_h[:, :]).then_inc(d_gw, 16)
                g.dma_start(out=cw_bf[:, :], in_=cwT_h[:, :]).then_inc(d_gw, 16)
                def wm_load(idx):
                    n2 = (0, 1, 3, 4)[idx // 16]
                    q = (idx // 8) % 2
                    j = idx % 8
                    if idx >= 4:
                        g.wait_ge(s_pro, idx - 3)
                    g.dma_start(out=wm_bf[:, idx % 4, :],
                                in_=bass.AP(wmod_h.tensor,
                                            j * 128 * 6 * D + n2 * 1024 + q * 512,
                                            [[6 * D, 128], [1, 512]])
                                ).then_inc(d_wm, 16)

                for idx in range(36):
                    wm_load(idx)
                g.wait_ge(s_modev, 4)
                g.dma_start(out=rowb_h[0:1, 0:2 * D],
                            in_=mod_b[:, :]).then_inc(d_fm, 16)
                g.wait_ge(d_fm, 16)
                for t_i, dst in ((0, sa_fm), (1, ca1_fm)):
                    g.dma_start(out=dst[:, :],
                                in_=bass.AP(rowb_h.tensor, t_i * D,
                                            [[1, 128], [128, NJ]]),
                                allow_slow_non_contiguous=True).then_inc(d_fm, 16)
                g.wait_ge(d_fm, 48)
                c_g.pre(); c_g(g.tensor_copy(sa_bf[:, :], sa_fm[:, :]), s_sabf)
                for idx in range(36, 64):
                    wm_load(idx)
                g.wait_ge(s_bu0, 1)
                g.wait_ge(s_dfm, 1)
                for j in range(NJ):
                    c_g.pre()
                    c_g(g.tensor_scalar(bwT[:, j, :], bwT[:, j, :],
                                        ca1_fm[:, j:j + 1], None, ALU.mult),
                        s_bwca if j == NJ - 1 else None)
                c_g.pre(); c_g(g.memset(C_f32[:, :], 0.0), s_cse0)
                for s in range(max(NT + 10, WMXN1 // 2 + 1)):
                    if s == 2:
                        g.wait_ge(s_modev, 8)
                        g.dma_start(out=rowb_h[0:1, 2 * D:4 * D],
                                    in_=mod_b[:, :]).then_inc(d_fm, 16)
                        g.wait_ge(d_fm, 64)
                        for t_i, dst in ((2, sbv_fm), (3, cb1_fm)):
                            g.dma_start(out=dst[:, :],
                                        in_=bass.AP(rowb_h.tensor, t_i * D,
                                                    [[1, 128], [128, NJ]]),
                                        allow_slow_non_contiguous=True
                                        ).then_inc(d_fm, 16)
                    i = s
                    if i < NT:
                        if i >= 4:
                            g.wait_ge(s_wexp, i - 3)
                        c_g.pre()
                        c_g(g.iota(tcol[:, i % 4:i % 4 + 1], [[1, 1]], base=i * 128,
                                   channel_multiplier=1,
                                   allow_small_or_imprecise_dtypes=True), s_iota)
                    for k in (2 * s, 2 * s + 1):
                        if k < WMXN1:
                            if k >= 2:
                                g.wait_ge(s_w1m, k - 1)
                            g.dma_start(out=wmx_buf[:, k % 2, :],
                                        in_=bass.AP(w1T_h.tensor, k * 128 * 512,
                                                    [[512, 128], [1, 512]])
                                        ).then_inc(d_wmx, 16)
                    i = s - 1
                    if 0 <= i < NT:
                        g.wait_ge(s_nb1, i + 1)
                        g.wait_ge(d_x, 32 * (i // 2 + 1))
                        if i >= 2:
                            g.wait_ge(s_trans1, i - 1)
                        cc = cols[:, i % 2, :]
                        c_g.pre()
                        c_g(g.tensor_scalar(xnorm_bf[:, i % 2, :], xbuf[:, i % 6, :],
                                            cc[:, 16:17], cc[:, 17:18],
                                            ALU.mult, ALU.add), s_norm1)
                    i = s - 7
                    if 0 <= i < NT:
                        g.wait_ge(s_nb2, i + 1)
                        if i >= 3:
                            g.wait_ge(s_trans2, i - 2)
                        cc = cols[:, i % 2, :]
                        c_g.pre()
                        c_g(g.tensor_scalar(xnorm2_bf[:, i % 3, :], xp_buf[:, i % 3, :],
                                            cc[:, 36:37], cc[:, 37:38],
                                            ALU.mult, ALU.add), s_norm2)
                    i = s - 4
                    if 0 <= i < NT:
                        g.wait_ge(s_attn, i + 1)
                        c_g.pre()
                        c_g(g.tensor_add(xp_buf[:, i % 3, :], xp_buf[:, i % 3, :],
                                         xbuf[:, i % 6, :]), s_xpadd)


            @blk.vector
            def _(v):
                v.wait_ge(s_sig0, 1)
                c_v.pre()
                c_v(v.tensor_scalar_max(ldrow_f[:, :], ldrow_f[:, :], 1e-38), s_clamp0)
                v.wait_ge(s_ln0, 1)
                c_v.pre()
                c_v(v.tensor_scalar_max(ldrow_f[:, :], ldrow_f[:, :], -87.0), s_ldbf)

                def clamp_pm(slot):
                    c_v.pre(); c_v(v.tensor_scalar(sgn_row[:, :], slot, 0.0, None,
                                                   ALU.is_ge))
                    c_v.pre(); c_v(v.tensor_scalar(sgn_row[:, :], sgn_row[:, :], 2.0,
                                                   -1.0, ALU.mult, ALU.add))
                    c_v.pre(); c_v(v.tensor_scalar(rscr[0:1, 0:D], slot, -1.0, None,
                                                   ALU.mult))
                    c_v.pre(); c_v(v.tensor_max(slot, slot, rscr[0:1, 0:D]))
                    c_v.pre(); c_v(v.tensor_scalar_max(slot, slot, 0.25))
                    c_v.pre(); c_v(v.tensor_mul(slot, slot, sgn_row[:, :]))

                v.wait_ge(s_modf, 2)
                clamp_pm(mod_f[:, 0:D])
                c_v.pre(); c_v(v.memset(scl[:, 15:16], 0.0), s_drow)
                v.wait_ge(d_fm, 48)
                c_v.pre(); c_v(v.tensor_scalar_add(ca1_fm[:, :], ca1_fm[:, :], 1.0),
                               s_dfm)
                v.wait_ge(s_gab, 1)
                v.wait_ge(d_gw, 16 * (NJ + 3))
                c_v.pre()
                c_v(v.scalar_tensor_tensor(cwga[:, :], cw_bf[:, :], 0.5,
                                           PG[0:64, 0:D], ALU.mult, ALU.mult), s_cwga)
                v.wait_ge(s_gb2, 2)
                v.wait_ge(s_dfm, 1)
                for j in range(NJ):
                    c_v.pre()
                    c_v(v.tensor_scalar(gwT[:, j, :], gwT[:, j, :],
                                        ca1_fm[:, j:j + 1], None, ALU.mult),
                        s_gwca if j == NJ - 1 else None)
                v.wait_ge(s_modf, 4)
                clamp_pm(mod_f[:, 0:D])
                c_v.pre(); c_v(v.tensor_copy(gb_row[:, :], mod_f[:, 0:D]), s_drow)

                for s in range(max(NT + 10, WMXN1 // 2 + 1)):
                    if s == 4:
                        v.wait_ge(d_fm, 16 * 6)
                        c_v.pre()
                        c_v(v.tensor_scalar_add(cb1_fm[:, :], cb1_fm[:, :], 1.0),
                            s_dfm)
                    i = s
                    if i < NT:
                        v.wait_ge(d_x, 32 * (i // 2 + 1))
                        cc = cols[:, i % 2, :]
                        c_v.pre()
                        c_v(v.bn_stats(cc[:, 0:6], xbuf[:, i % 6, 0:512]))
                        c_v.pre()
                        c_v(v.bn_stats(cc[:, 6:12], xbuf[:, i % 6, 512:1024]), s_sumx)
                        if i >= 2:
                            v.wait_ge(s_sqrt1, i - 1)
                        c_v.pre()
                        c_v(v.bn_aggr(cc[:, 12:14], cc[:, 0:12]), s_cols1)
                    i = s - 1
                    if 0 <= i < NT:
                        v.wait_ge(s_sqrt1, i + 1)
                        if i >= 2:
                            v.wait_ge(s_norm1, i - 1)
                        cc = cols[:, i % 2, :]
                        c_v.pre()
                        c_v(v.reciprocal(cc[:, 16:17], cc[:, 15:16]))
                        c_v.pre()
                        c_v(v.scalar_tensor_tensor(cc[:, 17:18], cc[:, 12:13], -1.0,
                                                   cc[:, 16:17], ALU.mult, ALU.mult),
                            s_nb1)
                    i = s - 4
                    if 0 <= i < NT:
                        v.wait_ge(s_bp, i + 1)
                        v.wait_ge(s_wexp, i + 1)
                        if i >= 2:
                            v.wait_ge(s_cs, i - 1)
                        c_v.pre()
                        c_v(v.tensor_mul(tmp_bf[:, i % 2, :], W_sb[:, i % 3, :],
                                         PBS[:, :]), s_tmp)
                    i = s - 5
                    if 0 <= i < NT:
                        v.wait_ge(s_xpadd, i + 1)
                        cc = cols[:, i % 2, :]
                        c_v.pre()
                        c_v(v.bn_stats(cc[:, 20:26], xp_buf[:, i % 3, 0:512]))
                        c_v.pre()
                        c_v(v.bn_stats(cc[:, 26:32], xp_buf[:, i % 3, 512:1024]),
                            s_sumx2)
                        if i >= 2:
                            v.wait_ge(s_sqrt2, i - 1)
                        c_v.pre()
                        c_v(v.bn_aggr(cc[:, 32:34], cc[:, 20:32]), s_cols2)
                    i = s - 4
                    if 0 <= i < NT:
                        v.wait_ge(s_scan, i + 1)
                        v.wait_ge(s_sig, i + 1)
                        if i >= 3:
                            v.wait_ge(s_norm2, i - 2)
                        if i >= 3:
                            v.wait_ge(d_xpst if (i - 3) % 2 == 0 else d_xps2,
                                      16 * ((i - 3) // 2 + 1))
                        c_v.pre()
                        c_v(v.scalar_tensor_tensor(xp_buf[:, i % 3, :],
                                                   gate_sb[:, i % 2, :], 1.0,
                                                   PBS[:, :], ALU.add, ALU.mult),
                            s_attn)
                        v.wait_ge(s_cs, i + 1)
                        if i == 0:
                            v.wait_ge(s_cse0, 1)
                        if i >= 1:
                            v.wait_ge(s_cse, i)
                        c_v.pre()
                        c_v(v.tensor_add(C_f32[:, 0:512], C_f32[:, 0:512], PCS[0:1, :]))
                        c_v.pre()
                        c_v(v.tensor_add(C_f32[:, 512:1024], C_f32[:, 512:1024],
                                         PCS[32:33, :]), s_csa)
                    i = s - 6
                    if 0 <= i < NT:
                        v.wait_ge(s_sqrt2, i + 1)
                        if i >= 2:
                            v.wait_ge(s_norm2, i - 1)
                        cc = cols[:, i % 2, :]
                        c_v.pre()
                        c_v(v.reciprocal(cc[:, 36:37], cc[:, 35:36]))
                        c_v.pre()
                        c_v(v.scalar_tensor_tensor(cc[:, 37:38], cc[:, 32:33], -1.0,
                                                   cc[:, 36:37], ALU.mult, ALU.mult),
                            s_nb2)
                    for k in (2 * s, 2 * s + 1):
                        if k < WMXN1:
                            v.wait_ge(d_wmx, 32 * (k // 2 + 1))
                            c_v.pre()
                            c_v(v.reduce_max(w1mxc[:, k:k + 1], wmx_buf[:, k % 2, :],
                                             AX, apply_absolute_value=True), s_w1m)
                    i = s - 10
                    if 0 <= i < NT:
                        v.wait_ge(s_h2ev, i + 1)
                        c_v.pre()
                        c_v(v.reduce_max(hmax2[:, 8 * i:8 * i + 8],
                                         h2q[:, :, i * 128:(i + 1) * 128], AX,
                                         apply_absolute_value=True), s_h2mx)

            @blk.scalar
            def _(sc):
                sc.wait_ge(d_sm, 16 * 12)
                c_s.pre()
                c_s(sc.activation(silu_bf[:, :], cfm[:, :], AF.Silu))
                c_s.pre()
                c_s(sc.activation(silu_f[:, :], cfm[:, :], AF.Silu), s_silu)
                c_s.pre()
                c_s(sc.activation(ldrow_f[:, :], ldrow_f[:, :], AF.Sigmoid), s_sig0)
                sc.wait_ge(s_clamp0, 1)
                c_s.pre()
                c_s(sc.activation(ldrow_f[:, :], ldrow_f[:, :], AF.Ln), s_ln0)
                psl = (PBS[0:1, 0:512], PBS[0:1, 512:1024],
                       PG[0:1, 0:512], PG[0:1, 512:1024])

                def modev(n512):
                    seg = n512 // 2
                    if n512 < 8:
                        sc.wait_ge(s_pro, 8 * (n512 + 1))
                    else:
                        sc.wait_ge(s_pro2, 16 * (n512 - 7))
                    if n512 == 4:
                        sc.wait_ge(d_fm, 16)
                    if n512 == 10:
                        sc.wait_ge(s_gab, 1)
                    c_s.pre()
                    if n512 < 8:
                        dst = mod_b[:, (seg % 2) * D + (n512 % 2) * 512:
                                    (seg % 2) * D + (n512 % 2) * 512 + 512]
                        c_s(sc.activation(dst, psl[n512 % 4], AF.Identity), s_modev)
                    else:
                        dst = mod_f[:, (n512 % 2) * 512:(n512 % 2) * 512 + 512]
                        psrc = (PCS[0:1, 0:512] if n512 % 2 == 0
                                else PCS[32:33, 0:512])
                        c_s(sc.activation(dst, psrc, AF.Identity), s_modf)

                fidx = 0
                for n2 in (2, 5):
                    for q in range(2):
                        for q2 in range(2):
                            for j in range(NJ):
                                if fidx >= 4:
                                    sc.wait_ge(s_pro2, fidx - 3)
                                sc.dma_start(out=wmf[:, fidx % 4, :],
                                             in_=bass.AP(wmod_h.tensor,
                                                         j * 128 * 6 * D + n2 * 1024
                                                         + q * 512 + q2 * 256,
                                                         [[6 * D, 128], [1, 256]])
                                             ).then_inc(d_wmf, 16)
                                fidx += 1
                    if n2 == 2:
                        modev(8)
                        modev(9)
                for n512 in range(4):
                    modev(n512)
                modev(10)
                modev(11)
                sc.wait_ge(s_ldbcm, 1)
                c_s.pre()
                c_s(sc.activation(ld_bc[:, :], PG[:, :], AF.Identity), s_ldbc)


                for s in range(NT + 10):
                    if s < 4:
                        modev(4 + s)
                    if s == 2:
                        sc.wait_ge(s_bu0, 1)
                        c_s.pre()
                        c_s(sc.activation(bu0_bf[:, :], PCS[0:1, 0:64],
                                          AF.Identity), s_bu0ev)
                        sc.wait_ge(s_gb2, 2)
                        c_s.pre()
                        c_s(sc.activation(gb2_bf[:, 0:512], PW[0:1, 0:512],
                                          AF.Identity))
                        c_s.pre()
                        c_s(sc.activation(gb2_bf[:, 512:1024], PW[32:33, 0:512],
                                          AF.Identity), s_gb2ev)
                    i = s - 1
                    if 0 <= i < NT:
                        sc.wait_ge(s_cols1, i + 1)
                        if i >= 2:
                            sc.wait_ge(s_norm1, i - 1)
                        cc = cols[:, i % 2, :]
                        c_s.pre()
                        c_s(sc.activation(cc[:, 15:16], cc[:, 13:14], AF.Sqrt,
                                          bias=epsc[:, 0:1]), s_sqrt1)
                    i = s - 6
                    if 0 <= i < NT:
                        sc.wait_ge(s_cols2, i + 1)
                        if i >= 2:
                            sc.wait_ge(s_norm2, i - 1)
                        cc = cols[:, i % 2, :]
                        c_s.pre()
                        c_s(sc.activation(cc[:, 35:36], cc[:, 33:34], AF.Sqrt,
                                          bias=epsc[:, 0:1]), s_sqrt2)
                    i = s - 4
                    if 0 <= i < NT:
                        sc.wait_ge(s_gate, i + 1)
                        if i >= 2:
                            sc.wait_ge(s_attn, i - 1)
                        c_s.pre()
                        c_s(sc.activation(gate_sb[:, i % 2, :], PG[:, :], AF.Tanh,
                                          scale=0.5), s_sig)
                    i = s - 2
                    if 0 <= i < NT:
                        sc.wait_ge(s_iota, i + 1)
                        if i == 0:
                            sc.wait_ge(s_ldbc, 1)
                        if i >= 3:
                            sc.wait_ge(s_tmp, i - 2)
                        c_s.pre()
                        c_s(sc.activation(W_sb[:, i % 3, :], ld_bc[:, :], AF.Exp,
                                          scale=tcol[:, i % 4:i % 4 + 1]), s_wexp)
                    i = s - 2
                    if 0 <= i < NT:
                        sc.wait_ge(s_trans1, i + 1)
                        if i >= 2:
                            sc.wait_ge(s_gate, i - 1)
                        c_s.pre()
                        c_s(sc.activation(ha_fm[:, i % 2, :], PT1[:, :], AF.Identity),
                            s_haev)
                    i = s - 3
                    if 0 <= i < NT:
                        sc.wait_ge(s_bu, i + 1)
                        if i >= 2:
                            sc.wait_ge(s_bp, i - 1)
                        c_s.pre()
                        c_s(sc.activation(bu_bf[:, i % 2, :],
                                          PW[0:64, 128 * (i % 2):128 * (i % 2) + 128],
                                          AF.Identity), s_buev)
                    i = s - 9
                    if 0 <= i < NT:
                        sc.wait_ge(s_trans2, i + 1)
                        if i == 0:
                            sc.wait_ge(s_dfm, 2)
                        for j in range(NJ):
                            c_s.pre()
                            c_s(sc.activation(h2q[:, j, i * 128:(i + 1) * 128],
                                              PT2[:, j * 128:(j + 1) * 128],
                                              AF.Identity, bias=sbv_fm[:, j:j + 1],
                                              scale=cb1_fm[:, j:j + 1]),
                                s_h2ev if j == NJ - 1 else None)
                    i = s - 4
                    if 0 <= i < NT:
                        sc.wait_ge(s_csa, i + 1)
                        c_s.pre()
                        c_s(sc.activation(C_sb[:, :], C_f32[:, :], AF.Identity), s_cse)

            @blk.tensor
            def _(t):
                t.wait_ge(s_silu, 1)
                psl = (PBS[0:1, 0:512], PBS[0:1, 512:1024],
                       PG[0:1, 0:512], PG[0:1, 512:1024])

                def wm_mm(idx):
                    n512 = idx // 8
                    if idx == 32:
                        t.wait_ge(s_modev, 2)
                    if idx == 48:
                        t.wait_ge(s_modev, 4)
                        t.wait_ge(s_ldbc, 1)
                    t.wait_ge(d_wm, min(64 * (idx // 4 + 1), 16 * 64))
                    t.matmul(psl[n512 % 4], silu_bf[:, idx % 8:idx % 8 + 1],
                             wm_bf[:, idx % 4, :],
                             start=(idx % 8 == 0), stop=(idx % 8 == NJ - 1)
                             ).then_inc(s_pro, 1)

                for i in range(32):
                    for fidx in (2 * i, 2 * i + 1):
                        seg = fidx // 32
                        q = (fidx // 16) % 2
                        j = fidx % 8
                        n512 = 2 * (4 + seg) + q
                        if fidx == 32:
                            t.wait_ge(s_modf, 2)
                        row = (PCS[0:1, 0:512] if n512 % 2 == 0
                               else PCS[32:33, 0:512])
                        q2 = (fidx // 8) % 2
                        t.wait_ge(d_wmf, min(64 * (fidx // 4 + 1), 16 * 64))
                        t.matmul(row[:, q2 * 256:q2 * 256 + 256],
                                 silu_f[:, j:j + 1], wmf[:, fidx % 4, :],
                                 start=(j == 0), stop=(j == NJ - 1)
                                 ).then_inc(s_pro2, 1)
                    wm_mm(i)
                t.wait_ge(s_drow, 1)
                t.wait_ge(s_modev, 4)
                t.matmul(PG[0:64, 0:512], onesrf[0:1, 0:64],
                         mod_f[:, 0:512], start=True, stop=True)
                t.matmul(PG[0:64, 512:1024], onesrf[0:1, 0:64],
                         mod_f[:, 512:1024], start=True,
                         stop=True).then_inc(s_gab, 1)
                t.wait_ge(s_cwga, 1)
                t.wait_ge(s_ldbf, 1)
                t.matmul(PG[:, 0:512], onesrf[0:1, :], ldrow_f[:, 0:512],
                         start=True, stop=True)
                t.matmul(PG[:, 512:1024], onesrf[0:1, :], ldrow_f[:, 512:1024],
                         start=True, stop=True).then_inc(s_ldbcm, 1)
                for idx in range(32, 48):
                    wm_mm(idx)
                t.wait_ge(s_sabf, 1)
                t.wait_ge(s_modf, 4)
                t.wait_ge(d_gw, 16 * (NJ + 3))
                last = None
                for j in range(NJ):
                    last = t.matmul(PCS[0:1, 0:64], sa_bf[:, j:j + 1], bwT[:, j, :],
                                    start=(j == 0), stop=(j == NJ - 1))
                last.then_inc(s_bu0, 1)
                for nh in range(2):
                    prow = PW[0:1, 0:512] if nh == 0 else PW[32:33, 0:512]
                    for j in range(NJ):
                        t.matmul(prow, sa_bf[:, j:j + 1],
                                 gwT[:, j, nh * 512:(nh + 1) * 512],
                                 start=(j == 0), stop=False)
                    t.matmul(prow, onesr[0:1, 0:1],
                             gateb_bf[:, nh * 512:(nh + 1) * 512],
                             start=False, stop=True).then_inc(s_gb2, 1)
                for idx in range(48, 64):
                    wm_mm(idx)

                for s in range(NT + 10):
                    i = s - 9
                    if 0 <= i < NT:
                        t.wait_ge(s_norm2, i + 1)
                        if i >= 1:
                            t.wait_ge(s_h2ev, i)
                        last = None
                        for j in range(NJ):
                            last = t.transpose(PT2[:, j * 128:(j + 1) * 128],
                                               xnorm2_bf[:, i % 3,
                                                         j * 128:(j + 1) * 128],
                                               ident[:, :])
                        last.then_inc(s_trans2, 1)
                    i = s - 4
                    if 0 <= i < NT:
                        if i == 0:
                            t.wait_ge(s_cwga, 1)
                            t.wait_ge(s_modev, 6)
                        if i >= 1:
                            t.wait_ge(s_attn, i)
                        t.wait_ge(s_buev, i + 1)
                        last = None
                        for nh in range(2):
                            last = t.matmul(PBS[:, nh * 512:(nh + 1) * 512],
                                            bu_bf[:, i % 2, :],
                                            cwga[:, nh * 512:(nh + 1) * 512],
                                            start=True, stop=True)
                        last.then_inc(s_bp, 1)
                    i = s - 2
                    if 0 <= i < NT:
                        t.wait_ge(s_norm1, i + 1)
                        if i >= 1:
                            t.wait_ge(s_haev, i)
                        last = None
                        for j in range(NJ):
                            last = t.transpose(PT1[:, j * 128:(j + 1) * 128],
                                               xnorm_bf[:, i % 2, j * 128:(j + 1) * 128],
                                               ident[:, :])
                        last.then_inc(s_trans1, 1)
                    i = s - 3
                    if 0 <= i < NT:
                        t.wait_ge(s_haev, i + 1)
                        if i == 0:
                            t.wait_ge(s_bwca, 1)
                            t.wait_ge(s_bu0ev, 1)
                            t.wait_ge(s_gb2ev, 1)
                        if i >= 2:
                            t.wait_ge(s_buev, i - 1)
                        reg = PW[0:64, 128 * (i % 2):128 * (i % 2) + 128]
                        for j in range(NJ):
                            t.matmul(reg, bwT[:, j, :],
                                     ha_fm[:, i % 2, j * 128:(j + 1) * 128],
                                     start=(j == 0), stop=False)
                        t.matmul(reg, bu0_bf[:, :], onesr[0:1, :],
                                 start=False, stop=True).then_inc(s_bu, 1)
                    i = s - 4
                    if 0 <= i < NT:
                        t.wait_ge(s_tmp, i + 1)
                        if i >= 1:
                            t.wait_ge(s_cse, i)
                        last = None
                        for nh in range(2):
                            last = t.matmul(PBS[:, nh * 512:(nh + 1) * 512],
                                            Texcl[:, :],
                                            tmp_bf[:, i % 2, nh * 512:(nh + 1) * 512],
                                            start=True, stop=(i == 0))
                            if i > 0:
                                last = t.matmul(PBS[:, nh * 512:(nh + 1) * 512],
                                                onesr[:, :],
                                                C_sb[:, nh * 512:(nh + 1) * 512],
                                                start=False, stop=True)
                        last.then_inc(s_scan, 1)
                        if i == 0:
                            t.wait_ge(s_bu0ev, 1)
                        if i >= 1:
                            t.wait_ge(s_csa, i)
                        t.matmul(PCS[0:33, :], onesA[:, :], tmp_bf[:, i % 2, 0:512],
                                 start=True, stop=False)
                        t.matmul(PCS[0:33, :], onesB[:, :], tmp_bf[:, i % 2, 512:1024],
                                 start=False, stop=True).then_inc(s_cs, 1)
                    i = s - 3
                    if 0 <= i < NT:
                        if i == 0:
                            t.wait_ge(s_gwca, 1)
                            t.wait_ge(s_modev, 8)
                        if i >= 1:
                            t.wait_ge(s_sig, i)
                        for j in range(NJ):
                            for nh in range(2):
                                t.matmul(PG[:, nh * 512:(nh + 1) * 512],
                                         ha_fm[:, i % 2, j * 128:(j + 1) * 128],
                                         gwT[:, j, nh * 512:(nh + 1) * 512],
                                         start=(j == 0), stop=False)
                        last = None
                        for nh in range(2):
                            last = t.matmul(PG[:, nh * 512:(nh + 1) * 512],
                                            onesr[:, :],
                                            gb2_bf[:, nh * 512:(nh + 1) * 512],
                                            start=False, stop=True)
                        last.then_inc(s_gate, 1)

        pro.close()
        ph1t.close()
        w1f_buf = sbuf2("w1f_buf", [128, NJ, 512], F32)
        wmx2 = sbuf2("wmx2", [128, 2, 1024], F32)
        w1q = sbuf2("w1q", [128, 2, NJ, 512], BF16)
        h1st = sbuf2("h1st", [128, 4, 4, 512], BF16)

        with nc.Block("bx1") as blk:

            @blk.vector
            def _(v):
                c_v.pre(); c_v(v.reduce_max(cols[:, 0, 0:1], hmax2[:, :], AX))
                c_v.pre(); c_v(v.reduce_max(cols[:, 0, 1:2], w1mxc[:, :], AX)
                               , s_bx1v)
                v.wait_ge(s_bx1t, 1)
                c_v.pre(); c_v(v.reduce_max(scl[:, 2:3], PG[0:1, 0:128], AX))
                c_v.pre(); c_v(v.reduce_max(scl[:, 0:1], PG[0:1, 128:256], AX))
                c_v.pre(); c_v(v.tensor_scalar_max(scl[:, 2:3], scl[:, 2:3], 1e-6))
                c_v.pre(); c_v(v.memset(ccb[:, :], 0.0))
                c_v.pre(); c_v(v.tensor_copy(ccb[:, 0:1], scl[:, 2:3]), s_bx1v)
                v.wait_ge(s_bx1g, 2)
                c_v.pre(); c_v(v.tensor_scalar_max(scl[:, 3:4], ccb[:, 0:1], 1e-6))
                c_v.pre(); c_v(v.tensor_scalar_max(scl[:, 0:1], scl[:, 0:1], 1e-6))
                c_v.pre(); c_v(v.reciprocal(scl[:, 4:5], scl[:, 3:4]))
                c_v.pre(); c_v(v.tensor_scalar_mul(scl[:, 4:5], scl[:, 4:5], 127.0))
                c_v.pre(); c_v(v.reciprocal(scl[:, 5:6], scl[:, 0:1]))
                c_v.pre(); c_v(v.tensor_scalar_mul(scl[:, 5:6], scl[:, 5:6], 127.0))
                c_v.pre(); c_v(v.tensor_mul(scl[:, 6:7], scl[:, 0:1], scl[:, 3:4]))
                c_v.pre(); c_v(v.tensor_scalar_mul(scl[:, 6:7], scl[:, 6:7],
                                                   1.0 / (127.0 * 127.0))
                               , s_bx1v)
                v.wait_ge(s_bx1s, 1)
                c_v.pre()
                c_v(v.tensor_scalar(b1s[:, :], b1f[:, :], s1_c[:, 0:1], None,
                                    ALU.mult), s_bx1v)

            @blk.tensor
            def _(t):
                t.wait_ge(s_bx1v, 1)
                t.transpose(PG[0:1, 0:128], cols[:, 0, 0:1], identf[:, :])
                t.transpose(PG[0:1, 128:256], cols[:, 0, 1:2], identf[:, :]).then_inc(s_bx1t, 1)
                t.wait_ge(s_bx1v, 3)
                t.matmul(PW[:, 0:1], onesrf[0:1, :], scl[:, 4:5], start=True, stop=False)
                t.matmul(PW[:, 1:2], onesrf[0:1, :], scl[:, 5:6], start=False, stop=False)
                t.matmul(PW[:, 2:3], onesrf[0:1, :], scl[:, 6:7], start=False,
                         stop=True).then_inc(s_bx1t, 1)

            @blk.scalar
            def _(sc):
                sc.wait_ge(s_bx1t, 2)
                c_s.pre(); c_s(sc.activation(invsx1_c[:, :], PW[:, 0:1], AF.Identity))
                c_s.pre(); c_s(sc.activation(invsw1_c[:, :], PW[:, 1:2], AF.Identity))
                c_s.pre(); c_s(sc.activation(s1_c[:, :], PW[:, 2:3], AF.Identity)
                               , s_bx1s)

            @blk.gpsimd
            def _(g):
                g.wait_ge(s_bx1v, 2)
                g.dma_start(out=cc1i.ap()[:, :], in_=ccb[:, :]).then_inc(d_cc, 16)
                g.wait_ge(d_cc, 16)
                g.collective_compute(
                    "AllReduce", ALU.max,
                    replica_groups=[list(range(n_cores))],
                    ins=[cc1i.ap().opt()], outs=[cc1o.ap().opt()],
                ).then_inc(s_bx1g, 1)
                g.wait_ge(s_bx1g, 1)
                g.dma_start(out=ccb[:, :], in_=cc1o.ap()[:, :]).then_inc(d_cc, 16)
                g.wait_ge(d_cc, 32)
                c_g.pre(); c_g(g.engine_nop(), s_bx1g)

        p1.close()
        F1 = [psum2("F1_%d" % k, [128, 512]) for k in range(8)]
        SETA = tuple(F1[k][:, :] for k in range(4))
        SETB = tuple(F1[k][:, :] for k in range(4, 8))

        with nc.Block("ph2") as blk:

            @blk.sync
            def _(sy):
                sy.wait_ge(d_xpst, 16 * (NT - NT // 2))
                sy.wait_ge(d_xps2, 16 * (NT // 2))
                def h1store(gt):
                    gg2, tt2 = gt // NT5, gt % NT5
                    sy.wait_ge(s_gelu, gt + 1)
                    sy.wait_ge(s_h1mx, gt + 1)
                    sy.dma_start(
                        out=bass.AP(h1_h.tensor,
                                    (tt2 * NM + gg2 * 4) * 128 * 512,
                                    [[512, 128], [128 * 512, 4], [1, 512]]),
                        in_=h1st[:, gt % 4, :, :]).then_inc(d_h1st, 16)
                for gg in range(NG):
                    for gq in range(2):
                        if gg >= 1:
                            sy.wait_ge(s_w1rh, 2 * (gg - 1) + gq + 1)
                        sy.dma_start(out=w1f_buf[:, 4 * gq:4 * gq + 4, :],
                                     in_=bass.AP(w1T_h.tensor,
                                                 gg * 512 + 4 * gq * 128 * H,
                                                 [[H, 128], [128 * H, 4], [1, 512]])
                                     ).then_inc(d_w1 if gq == 0 else d_w1b, 16)
                    if gg >= 1:
                        for tt in range(NT5):
                            h1store((gg - 1) * NT5 + tt)
                    for kk in range(4 * gg, 4 * gg + 4):
                        if kk >= 2:
                            sy.wait_ge(s_w2m, kk - 1)
                        sy.dma_start(out=wmx2[:, kk % 2, :],
                                     in_=bass.AP(w2T_h.tensor, kk * 128 * 1024,
                                                 [[1024, 128], [1, 1024]])
                                     ).then_inc(d_wmx2, 16)
                for tt in range(NT5):
                    h1store((NG - 1) * NT5 + tt)

            @blk.vector
            def _(v):
                def w1round(gg):
                    v.wait_ge(s_bx1s, 1)
                    if gg >= 2:
                        v.wait_ge(s_f1, NT5 * (gg - 1))
                    for hh in range(4):
                        if hh == 0:
                            v.wait_ge(d_w1, 16 * (gg + 1))
                        if hh == 2:
                            v.wait_ge(d_w1b, 16 * (gg + 1))
                        c_v.pre()
                        c_v(v.tensor_scalar(rscr[:, :],
                                            w1f_buf[:, 2 * hh:2 * (hh + 1), :],
                                            invsw1_c[:, 0:1], MAGIC, ALU.mult, ALU.add))
                        c_v.pre()
                        c_v(v.tensor_scalar(w1q[:, gg % 2, 2 * hh:2 * (hh + 1), :],
                                            rscr[:, :], MAGIC, None, ALU.subtract),
                            s_w1rh if hh in (1, 3) else None)
                    c_v.pre()
                    c_v(v.memset(scl[:, 15:16], 0.0), s_w1rnd)
                def h2rnd(tg):
                    v.wait_ge(s_bx1s, 1)
                    for j in range(NJ):
                        c_v.pre()
                        c_v(v.tensor_scalar(rscr[:, 0:512],
                                            h2q[:, j, tg * 512:(tg + 1) * 512],
                                            invsx1_c[:, 0:1], MAGIC, ALU.mult, ALU.add))
                        c_v.pre()
                        c_v(v.tensor_scalar(h2q[:, j, tg * 512:(tg + 1) * 512],
                                            rscr[:, 0:512], MAGIC, None, ALU.subtract)
                            , s_h2rnd)
                h2rnd(0)
                w1round(0)
                if NT5 > 1:
                    h2rnd(1)
                for gg in range(NG):
                    for kk in range(4 * gg, 4 * gg + 4):
                        v.wait_ge(d_wmx2, 32 * (kk // 2 + 1))
                        c_v.pre()
                        c_v(v.reduce_max(w2mxc[:, kk:kk + 1], wmx2[:, kk % 2, :], AX,
                                         apply_absolute_value=True), s_w2m)
                    if gg + 1 < NG:
                        w1round(gg + 1)
                    for tt in range(NT5):
                        gt = gg * NT5 + tt
                        if gg == 0 and tt + 2 < NT5:
                            h2rnd(tt + 2)
                        v.wait_ge(s_gelu, gt + 1)
                        for m in range(4):
                            c_v.pre()
                            c_v(v.reduce_max(
                                hmax1[:, 4 * gt + m:4 * gt + m + 1],
                                h1st[:, gt % 4, m, :], AX,
                                apply_absolute_value=True),
                                s_h1mx if m == 3 else None)

            @blk.tensor
            def _(t):
                for gg in range(NG):
                    t.wait_ge(s_w1rnd, gg + 1)
                    for tt in range(NT5):
                        gt = gg * NT5 + tt
                        pset = SETA if gt % 2 == 0 else SETB
                        t.wait_ge(s_h2rnd, NJ * (tt + 1))
                        if gt >= 2:
                            t.wait_ge(s_gelu, gt - 1)
                        last = None
                        for m in range(4):
                            for j in range(NJ):
                                last = t.matmul(
                                    pset[m],
                                    w1q[:, gg % 2, j, m * 128:(m + 1) * 128],
                                    h2q[:, j, tt * 512:(tt + 1) * 512],
                                    start=(j == 0), stop=(j == NJ - 1))
                        last.then_inc(s_f1, 1)

            @blk.scalar
            def _(sc):
                for gg in range(NG):
                    for tt in range(NT5):
                        gt = gg * NT5 + tt
                        pset = SETA if gt % 2 == 0 else SETB
                        sc.wait_ge(s_f1, gt + 1)
                        if gt >= 4:
                            sc.wait_ge(d_h1st, 64 * (gt // 4))
                            sc.wait_ge(s_h1mx, gt - 3)
                        for m in range(4):
                            c_s.pre()
                            c_s(sc.activation(
                                h1st[:, gt % 4, m, :], pset[m], AF.Gelu,
                                bias=b1s[:, gg * 4 + m:gg * 4 + m + 1],
                                scale=s1_c[:, 0:1]),
                                s_gelu if m == 3 else None)


        ph2s.close()
        s12.close()
        w2f_buf = sbuf3("w2f_buf", [128, 4, D], F32)
        b2row = sbuf3("b2row", [1, D], F32)
        b2s_bf = sbuf3("b2s_bf", [1, D], BF16)
        w2qT = sbuf3("w2qT", [128, NM, D], BF16)
        h1buf = sbuf3("h1buf", [128, NM // 2, 2, 512], BF16)
        xp_t = sbuf3("xp_t", [128, 2, 4, D], F32)
        epi = sbuf3("epi", [128, 4, D], F32)
        gb_rep = sbuf3("gb_rep", [128, D], F32)

        with nc.Block("bx2") as blk:

            @blk.vector
            def _(v):
                v.wait_ge(d_xpst, 16 * (NT - NT // 2))
                v.wait_ge(d_xps2, 16 * (NT // 2))
                v.wait_ge(d_h1st, 16 * NG * NT5)
                c_v.pre()
                c_v(v.reduce_max(cols[:, 0, 0:1], hmax1[:, :], AX))
                c_v.pre()
                c_v(v.reduce_max(cols[:, 0, 2:3], w2mxc[:, :], AX), s_bx2v)
                v.wait_ge(s_bx2t, 1)
                c_v.pre(); c_v(v.reduce_max(scl[:, 1:2], F1[0][0:1, 128:256], AX))
                c_v.pre(); c_v(v.tensor_scalar_max(scl[:, 1:2], scl[:, 1:2], 1e-6))
                c_v.pre(); c_v(v.reciprocal(scl[:, 11:12], scl[:, 1:2]))
                c_v.pre(); c_v(v.tensor_scalar_mul(scl[:, 11:12], scl[:, 11:12], 127.0)
                               , s_bx2w)
                c_v.pre(); c_v(v.reduce_max(scl[:, 8:9], F1[0][0:1, 0:128], AX))
                c_v.pre(); c_v(v.tensor_scalar_max(scl[:, 8:9], scl[:, 8:9], 1e-6))
                c_v.pre(); c_v(v.memset(ccb[:, :], 0.0))
                c_v.pre(); c_v(v.tensor_copy(ccb[:, 0:1], scl[:, 8:9]), s_bx2v)
                v.wait_ge(s_bx2g, 2)
                c_v.pre(); c_v(v.tensor_scalar_max(scl[:, 9:10], ccb[:, 0:1], 1e-6))
                c_v.pre(); c_v(v.reciprocal(scl[:, 10:11], scl[:, 9:10]))
                c_v.pre(); c_v(v.tensor_scalar_mul(scl[:, 10:11], scl[:, 10:11], 127.0))
                c_v.pre(); c_v(v.tensor_mul(scl[:, 12:13], scl[:, 1:2], scl[:, 9:10]))
                c_v.pre(); c_v(v.tensor_scalar_mul(scl[:, 12:13], scl[:, 12:13],
                                                   1.0 / (127.0 * 127.0))
                               , s_bx2v)
                v.wait_ge(s_bx2s, 1)
                v.wait_ge(d_sm, 16 * 13)
                c_v.pre()
                c_v(v.tensor_scalar(b2s_bf[:, :], b2row[:, :], scl[:, 12:13], None,
                                    ALU.mult), s_b2s)

            @blk.tensor
            def _(t):
                t.wait_ge(s_bx2v, 1)
                t.transpose(F1[0][0:1, 0:128], cols[:, 0, 0:1], identf[:, :])
                t.transpose(F1[0][0:1, 128:256], cols[:, 0, 2:3],
                            identf[:, :]).then_inc(s_bx2t, 1)
                t.wait_ge(s_bx2w, 1)
                t.matmul(F1[1][:, 1:2], onesrf[0:1, :], scl[:, 11:12],
                         start=True, stop=True).then_inc(s_bx2t, 1)
                t.wait_ge(s_bx2v, 3)
                t.matmul(F1[1][:, 0:1], onesrf[0:1, :], scl[:, 10:11], start=True, stop=False)
                t.matmul(F1[1][:, 2:3], onesrf[0:1, :], scl[:, 12:13], start=False, stop=True)
                t.wait_ge(s_drow, 2)
                t.matmul(F1[2][:, :], onesrf[0:1, :], gb_row[:, 0:512],
                         start=True, stop=True)
                t.matmul(F1[3][:, :], onesrf[0:1, :], gb_row[:, 512:1024],
                         start=True, stop=True).then_inc(s_bx2t, 1)

            @blk.scalar
            def _(sc):
                sc.wait_ge(s_bx2t, 2)
                c_s.pre(); c_s(sc.activation(invsw2_c[:, :], F1[1][:, 1:2], AF.Identity)
                               , s_w2sc)
                sc.wait_ge(s_bx2t, 3)
                c_s.pre(); c_s(sc.activation(invsx2_c[:, :], F1[1][:, 0:1], AF.Identity))
                c_s.pre(); c_s(sc.activation(s2_c[:, :], F1[1][:, 2:3], AF.Identity))
                c_s.pre(); c_s(sc.activation(gb_rep[:, 0:512], F1[2][:, :], AF.Identity))
                c_s.pre(); c_s(sc.activation(gb_rep[:, 512:1024], F1[3][:, :], AF.Identity)
                               , s_bx2s)

            @blk.gpsimd
            def _(g):
                g.wait_ge(s_bx2v, 2)
                g.dma_start(out=cc2i.ap()[:, :], in_=ccb[:, :]).then_inc(d_cc, 16)
                g.wait_ge(d_cc, 48)
                g.collective_compute(
                    "AllReduce", ALU.max,
                    replica_groups=[list(range(n_cores))],
                    ins=[cc2i.ap().opt()], outs=[cc2o.ap().opt()],
                ).then_inc(s_bx2g, 1)
                g.wait_ge(s_bx2g, 1)
                g.dma_start(out=ccb[:, :], in_=cc2o.ap()[:, :]).then_inc(d_cc, 16)
                g.wait_ge(d_cc, 64)
                c_g.pre(); c_g(g.engine_nop(), s_bx2g)

            @blk.sync
            def _(sy):
                sy.wait_ge(d_xpst, 16 * (NT - NT // 2))
                sy.wait_ge(d_xps2, 16 * (NT // 2))
                sy.wait_ge(d_h1st, 16 * NG * NT5)
                sy.dma_start(out=b2row[:, :], in_=b2_h[:, :]).then_inc(d_sm, 16)
                for half in range(2):
                    for gq in range(4):
                        sy.dma_start(
                            out=h1buf[:, 4 * gq:4 * gq + 4, half, :],
                            in_=bass.AP(h1_h.tensor,
                                        (half * 16 + 4 * gq) * 128 * 512,
                                        [[512, 128], [128 * 512, 4], [1, 512]])
                            ).then_inc(d_h1ld if half == 0 else d_h1lb, 16)
                sy.dma_start(out=xp_t[:, 0, :, :],
                             in_=bass.AP(xp_h.tensor, 0,
                                         [[D, 128], [128 * D, 4], [1, D]])
                             ).then_inc(d_xpld, 16)
                for k in range(4):
                    sy.dma_start(out=w2f_buf[:, k % 4, :],
                                 in_=w2T_h[k * 128:(k + 1) * 128, :]).then_inc(d_w2, 16)

        p2.close()
        Q = [psum3("Q_%d" % q, [128, D]) for q in range(4)]

        with nc.Block("ph3") as blk:

            @blk.sync
            def _(sy):
                sy.wait_ge(d_xpst, 16 * (NT - NT // 2))
                sy.wait_ge(d_xps2, 16 * (NT // 2))
                for tt in range(1, NT5):
                    for half in range(2):
                        hh = 2 * tt + half
                        if hh >= 2:
                            sy.wait_ge(s_h1rnd, hh - 1)
                        sy.wait_ge(s_f2, 64 * (tt - 1) + 32 * (half + 1))
                        for gq in range(4):
                            sy.dma_start(
                                out=h1buf[:, 4 * gq:4 * gq + 4, half, :],
                                in_=bass.AP(h1_h.tensor,
                                            (tt * NM + half * 16 + 4 * gq) * 128 * 512,
                                            [[512, 128], [128 * 512, 4], [1, 512]])
                                ).then_inc(d_h1ld if half == 0 else d_h1lb, 16)
                sy.wait_ge(d_out, 16 * NT5)

            @blk.vector
            def _(v):
                w2k = [0]

                def w2rnd(n):
                    for k in range(w2k[0], min(w2k[0] + n, NM)):
                        if k == 0:
                            v.wait_ge(s_w2sc, 1)
                        v.wait_ge(d_w2, min(64 * (k // 4 + 1), 16 * NM))
                        c_v.pre()
                        c_v(v.tensor_scalar(rscr[:, 0:1024], w2f_buf[:, k % 4, :],
                                            invsw2_c[:, 0:1], MAGIC, ALU.mult,
                                            ALU.add))
                        c_v.pre()
                        c_v(v.tensor_scalar(w2qT[:, k, :], rscr[:, 0:1024], MAGIC,
                                            None, ALU.subtract), s_w2rnd)
                    w2k[0] = min(w2k[0] + n, NM)

                for tt in range(NT5):
                    for half in range(2):
                        v.wait_ge(d_h1ld if half == 0 else d_h1lb, 64 * (tt + 1))
                        v.wait_ge(s_bx2v, 3)
                        for qq in range(8):
                            c_v.pre()
                            c_v(v.tensor_scalar(
                                rscr[:, :],
                                h1buf[:, 2 * qq:2 * (qq + 1), half, :],
                                invsx2_c[:, 0:1], MAGIC, ALU.mult, ALU.add))
                            c_v.pre()
                            c_v(v.tensor_scalar(
                                h1buf[:, 2 * qq:2 * (qq + 1), half, :],
                                rscr[:, :], MAGIC, None, ALU.subtract))
                        c_v.pre()
                        c_v(v.memset(scl[:, 15:16], 0.0)
                            , s_h1rnd)
                        w2rnd(8)
                    if tt == 0:
                        w2rnd(NM)


            @blk.scalar
            def _(sc):
                for k in range(4, NM):
                    sc.wait_ge(s_w2rnd, k - 3)
                    sc.dma_start(out=w2f_buf[:, k % 4, :],
                                 in_=w2T_h[k * 128:(k + 1) * 128, :]).then_inc(d_w2, 16)
                for tt in range(NT5):
                    if tt >= 1:
                        sc.wait_ge(d_xpld, 16 * tt)
                        if tt >= 2:
                            sc.wait_ge(s_oadd, 4 * (tt - 1))
                            sc.wait_ge(d_out, 16 * (tt - 1))
                        sc.dma_start(out=xp_t[:, tt % 2, :, :],
                                     in_=bass.AP(xp_h.tensor, tt * 512 * D,
                                                 [[D, 128], [128 * D, 4], [1, D]])
                                     ).then_inc(d_xpld, 16)
                    if tt >= 1:
                        sc.wait_ge(s_oadd, 4 * tt)
                        sc.dma_start(out=bass.AP(out_h.tensor, (tt - 1) * 512 * D,
                                                 [[D, 128], [128 * D, 4], [1, D]]),
                                     in_=xp_t[:, (tt - 1) % 2, :, :]).then_inc(d_out, 16)
                    for q in range(4):
                        tq = 4 * tt + q
                        sc.wait_ge(s_f2b, 2 * tt + (1 if q < 2 else 2))
                        if tq >= 4:
                            sc.wait_ge(s_oadd, tq - 3)
                        c_s.pre()
                        c_s(sc.activation(epi[:, q, :], Q[q][:, :], AF.Identity,
                                          scale=s2_c[:, 0:1]), s_epi)
                sc.wait_ge(s_oadd, 4 * NT5)
                sc.dma_start(out=bass.AP(out_h.tensor, (NT5 - 1) * 512 * D,
                                         [[D, 128], [128 * D, 4], [1, D]]),
                             in_=xp_t[:, (NT5 - 1) % 2, :, :]).then_inc(d_out, 16)

            @blk.gpsimd
            def _(g):
                for tt in range(NT5):
                    for q in range(4):
                        tq = 4 * tt + q
                        g.wait_ge(s_epi, tq + 1)
                        if tq >= 2:
                            g.wait_ge(s_oadd, tq - 1)
                        if tq == 0:
                            g.wait_ge(s_bx2s, 1)
                        c_g.pre()
                        c_g(g.tensor_mul(epi[:, q, :], epi[:, q, :],
                                         gb_rep[:, :]), s_gbm)
                        g.wait_ge(d_xpld, 16 * (tt + 1))
                        c_g.pre()
                        c_g(g.tensor_add(xp_t[:, tt % 2, q, :], xp_t[:, tt % 2, q, :],
                                         epi[:, q, :]), s_oadd)

            @blk.tensor
            def _(t):
                for tt in range(NT5):
                    for ph, (qs, half) in enumerate((((0, 1), 0), ((2, 3), 0),
                                                     ((0, 1), 1), ((2, 3), 1))):
                        t.wait_ge(s_h1rnd, 2 * tt + half + 1)
                        if ph == 0:
                            t.wait_ge(s_b2s, 1)
                            if tt >= 1:
                                t.wait_ge(s_epi, 4 * (tt - 1) + 2)
                        if ph == 1 and tt >= 1:
                            t.wait_ge(s_epi, 4 * tt)
                        for kk in range(16):
                            k = half * 16 + kk
                            if tt == 0:
                                t.wait_ge(s_w2rnd, k + 1)
                            last = None
                            for q in qs:
                                t.matmul(Q[q][:, 0:512],
                                         h1buf[:, kk, half, :][:, q * 128:(q + 1) * 128],
                                         w2qT[:, k, 0:512], start=(k == 0), stop=False)
                                last = t.matmul(Q[q][:, 512:1024],
                                                h1buf[:, kk, half, :][:, q * 128:(q + 1) * 128],
                                                w2qT[:, k, 512:1024],
                                                start=(k == 0), stop=False)
                            last.then_inc(s_f2, 1)
                        if ph >= 2:
                            last = None
                            for q in qs:
                                t.matmul(Q[q][:, 0:512], onesr[:, :], b2s_bf[:, 0:512],
                                         start=False, stop=True)
                                last = t.matmul(Q[q][:, 512:1024], onesr[:, :],
                                                b2s_bf[:, 512:1024],
                                                start=False, stop=True)
                            last.then_inc(s_f2b, 1)

        s3.close()
        p3.close()

    return nc


def prep_inputs(inputs, n_cores=8):
    L = inputs["L0_attn_x"].shape[1]
    D = 1024
    NM = 32
    x = np.ascontiguousarray(np.asarray(inputs["L0_attn_x"], dtype=np.float32))
    B = x.shape[0]
    shared = {
        "bwT": np.ascontiguousarray(np.asarray(inputs["L0_attn_B_w"]).T),
        "cwT": np.ascontiguousarray(np.asarray(inputs["L0_attn_C_w"]).T),
        "gwT": np.ascontiguousarray(np.asarray(inputs["L0_attn_gate_w"]).T),
        "gate_b": np.asarray(inputs["L0_attn_gate_b"]).reshape(1, D),
        "log_dec": np.asarray(inputs["L0_attn_log_decay"]).reshape(1, D),
        "w1T": np.ascontiguousarray(np.asarray(inputs["L0_ffn_w1"]).T),
        "b1f": np.ascontiguousarray(np.asarray(inputs["L0_ffn_b1"]).reshape(NM, 128).T),
        "w2T": np.ascontiguousarray(np.asarray(inputs["L0_ffn_w2"]).T),
        "b2": np.asarray(inputs["L0_ffn_b2"]).reshape(1, D),
        "wmod": np.ascontiguousarray(np.asarray(inputs["L0_wmod"])),
    }
    shared = {k: v.astype(np.float32) for k, v in shared.items()}
    c = np.asarray(inputs["L0_c"], dtype=np.float32)
    in_maps = []
    for b in range(n_cores):
        m = dict(shared)
        m["x"] = np.ascontiguousarray(x[b % B])
        m["c_fm"] = np.ascontiguousarray(c[b % B].reshape(8, 128).T)
        in_maps.append(m)
    return in_maps




def kernel(**inputs):
    from concourse.bass_utils import run_bass_kernel_spmd
    n_cores = 8
    nc = build_nc(L=4096, n_cores=n_cores)
    in_maps = prep_inputs(inputs, n_cores=n_cores)
    res = run_bass_kernel_spmd(nc, in_maps, core_ids=list(range(n_cores)))
    return np.stack([np.asarray(r["out"]) for r in res.results])


# version 30
# speedup vs baseline: 1.0389x; 1.0012x over previous
import contextlib
import numpy as np
import concourse.bass as bass
import concourse.mybir as mybir

F32 = mybir.dt.float32
F32R = mybir.dt.float32r
BF16 = mybir.dt.bfloat16
AF = mybir.ActivationFunctionType
ALU = mybir.AluOpType
AX = mybir.AxisListType.X
MAGIC = float(np.float32(1.5 * 2 ** 23))
NPBF = mybir.dt.np(BF16)


class Ch:

    def __init__(self, eng, sem_h):
        self.eng, self.sem = eng, sem_h
        self.cnt = {}
        self.last = None

    def __call__(self, inst, purpose=None):
        s = purpose if purpose is not None else self.sem
        k = id(s)
        self.cnt[k] = self.cnt.get(k, 0) + 1
        inst.then_inc(s, 1)
        self.last = (s, self.cnt[k])
        return inst

    def pre(self):
        if self.last is not None:
            self.eng.wait_ge(self.last[0], self.last[1])


def build_nc(L=4096, n_cores=8, dbg=False):
    D, H, S = 1024, 4096, 64
    NT = L // 128
    NT5 = L // 512
    NJ = D // 128
    NM = H // 128
    NG = NM // 4
    WMXN1 = (D * H) // (128 * 512)
    WMXN2 = (H * D) // (128 * 1024)
    MODN = (6 * D) // 512

    nc = bass.Bass("TRN2", target_bir_lowering=False, debug=dbg)

    x_h = nc.dram_tensor("x", [L, D], F32, kind="ExternalInput").ap()
    bwT_h = nc.dram_tensor("bwT", [D, S], F32, kind="ExternalInput").ap()
    cwT_h = nc.dram_tensor("cwT", [S, D], F32, kind="ExternalInput").ap()
    gwT_h = nc.dram_tensor("gwT", [D, D], F32, kind="ExternalInput").ap()
    gateb_h = nc.dram_tensor("gate_b", [1, D], F32, kind="ExternalInput").ap()
    logdec_h = nc.dram_tensor("log_dec", [1, D], F32, kind="ExternalInput").ap()
    w1T_h = nc.dram_tensor("w1T", [D, H], F32, kind="ExternalInput").ap()
    b1f_h = nc.dram_tensor("b1f", [128, NM], F32, kind="ExternalInput").ap()
    w2T_h = nc.dram_tensor("w2T", [H, D], F32, kind="ExternalInput").ap()
    b2_h = nc.dram_tensor("b2", [1, D], F32, kind="ExternalInput").ap()
    cfm_h = nc.dram_tensor("c_fm", [128, NJ], F32, kind="ExternalInput").ap()
    wmod_h = nc.dram_tensor("wmod", [D, 6 * D], F32, kind="ExternalInput").ap()
    out_h = nc.dram_tensor("out", [L, D], F32, kind="ExternalOutput").ap()

    xp_h = nc.dram_tensor("xp_scr", [L, D], F32).ap()
    h1_h = nc.dram_tensor("h1_scr", [NT5 * NM * 128, 512], BF16).ap()
    rowb_h = nc.dram_tensor("rowb_scr", [1, 8 * D], BF16).ap()
    cc1i = nc.dram_tensor("cc1_in", [1, 16], F32)
    cc1o = nc.dram_tensor("cc1_out", [1, 16], F32)
    cc2i = nc.dram_tensor("cc2_in", [1, 16], F32)
    cc2o = nc.dram_tensor("cc2_out", [1, 16], F32)

    t_excl_np = np.triu(np.ones((128, 128), dtype=np.float32), 1).astype(NPBF)
    Texcl_h = nc.inline_tensor(t_excl_np, "Texcl_c").ap()
    ident_h = nc.inline_tensor(np.eye(128, dtype=np.float32).astype(NPBF), "ident_c").ap()
    identf_h = nc.inline_tensor(np.eye(128, dtype=np.float32), "identf_c").ap()
    onesr_h = nc.inline_tensor(np.ones((1, 128), dtype=NPBF), "onesr_c").ap()
    onesrf_h = nc.inline_tensor(np.ones((1, 128), dtype=np.float32), "onesrf_c").ap()
    onesc_h = nc.inline_tensor(np.ones((128, 1), dtype=NPBF), "onesc_c").ap()
    oA = np.zeros((128, 33), dtype=np.float32); oA[:, 0] = 1.0
    oB = np.zeros((128, 33), dtype=np.float32); oB[:, 32] = 1.0
    onesA_h = nc.inline_tensor(oA.astype(NPBF), "onesA_c").ap()
    onesB_h = nc.inline_tensor(oB.astype(NPBF), "onesB_c").ap()
    epsc_h = nc.inline_tensor(np.full((128, 1), 1e-5, dtype=np.float32),
                              "epsc_c").ap()

    stk = contextlib.ExitStack()
    s12 = contextlib.ExitStack()
    p1 = contextlib.ExitStack()
    p2 = contextlib.ExitStack()
    p3 = contextlib.ExitStack()
    ph1t = contextlib.ExitStack()
    pro = contextlib.ExitStack()
    ph2s = contextlib.ExitStack()
    s3 = contextlib.ExitStack()

    def _mk(stack):
        def sbufF(name, shape, dt):
            return stack.enter_context(nc.sbuf_tensor(name, shape, dt))
        return sbufF

    sbuf = _mk(stk)
    sbuf12 = _mk(s12)
    sbuf1 = _mk(ph1t)
    sbufP = _mk(pro)
    sbuf2 = _mk(ph2s)
    sbuf3 = _mk(s3)

    def psum1(name, shape, dt=F32):
        return p1.enter_context(nc.psum_tensor(name, shape, dt))

    def psum2(name, shape, dt=F32):
        return p2.enter_context(nc.psum_tensor(name, shape, dt))

    def psum3(name, shape, dt=F32):
        return p3.enter_context(nc.psum_tensor(name, shape, dt))

    def sem(name):
        return stk.enter_context(nc.semaphore(name))

    with stk:
        Texcl = sbuf("Texcl_s", [128, 128], BF16)
        ident = sbuf("ident_s", [128, 128], BF16)
        identf = sbuf("identf_s", [128, 128], F32)
        onesr = sbuf("onesr_s", [1, 128], BF16)
        onesrf = sbuf("onesrf_s", [1, 128], F32)
        onesc = sbuf("onesc_s", [128, 1], BF16)
        onesA = sbuf("onesA_s", [128, 33], BF16)
        onesB = sbuf("onesB_s", [128, 33], BF16)
        gateb_bf = sbuf("gateb_bf", [1, D], BF16)
        gb2_bf = sbuf("gb2_bf", [1, D], BF16)
        bu0_bf = sbuf("bu0_bf", [1, 64], BF16)
        sa_bf = sbuf("sa_bf", [128, NJ], BF16)
        ca1_fm = sbuf("ca1_fm", [128, NJ], F32)
        sa_fm = sbuf("sa_fm", [128, NJ], F32)
        cb1_fm = sbuf("cb1_fm", [128, NJ], F32)
        sbv_fm = sbuf("sbv_fm", [128, NJ], F32)
        gb_row = sbuf("gb_row", [1, D], F32)
        b1f = sbuf("b1f_s", [128, NM], F32)
        b1s = sbuf("b1s", [128, NM], F32)
        C_sb = sbuf("C_sb", [1, D], BF16)
        C_f32 = sbuf("C_f32", [1, D], F32)
        hmax2 = sbuf("hmax2", [128, 8 * NT], F32)
        hmax1 = sbuf("hmax1", [128, 4 * NG * NT5], F32)
        w1mxc = sbuf("w1mxc", [128, WMXN1], F32)
        w2mxc = sbuf("w2mxc", [128, WMXN2], F32)
        scl = sbuf("scl", [1, 16], F32)
        invsw1_c = sbuf("invsw1_c", [128, 1], F32)
        invsx1_c = sbuf("invsx1_c", [128, 1], F32)
        s1_c = sbuf("s1_c", [128, 1], F32)
        invsw2_c = sbuf("invsw2_c", [128, 1], F32)
        invsx2_c = sbuf("invsx2_c", [128, 1], F32)
        s2_c = sbuf("s2_c", [128, 1], F32)
        ccb = sbuf("ccb", [1, 16], F32)
        epsc = sbuf("epsc", [128, 1], F32)

        cols = sbuf("cols", [128, 2, 40], F32)
        rscr = sbuf("rscr", [128, 1024], F32)
        h2q = sbuf12("h2q", [128, NJ, L], BF16)
        gwT = sbuf12("gwT_s", [128, NJ, D], BF16)
        bwT = sbuf12("bwT_s", [128, NJ, S], BF16)
        cwga = sbuf12("cwga", [64, D], BF16)
        xbuf = sbuf1("xbuf", [128, 6, D], F32)
        xnorm_bf = sbuf1("xnorm_bf", [128, 2, D], BF16)
        xnorm2_bf = sbuf1("xnorm2_bf", [128, 3, D], BF16)
        ha_fm = sbuf1("ha_fm", [128, 2, D], BF16)
        bu_bf = sbuf1("bu_bf", [64, 2, 128], BF16)
        W_sb = sbuf1("W_sb", [128, 3, D], BF16)
        ld_bc = sbuf1("ld_bc", [128, D], F32)
        tmp_bf = sbuf1("tmp_bf", [128, 2, D], BF16)
        gate_sb = sbuf1("gate_sb", [128, 2, D], BF16)
        xp_buf = sbuf1("xp_buf", [128, 3, D], F32)
        tcol = sbuf1("tcol", [128, 4], F32)
        wmx_buf = sbufP("wmx_buf", [128, 2, 512], F32)
        wm_bf = sbufP("wm_bf", [128, 4, 512], BF16)
        silu_bf = sbufP("silu_bf", [128, NJ], BF16)
        silu_f = sbufP("silu_f", [128, NJ], F32)
        wmf = sbufP("wmf", [128, 4, 256], F32)
        cfm = sbufP("cfm_s", [128, NJ], F32)
        mod_b = sbufP("mod_b", [1, 2 * D], BF16)
        mod_f = sbufP("mod_f", [1, D], F32)
        sgn_row = sbufP("sgn_row", [1, D], BF16)
        ldrow_f = sbufP("ldrow_f", [1, D], F32)
        cw_bf = sbufP("cw_bf", [64, D], BF16)

        PT1 = psum1("PT1", [128, D], BF16)
        PW = psum1("PW", [128, 512])
        PBS = psum1("PBS", [128, D])
        PT2 = psum1("PT2", [128, D], BF16)
        PCS = psum1("PCS", [64, 512])
        PG = psum1("PG", [128, D])

        d_sm = sem("d_sm"); d_x = sem("d_x"); d_xpst = sem("d_xpst")
        d_xps2 = sem("d_xps2")
        d_wmx = sem("d_wmx"); d_w1 = sem("d_w1"); d_w2 = sem("d_w2")
        d_w1b = sem("d_w1b"); s_w1rh = sem("s_w1rh"); d_h1lb = sem("d_h1lb")
        d_h1st = sem("d_h1st"); d_h1ld = sem("d_h1ld"); d_xpld = sem("d_xpld")
        d_out = sem("d_out"); d_gw = sem("d_gw"); d_wm = sem("d_wm")
        d_fm = sem("d_fm"); d_cc = sem("d_cc"); d_wmf = sem("d_wmf")
        d_wmx2 = sem("d_wmx2")
        c_v = Ch(nc.vector, sem("c_v"))
        c_s = Ch(nc.scalar, sem("c_s"))
        c_g = Ch(nc.gpsimd, sem("c_g"))
        s_sumx = sem("s_sumx")
        s_cols1 = sem("s_cols1"); s_sqrt1 = sem("s_sqrt1"); s_nb1 = sem("s_nb1")
        s_norm1 = sem("s_norm1"); s_trans1 = sem("s_trans1"); s_haev = sem("s_haev")
        s_bu = sem("s_bu"); s_buev = sem("s_buev"); s_iota = sem("s_iota")
        s_wexp = sem("s_wexp")
        s_bp = sem("s_bp"); s_tmp = sem("s_tmp")
        s_gate = sem("s_gate"); s_sig = sem("s_sig")
        s_scan = sem("s_scan"); s_attn = sem("s_attn")
        s_cs = sem("s_cs"); s_cse = sem("s_cse"); s_cse0 = sem("s_cse0")
        s_xpadd = sem("s_xpadd")
        s_sumx2 = sem("s_sumx2")
        s_pro2 = sem("s_pro2"); s_modf = sem("s_modf")
        s_cols2 = sem("s_cols2"); s_sqrt2 = sem("s_sqrt2"); s_nb2 = sem("s_nb2")
        s_norm2 = sem("s_norm2"); s_trans2 = sem("s_trans2"); s_h2ev = sem("s_h2ev")
        s_h2mx = sem("s_h2mx")
        s_silu = sem("s_silu"); s_sig0 = sem("s_sig0"); s_clamp0 = sem("s_clamp0")
        s_ln0 = sem("s_ln0"); s_ldbf = sem("s_ldbf")
        s_drow = sem("s_drow"); s_gab = sem("s_gab"); s_cwga = sem("s_cwga")
        s_dfm = sem("s_dfm")
        s_pro = sem("s_pro"); s_modev = sem("s_modev")
        s_w1m = sem("s_w1m"); s_w2m = sem("s_w2m")
        s_sabf = sem("s_sabf"); s_bu0 = sem("s_bu0"); s_gb2 = sem("s_gb2")
        s_bu0ev = sem("s_bu0ev"); s_gb2ev = sem("s_gb2ev")
        s_bwca = sem("s_bwca"); s_gwca = sem("s_gwca")
        s_ldbc = sem("s_ldbc"); s_ldbcm = sem("s_ldbcm"); s_csa = sem("s_csa")
        s_bx2w = sem("s_bx2w"); s_w2sc = sem("s_w2sc")
        s_bx1v = sem("s_bx1v"); s_bx1t = sem("s_bx1t"); s_bx1s = sem("s_bx1s")
        s_bx1g = sem("s_bx1g")
        s_h2rnd = sem("s_h2rnd")
        s_w1rnd = sem("s_w1rnd"); s_f1 = sem("s_f1")
        s_gelu = sem("s_gelu"); s_h1mx = sem("s_h1mx")
        s_bx2v = sem("s_bx2v"); s_bx2t = sem("s_bx2t"); s_bx2s = sem("s_bx2s")
        s_bx2g = sem("s_bx2g")
        s_w2rnd = sem("s_w2rnd"); s_h1rnd = sem("s_h1rnd")
        s_f2 = sem("s_f2"); s_f2b = sem("s_f2b")
        s_epi = sem("s_epi"); s_gbm = sem("s_gbm"); s_oadd = sem("s_oadd")
        s_b2s = sem("s_b2s")

        DG_FIX = NJ + 3
        DG_WM0 = DG_FIX
        DG_DROW = DG_WM0 + 6 * NJ
        DG_FM_END = DG_DROW + 2 + 4

        with nc.Block("ph1") as blk:

            @blk.sync
            def _(sy):
                for dst, srct in ((Texcl, Texcl_h), (ident, ident_h), (identf, identf_h),
                                  (onesr, onesr_h), (onesrf, onesrf_h), (onesc, onesc_h),
                                  (onesA, onesA_h), (onesB, onesB_h),
                                  (ldrow_f, logdec_h), (cfm, cfm_h), (b1f, b1f_h),
                                  (epsc, epsc_h)):
                    sy.dma_start(out=dst[:, :], in_=srct[:, :]).then_inc(d_sm, 16)
                for s in range(NT + 10):
                    i = s
                    if i < NT:
                        if i >= 2:
                            sy.wait_ge(s_sumx, i - 1)
                        if i >= 6:
                            sy.wait_ge(s_xpadd, i - 5)
                        sy.dma_start(out=xbuf[:, i % 6, :],
                                     in_=x_h[i * 128:(i + 1) * 128, :]).then_inc(d_x, 16)
                    i = s - 5
                    if 0 <= i < NT:
                        sy.wait_ge(s_xpadd, i + 1)
                        if i >= 2:
                            sy.wait_ge(d_xpst if i % 2 == 0 else d_xps2,
                                       16 * (i // 2))
                        sy.dma_start(out=xp_h[i * 128:(i + 1) * 128, :],
                                     in_=xp_buf[:, i % 3, :]
                                     ).then_inc(d_xpst if i % 2 == 0 else d_xps2, 16)

            @blk.gpsimd
            def _(g):
                for j in range(NJ):
                    g.dma_start(out=gwT[:, j, :],
                                in_=gwT_h[j * 128:(j + 1) * 128, :]).then_inc(d_gw, 16)
                g.dma_start(out=bwT[:, :, :],
                            in_=bass.AP(bwT_h.tensor, 0,
                                        [[S, 128], [128 * S, NJ], [1, S]])
                            ).then_inc(d_gw, 16)
                g.dma_start(out=gateb_bf[:, :], in_=gateb_h[:, :]).then_inc(d_gw, 16)
                g.dma_start(out=cw_bf[:, :], in_=cwT_h[:, :]).then_inc(d_gw, 16)
                def wm_load(idx):
                    n2 = (0, 1, 3, 4)[idx // 16]
                    q = (idx // 8) % 2
                    j = idx % 8
                    if idx >= 4:
                        g.wait_ge(s_pro, idx - 3)
                    g.dma_start(out=wm_bf[:, idx % 4, :],
                                in_=bass.AP(wmod_h.tensor,
                                            j * 128 * 6 * D + n2 * 1024 + q * 512,
                                            [[6 * D, 128], [1, 512]])
                                ).then_inc(d_wm, 16)

                for idx in range(36):
                    wm_load(idx)
                g.wait_ge(s_modev, 4)
                g.dma_start(out=rowb_h[0:1, 0:2 * D],
                            in_=mod_b[:, :]).then_inc(d_fm, 16)
                g.wait_ge(d_fm, 16)
                for t_i, dst in ((0, sa_fm), (1, ca1_fm)):
                    g.dma_start(out=dst[:, :],
                                in_=bass.AP(rowb_h.tensor, t_i * D,
                                            [[1, 128], [128, NJ]]),
                                allow_slow_non_contiguous=True).then_inc(d_fm, 16)
                g.wait_ge(d_fm, 48)
                c_g.pre(); c_g(g.tensor_copy(sa_bf[:, :], sa_fm[:, :]), s_sabf)
                for idx in range(36, 64):
                    wm_load(idx)
                g.wait_ge(s_bu0, 1)
                g.wait_ge(s_dfm, 1)
                for j in range(NJ):
                    c_g.pre()
                    c_g(g.tensor_scalar(bwT[:, j, :], bwT[:, j, :],
                                        ca1_fm[:, j:j + 1], None, ALU.mult),
                        s_bwca if j == NJ - 1 else None)
                g.wait_ge(s_modev, 8)
                g.dma_start(out=rowb_h[0:1, 2 * D:4 * D],
                            in_=mod_b[:, :]).then_inc(d_fm, 16)
                g.wait_ge(d_fm, 64)
                for t_i, dst in ((2, sbv_fm), (3, cb1_fm)):
                    g.dma_start(out=dst[:, :],
                                in_=bass.AP(rowb_h.tensor, t_i * D,
                                            [[1, 128], [128, NJ]]),
                                allow_slow_non_contiguous=True).then_inc(d_fm, 16)
                c_g.pre(); c_g(g.memset(C_f32[:, :], 0.0), s_cse0)
                for s in range(max(NT + 10, WMXN1 // 2 + 1)):
                    i = s
                    if i < NT:
                        if i >= 4:
                            g.wait_ge(s_wexp, i - 3)
                        c_g.pre()
                        c_g(g.iota(tcol[:, i % 4:i % 4 + 1], [[1, 1]], base=i * 128,
                                   channel_multiplier=1,
                                   allow_small_or_imprecise_dtypes=True), s_iota)
                    for k in (2 * s, 2 * s + 1):
                        if k < WMXN1:
                            if k >= 2:
                                g.wait_ge(s_w1m, k - 1)
                            g.dma_start(out=wmx_buf[:, k % 2, :],
                                        in_=bass.AP(w1T_h.tensor, k * 128 * 512,
                                                    [[512, 128], [1, 512]])
                                        ).then_inc(d_wmx, 16)
                    i = s - 1
                    if 0 <= i < NT:
                        g.wait_ge(s_nb1, i + 1)
                        g.wait_ge(d_x, 32 * (i // 2 + 1))
                        if i >= 2:
                            g.wait_ge(s_trans1, i - 1)
                        cc = cols[:, i % 2, :]
                        c_g.pre()
                        c_g(g.tensor_scalar(xnorm_bf[:, i % 2, :], xbuf[:, i % 6, :],
                                            cc[:, 16:17], cc[:, 17:18],
                                            ALU.mult, ALU.add), s_norm1)
                    i = s - 7
                    if 0 <= i < NT:
                        g.wait_ge(s_nb2, i + 1)
                        if i >= 3:
                            g.wait_ge(s_trans2, i - 2)
                        cc = cols[:, i % 2, :]
                        c_g.pre()
                        c_g(g.tensor_scalar(xnorm2_bf[:, i % 3, :], xp_buf[:, i % 3, :],
                                            cc[:, 36:37], cc[:, 37:38],
                                            ALU.mult, ALU.add), s_norm2)
                    i = s - 4
                    if 0 <= i < NT:
                        g.wait_ge(s_attn, i + 1)
                        c_g.pre()
                        c_g(g.tensor_add(xp_buf[:, i % 3, :], xp_buf[:, i % 3, :],
                                         xbuf[:, i % 6, :]), s_xpadd)


            @blk.vector
            def _(v):
                v.wait_ge(s_sig0, 1)
                c_v.pre()
                c_v(v.tensor_scalar_max(ldrow_f[:, :], ldrow_f[:, :], 1e-38), s_clamp0)
                v.wait_ge(s_ln0, 1)
                c_v.pre()
                c_v(v.tensor_scalar_max(ldrow_f[:, :], ldrow_f[:, :], -87.0), s_ldbf)

                def clamp_pm(slot):
                    c_v.pre(); c_v(v.tensor_scalar(sgn_row[:, :], slot, 0.0, None,
                                                   ALU.is_ge))
                    c_v.pre(); c_v(v.tensor_scalar(sgn_row[:, :], sgn_row[:, :], 2.0,
                                                   -1.0, ALU.mult, ALU.add))
                    c_v.pre(); c_v(v.tensor_scalar(rscr[0:1, 0:D], slot, -1.0, None,
                                                   ALU.mult))
                    c_v.pre(); c_v(v.tensor_max(slot, slot, rscr[0:1, 0:D]))
                    c_v.pre(); c_v(v.tensor_scalar_max(slot, slot, 0.25))
                    c_v.pre(); c_v(v.tensor_mul(slot, slot, sgn_row[:, :]))

                v.wait_ge(s_modf, 2)
                clamp_pm(mod_f[:, 0:D])
                c_v.pre(); c_v(v.memset(scl[:, 15:16], 0.0), s_drow)
                v.wait_ge(d_fm, 48)
                c_v.pre(); c_v(v.tensor_scalar_add(ca1_fm[:, :], ca1_fm[:, :], 1.0),
                               s_dfm)
                v.wait_ge(s_gab, 1)
                v.wait_ge(d_gw, 16 * (NJ + 3))
                c_v.pre()
                c_v(v.scalar_tensor_tensor(cwga[:, :], cw_bf[:, :], 0.5,
                                           PG[0:64, 0:D], ALU.mult, ALU.mult), s_cwga)
                v.wait_ge(s_gb2, 2)
                v.wait_ge(s_dfm, 1)
                for j in range(NJ):
                    c_v.pre()
                    c_v(v.tensor_scalar(gwT[:, j, :], gwT[:, j, :],
                                        ca1_fm[:, j:j + 1], None, ALU.mult),
                        s_gwca if j == NJ - 1 else None)
                v.wait_ge(s_modf, 4)
                clamp_pm(mod_f[:, 0:D])
                c_v.pre(); c_v(v.tensor_copy(gb_row[:, :], mod_f[:, 0:D]), s_drow)

                for s in range(max(NT + 10, WMXN1 // 2 + 1)):
                    if s == 2:
                        v.wait_ge(d_fm, 16 * 6)
                        c_v.pre()
                        c_v(v.tensor_scalar_add(cb1_fm[:, :], cb1_fm[:, :], 1.0),
                            s_dfm)
                    i = s
                    if i < NT:
                        v.wait_ge(d_x, 32 * (i // 2 + 1))
                        cc = cols[:, i % 2, :]
                        c_v.pre()
                        c_v(v.bn_stats(cc[:, 0:6], xbuf[:, i % 6, 0:512]))
                        c_v.pre()
                        c_v(v.bn_stats(cc[:, 6:12], xbuf[:, i % 6, 512:1024]), s_sumx)
                        if i >= 2:
                            v.wait_ge(s_sqrt1, i - 1)
                        c_v.pre()
                        c_v(v.bn_aggr(cc[:, 12:14], cc[:, 0:12]), s_cols1)
                    i = s - 1
                    if 0 <= i < NT:
                        v.wait_ge(s_sqrt1, i + 1)
                        if i >= 2:
                            v.wait_ge(s_norm1, i - 1)
                        cc = cols[:, i % 2, :]
                        c_v.pre()
                        c_v(v.reciprocal(cc[:, 16:17], cc[:, 15:16]))
                        c_v.pre()
                        c_v(v.scalar_tensor_tensor(cc[:, 17:18], cc[:, 12:13], -1.0,
                                                   cc[:, 16:17], ALU.mult, ALU.mult),
                            s_nb1)
                    i = s - 4
                    if 0 <= i < NT:
                        v.wait_ge(s_bp, i + 1)
                        v.wait_ge(s_wexp, i + 1)
                        if i >= 2:
                            v.wait_ge(s_cs, i - 1)
                        c_v.pre()
                        c_v(v.tensor_mul(tmp_bf[:, i % 2, :], W_sb[:, i % 3, :],
                                         PBS[:, :]), s_tmp)
                    i = s - 5
                    if 0 <= i < NT:
                        v.wait_ge(s_xpadd, i + 1)
                        cc = cols[:, i % 2, :]
                        c_v.pre()
                        c_v(v.bn_stats(cc[:, 20:26], xp_buf[:, i % 3, 0:512]))
                        c_v.pre()
                        c_v(v.bn_stats(cc[:, 26:32], xp_buf[:, i % 3, 512:1024]),
                            s_sumx2)
                        if i >= 2:
                            v.wait_ge(s_sqrt2, i - 1)
                        c_v.pre()
                        c_v(v.bn_aggr(cc[:, 32:34], cc[:, 20:32]), s_cols2)
                    i = s - 4
                    if 0 <= i < NT:
                        v.wait_ge(s_scan, i + 1)
                        v.wait_ge(s_sig, i + 1)
                        if i >= 3:
                            v.wait_ge(s_norm2, i - 2)
                        if i >= 3:
                            v.wait_ge(d_xpst if (i - 3) % 2 == 0 else d_xps2,
                                      16 * ((i - 3) // 2 + 1))
                        c_v.pre()
                        c_v(v.scalar_tensor_tensor(xp_buf[:, i % 3, :],
                                                   gate_sb[:, i % 2, :], 1.0,
                                                   PBS[:, :], ALU.add, ALU.mult),
                            s_attn)
                        v.wait_ge(s_cs, i + 1)
                        if i == 0:
                            v.wait_ge(s_cse0, 1)
                        if i >= 1:
                            v.wait_ge(s_cse, i)
                        c_v.pre()
                        c_v(v.tensor_add(C_f32[:, 0:512], C_f32[:, 0:512], PCS[0:1, :]))
                        c_v.pre()
                        c_v(v.tensor_add(C_f32[:, 512:1024], C_f32[:, 512:1024],
                                         PCS[32:33, :]), s_csa)
                    i = s - 6
                    if 0 <= i < NT:
                        v.wait_ge(s_sqrt2, i + 1)
                        if i >= 2:
                            v.wait_ge(s_norm2, i - 1)
                        cc = cols[:, i % 2, :]
                        c_v.pre()
                        c_v(v.reciprocal(cc[:, 36:37], cc[:, 35:36]))
                        c_v.pre()
                        c_v(v.scalar_tensor_tensor(cc[:, 37:38], cc[:, 32:33], -1.0,
                                                   cc[:, 36:37], ALU.mult, ALU.mult),
                            s_nb2)
                    for k in (2 * s, 2 * s + 1):
                        if k < WMXN1:
                            v.wait_ge(d_wmx, 32 * (k // 2 + 1))
                            c_v.pre()
                            c_v(v.reduce_max(w1mxc[:, k:k + 1], wmx_buf[:, k % 2, :],
                                             AX, apply_absolute_value=True), s_w1m)
                    i = s - 10
                    if 0 <= i < NT:
                        v.wait_ge(s_h2ev, i + 1)
                        c_v.pre()
                        c_v(v.reduce_max(hmax2[:, 8 * i:8 * i + 8],
                                         h2q[:, :, i * 128:(i + 1) * 128], AX,
                                         apply_absolute_value=True), s_h2mx)

            @blk.scalar
            def _(sc):
                sc.wait_ge(d_sm, 16 * 12)
                c_s.pre()
                c_s(sc.activation(silu_bf[:, :], cfm[:, :], AF.Silu))
                c_s.pre()
                c_s(sc.activation(silu_f[:, :], cfm[:, :], AF.Silu), s_silu)
                c_s.pre()
                c_s(sc.activation(ldrow_f[:, :], ldrow_f[:, :], AF.Sigmoid), s_sig0)
                sc.wait_ge(s_clamp0, 1)
                c_s.pre()
                c_s(sc.activation(ldrow_f[:, :], ldrow_f[:, :], AF.Ln), s_ln0)
                psl = (PBS[0:1, 0:512], PBS[0:1, 512:1024],
                       PG[0:1, 0:512], PG[0:1, 512:1024])

                def modev(n512):
                    seg = n512 // 2
                    if n512 < 8:
                        sc.wait_ge(s_pro, 8 * (n512 + 1))
                    else:
                        sc.wait_ge(s_pro2, 16 * (n512 - 7))
                    if n512 == 4:
                        sc.wait_ge(d_fm, 16)
                    if n512 == 10:
                        sc.wait_ge(s_gab, 1)
                    c_s.pre()
                    if n512 < 8:
                        dst = mod_b[:, (seg % 2) * D + (n512 % 2) * 512:
                                    (seg % 2) * D + (n512 % 2) * 512 + 512]
                        c_s(sc.activation(dst, psl[n512 % 4], AF.Identity), s_modev)
                    else:
                        dst = mod_f[:, (n512 % 2) * 512:(n512 % 2) * 512 + 512]
                        psrc = (PCS[0:1, 0:512] if n512 % 2 == 0
                                else PCS[32:33, 0:512])
                        c_s(sc.activation(dst, psrc, AF.Identity), s_modf)

                fidx = 0
                for n2 in (2, 5):
                    for q in range(2):
                        for q2 in range(2):
                            for j in range(NJ):
                                if fidx >= 4:
                                    sc.wait_ge(s_pro2, fidx - 3)
                                sc.dma_start(out=wmf[:, fidx % 4, :],
                                             in_=bass.AP(wmod_h.tensor,
                                                         j * 128 * 6 * D + n2 * 1024
                                                         + q * 512 + q2 * 256,
                                                         [[6 * D, 128], [1, 256]])
                                             ).then_inc(d_wmf, 16)
                                fidx += 1
                    if n2 == 2:
                        modev(8)
                        modev(9)
                for n512 in range(4):
                    modev(n512)
                modev(10)
                modev(11)
                sc.wait_ge(s_ldbcm, 1)
                c_s.pre()
                c_s(sc.activation(ld_bc[:, :], PG[:, :], AF.Identity), s_ldbc)
                for n512 in range(4, 8):
                    modev(n512)
                sc.wait_ge(s_bu0, 1)
                c_s.pre()
                c_s(sc.activation(bu0_bf[:, :], PCS[0:1, 0:64], AF.Identity), s_bu0ev)
                sc.wait_ge(s_gb2, 2)
                c_s.pre()
                c_s(sc.activation(gb2_bf[:, 0:512], PW[0:1, 0:512], AF.Identity))
                c_s.pre()
                c_s(sc.activation(gb2_bf[:, 512:1024], PW[32:33, 0:512], AF.Identity),
                    s_gb2ev)


                for s in range(NT + 10):
                    i = s - 1
                    if 0 <= i < NT:
                        sc.wait_ge(s_cols1, i + 1)
                        if i >= 2:
                            sc.wait_ge(s_norm1, i - 1)
                        cc = cols[:, i % 2, :]
                        c_s.pre()
                        c_s(sc.activation(cc[:, 15:16], cc[:, 13:14], AF.Sqrt,
                                          bias=epsc[:, 0:1]), s_sqrt1)
                    i = s - 6
                    if 0 <= i < NT:
                        sc.wait_ge(s_cols2, i + 1)
                        if i >= 2:
                            sc.wait_ge(s_norm2, i - 1)
                        cc = cols[:, i % 2, :]
                        c_s.pre()
                        c_s(sc.activation(cc[:, 35:36], cc[:, 33:34], AF.Sqrt,
                                          bias=epsc[:, 0:1]), s_sqrt2)
                    i = s - 4
                    if 0 <= i < NT:
                        sc.wait_ge(s_gate, i + 1)
                        if i >= 2:
                            sc.wait_ge(s_attn, i - 1)
                        c_s.pre()
                        c_s(sc.activation(gate_sb[:, i % 2, :], PG[:, :], AF.Tanh,
                                          scale=0.5), s_sig)
                    i = s - 2
                    if 0 <= i < NT:
                        sc.wait_ge(s_iota, i + 1)
                        if i == 0:
                            sc.wait_ge(s_ldbc, 1)
                        if i >= 3:
                            sc.wait_ge(s_tmp, i - 2)
                        c_s.pre()
                        c_s(sc.activation(W_sb[:, i % 3, :], ld_bc[:, :], AF.Exp,
                                          scale=tcol[:, i % 4:i % 4 + 1]), s_wexp)
                    i = s - 2
                    if 0 <= i < NT:
                        sc.wait_ge(s_trans1, i + 1)
                        if i >= 2:
                            sc.wait_ge(s_gate, i - 1)
                        c_s.pre()
                        c_s(sc.activation(ha_fm[:, i % 2, :], PT1[:, :], AF.Identity),
                            s_haev)
                    i = s - 3
                    if 0 <= i < NT:
                        sc.wait_ge(s_bu, i + 1)
                        if i >= 2:
                            sc.wait_ge(s_bp, i - 1)
                        c_s.pre()
                        c_s(sc.activation(bu_bf[:, i % 2, :],
                                          PW[0:64, 128 * (i % 2):128 * (i % 2) + 128],
                                          AF.Identity), s_buev)
                    i = s - 9
                    if 0 <= i < NT:
                        sc.wait_ge(s_trans2, i + 1)
                        if i == 0:
                            sc.wait_ge(s_dfm, 2)
                        for j in range(NJ):
                            c_s.pre()
                            c_s(sc.activation(h2q[:, j, i * 128:(i + 1) * 128],
                                              PT2[:, j * 128:(j + 1) * 128],
                                              AF.Identity, bias=sbv_fm[:, j:j + 1],
                                              scale=cb1_fm[:, j:j + 1]),
                                s_h2ev if j == NJ - 1 else None)
                    i = s - 4
                    if 0 <= i < NT:
                        sc.wait_ge(s_csa, i + 1)
                        c_s.pre()
                        c_s(sc.activation(C_sb[:, :], C_f32[:, :], AF.Identity), s_cse)

            @blk.tensor
            def _(t):
                t.wait_ge(s_silu, 1)
                psl = (PBS[0:1, 0:512], PBS[0:1, 512:1024],
                       PG[0:1, 0:512], PG[0:1, 512:1024])

                def wm_mm(idx):
                    n512 = idx // 8
                    if idx == 32:
                        t.wait_ge(s_modev, 2)
                    if idx == 48:
                        t.wait_ge(s_modev, 4)
                        t.wait_ge(s_ldbc, 1)
                    t.wait_ge(d_wm, min(64 * (idx // 4 + 1), 16 * 64))
                    t.matmul(psl[n512 % 4], silu_bf[:, idx % 8:idx % 8 + 1],
                             wm_bf[:, idx % 4, :],
                             start=(idx % 8 == 0), stop=(idx % 8 == NJ - 1)
                             ).then_inc(s_pro, 1)

                for i in range(32):
                    for fidx in (2 * i, 2 * i + 1):
                        seg = fidx // 32
                        q = (fidx // 16) % 2
                        j = fidx % 8
                        n512 = 2 * (4 + seg) + q
                        if fidx == 32:
                            t.wait_ge(s_modf, 2)
                        row = (PCS[0:1, 0:512] if n512 % 2 == 0
                               else PCS[32:33, 0:512])
                        q2 = (fidx // 8) % 2
                        t.wait_ge(d_wmf, min(64 * (fidx // 4 + 1), 16 * 64))
                        t.matmul(row[:, q2 * 256:q2 * 256 + 256],
                                 silu_f[:, j:j + 1], wmf[:, fidx % 4, :],
                                 start=(j == 0), stop=(j == NJ - 1)
                                 ).then_inc(s_pro2, 1)
                    wm_mm(i)
                t.wait_ge(s_drow, 1)
                t.wait_ge(s_modev, 4)
                t.matmul(PG[0:64, 0:512], onesrf[0:1, 0:64],
                         mod_f[:, 0:512], start=True, stop=True)
                t.matmul(PG[0:64, 512:1024], onesrf[0:1, 0:64],
                         mod_f[:, 512:1024], start=True,
                         stop=True).then_inc(s_gab, 1)
                t.wait_ge(s_cwga, 1)
                t.wait_ge(s_ldbf, 1)
                t.matmul(PG[:, 0:512], onesrf[0:1, :], ldrow_f[:, 0:512],
                         start=True, stop=True)
                t.matmul(PG[:, 512:1024], onesrf[0:1, :], ldrow_f[:, 512:1024],
                         start=True, stop=True).then_inc(s_ldbcm, 1)
                for idx in range(32, 48):
                    wm_mm(idx)
                t.wait_ge(s_sabf, 1)
                t.wait_ge(s_modf, 4)
                t.wait_ge(d_gw, 16 * (NJ + 3))
                last = None
                for j in range(NJ):
                    last = t.matmul(PCS[0:1, 0:64], sa_bf[:, j:j + 1], bwT[:, j, :],
                                    start=(j == 0), stop=(j == NJ - 1))
                last.then_inc(s_bu0, 1)
                for nh in range(2):
                    prow = PW[0:1, 0:512] if nh == 0 else PW[32:33, 0:512]
                    for j in range(NJ):
                        t.matmul(prow, sa_bf[:, j:j + 1],
                                 gwT[:, j, nh * 512:(nh + 1) * 512],
                                 start=(j == 0), stop=False)
                    t.matmul(prow, onesr[0:1, 0:1],
                             gateb_bf[:, nh * 512:(nh + 1) * 512],
                             start=False, stop=True).then_inc(s_gb2, 1)
                for idx in range(48, 64):
                    wm_mm(idx)

                for s in range(NT + 10):
                    i = s - 9
                    if 0 <= i < NT:
                        t.wait_ge(s_norm2, i + 1)
                        if i >= 1:
                            t.wait_ge(s_h2ev, i)
                        last = None
                        for j in range(NJ):
                            last = t.transpose(PT2[:, j * 128:(j + 1) * 128],
                                               xnorm2_bf[:, i % 3,
                                                         j * 128:(j + 1) * 128],
                                               ident[:, :])
                        last.then_inc(s_trans2, 1)
                    i = s - 4
                    if 0 <= i < NT:
                        if i == 0:
                            t.wait_ge(s_cwga, 1)
                            t.wait_ge(s_modev, 6)
                        if i >= 1:
                            t.wait_ge(s_attn, i)
                        t.wait_ge(s_buev, i + 1)
                        last = None
                        for nh in range(2):
                            last = t.matmul(PBS[:, nh * 512:(nh + 1) * 512],
                                            bu_bf[:, i % 2, :],
                                            cwga[:, nh * 512:(nh + 1) * 512],
                                            start=True, stop=True)
                        last.then_inc(s_bp, 1)
                    i = s - 2
                    if 0 <= i < NT:
                        t.wait_ge(s_norm1, i + 1)
                        if i >= 1:
                            t.wait_ge(s_haev, i)
                        last = None
                        for j in range(NJ):
                            last = t.transpose(PT1[:, j * 128:(j + 1) * 128],
                                               xnorm_bf[:, i % 2, j * 128:(j + 1) * 128],
                                               ident[:, :])
                        last.then_inc(s_trans1, 1)
                    i = s - 3
                    if 0 <= i < NT:
                        t.wait_ge(s_haev, i + 1)
                        if i == 0:
                            t.wait_ge(s_bwca, 1)
                            t.wait_ge(s_bu0ev, 1)
                            t.wait_ge(s_gb2ev, 1)
                        if i >= 2:
                            t.wait_ge(s_buev, i - 1)
                        reg = PW[0:64, 128 * (i % 2):128 * (i % 2) + 128]
                        for j in range(NJ):
                            t.matmul(reg, bwT[:, j, :],
                                     ha_fm[:, i % 2, j * 128:(j + 1) * 128],
                                     start=(j == 0), stop=False)
                        t.matmul(reg, bu0_bf[:, :], onesr[0:1, :],
                                 start=False, stop=True).then_inc(s_bu, 1)
                    i = s - 4
                    if 0 <= i < NT:
                        t.wait_ge(s_tmp, i + 1)
                        if i >= 1:
                            t.wait_ge(s_cse, i)
                        last = None
                        for nh in range(2):
                            last = t.matmul(PBS[:, nh * 512:(nh + 1) * 512],
                                            Texcl[:, :],
                                            tmp_bf[:, i % 2, nh * 512:(nh + 1) * 512],
                                            start=True, stop=(i == 0))
                            if i > 0:
                                last = t.matmul(PBS[:, nh * 512:(nh + 1) * 512],
                                                onesr[:, :],
                                                C_sb[:, nh * 512:(nh + 1) * 512],
                                                start=False, stop=True)
                        last.then_inc(s_scan, 1)
                        if i == 0:
                            t.wait_ge(s_bu0ev, 1)
                        if i >= 1:
                            t.wait_ge(s_csa, i)
                        t.matmul(PCS[0:33, :], onesA[:, :], tmp_bf[:, i % 2, 0:512],
                                 start=True, stop=False)
                        t.matmul(PCS[0:33, :], onesB[:, :], tmp_bf[:, i % 2, 512:1024],
                                 start=False, stop=True).then_inc(s_cs, 1)
                    i = s - 3
                    if 0 <= i < NT:
                        if i == 0:
                            t.wait_ge(s_gwca, 1)
                            t.wait_ge(s_modev, 8)
                        if i >= 1:
                            t.wait_ge(s_sig, i)
                        for j in range(NJ):
                            for nh in range(2):
                                t.matmul(PG[:, nh * 512:(nh + 1) * 512],
                                         ha_fm[:, i % 2, j * 128:(j + 1) * 128],
                                         gwT[:, j, nh * 512:(nh + 1) * 512],
                                         start=(j == 0), stop=False)
                        last = None
                        for nh in range(2):
                            last = t.matmul(PG[:, nh * 512:(nh + 1) * 512],
                                            onesr[:, :],
                                            gb2_bf[:, nh * 512:(nh + 1) * 512],
                                            start=False, stop=True)
                        last.then_inc(s_gate, 1)

        pro.close()
        ph1t.close()
        w1f_buf = sbuf2("w1f_buf", [128, NJ, 512], F32)
        wmx2 = sbuf2("wmx2", [128, 2, 1024], F32)
        w1q = sbuf2("w1q", [128, 2, NJ, 512], BF16)
        h1st = sbuf2("h1st", [128, 4, 4, 512], BF16)

        with nc.Block("bx1") as blk:

            @blk.vector
            def _(v):
                c_v.pre(); c_v(v.reduce_max(cols[:, 0, 0:1], hmax2[:, :], AX))
                c_v.pre(); c_v(v.reduce_max(cols[:, 0, 1:2], w1mxc[:, :], AX)
                               , s_bx1v)
                v.wait_ge(s_bx1t, 1)
                c_v.pre(); c_v(v.reduce_max(scl[:, 2:3], PG[0:1, 0:128], AX))
                c_v.pre(); c_v(v.reduce_max(scl[:, 0:1], PG[0:1, 128:256], AX))
                c_v.pre(); c_v(v.tensor_scalar_max(scl[:, 2:3], scl[:, 2:3], 1e-6))
                c_v.pre(); c_v(v.memset(ccb[:, :], 0.0))
                c_v.pre(); c_v(v.tensor_copy(ccb[:, 0:1], scl[:, 2:3]), s_bx1v)
                v.wait_ge(s_bx1g, 2)
                c_v.pre(); c_v(v.tensor_scalar_max(scl[:, 3:4], ccb[:, 0:1], 1e-6))
                c_v.pre(); c_v(v.tensor_scalar_max(scl[:, 0:1], scl[:, 0:1], 1e-6))
                c_v.pre(); c_v(v.reciprocal(scl[:, 4:5], scl[:, 3:4]))
                c_v.pre(); c_v(v.tensor_scalar_mul(scl[:, 4:5], scl[:, 4:5], 127.0))
                c_v.pre(); c_v(v.reciprocal(scl[:, 5:6], scl[:, 0:1]))
                c_v.pre(); c_v(v.tensor_scalar_mul(scl[:, 5:6], scl[:, 5:6], 127.0))
                c_v.pre(); c_v(v.tensor_mul(scl[:, 6:7], scl[:, 0:1], scl[:, 3:4]))
                c_v.pre(); c_v(v.tensor_scalar_mul(scl[:, 6:7], scl[:, 6:7],
                                                   1.0 / (127.0 * 127.0))
                               , s_bx1v)
                v.wait_ge(s_bx1s, 1)
                c_v.pre()
                c_v(v.tensor_scalar(b1s[:, :], b1f[:, :], s1_c[:, 0:1], None,
                                    ALU.mult), s_bx1v)

            @blk.tensor
            def _(t):
                t.wait_ge(s_bx1v, 1)
                t.transpose(PG[0:1, 0:128], cols[:, 0, 0:1], identf[:, :])
                t.transpose(PG[0:1, 128:256], cols[:, 0, 1:2], identf[:, :]).then_inc(s_bx1t, 1)
                t.wait_ge(s_bx1v, 3)
                t.matmul(PW[:, 0:1], onesrf[0:1, :], scl[:, 4:5], start=True, stop=False)
                t.matmul(PW[:, 1:2], onesrf[0:1, :], scl[:, 5:6], start=False, stop=False)
                t.matmul(PW[:, 2:3], onesrf[0:1, :], scl[:, 6:7], start=False,
                         stop=True).then_inc(s_bx1t, 1)

            @blk.scalar
            def _(sc):
                sc.wait_ge(s_bx1t, 2)
                c_s.pre(); c_s(sc.activation(invsx1_c[:, :], PW[:, 0:1], AF.Identity))
                c_s.pre(); c_s(sc.activation(invsw1_c[:, :], PW[:, 1:2], AF.Identity))
                c_s.pre(); c_s(sc.activation(s1_c[:, :], PW[:, 2:3], AF.Identity)
                               , s_bx1s)

            @blk.gpsimd
            def _(g):
                g.wait_ge(s_bx1v, 2)
                g.dma_start(out=cc1i.ap()[:, :], in_=ccb[:, :]).then_inc(d_cc, 16)
                g.wait_ge(d_cc, 16)
                g.collective_compute(
                    "AllReduce", ALU.max,
                    replica_groups=[list(range(n_cores))],
                    ins=[cc1i.ap().opt()], outs=[cc1o.ap().opt()],
                ).then_inc(s_bx1g, 1)
                g.wait_ge(s_bx1g, 1)
                g.dma_start(out=ccb[:, :], in_=cc1o.ap()[:, :]).then_inc(d_cc, 16)
                g.wait_ge(d_cc, 32)
                c_g.pre(); c_g(g.engine_nop(), s_bx1g)

        p1.close()
        F1 = [psum2("F1_%d" % k, [128, 512]) for k in range(8)]
        SETA = tuple(F1[k][:, :] for k in range(4))
        SETB = tuple(F1[k][:, :] for k in range(4, 8))

        with nc.Block("ph2") as blk:

            @blk.sync
            def _(sy):
                sy.wait_ge(d_xpst, 16 * (NT - NT // 2))
                sy.wait_ge(d_xps2, 16 * (NT // 2))
                def h1store(gt):
                    gg2, tt2 = gt // NT5, gt % NT5
                    sy.wait_ge(s_gelu, gt + 1)
                    sy.wait_ge(s_h1mx, gt + 1)
                    sy.dma_start(
                        out=bass.AP(h1_h.tensor,
                                    (tt2 * NM + gg2 * 4) * 128 * 512,
                                    [[512, 128], [128 * 512, 4], [1, 512]]),
                        in_=h1st[:, gt % 4, :, :]).then_inc(d_h1st, 16)
                for gg in range(NG):
                    for gq in range(2):
                        if gg >= 1:
                            sy.wait_ge(s_w1rh, 2 * (gg - 1) + gq + 1)
                        sy.dma_start(out=w1f_buf[:, 4 * gq:4 * gq + 4, :],
                                     in_=bass.AP(w1T_h.tensor,
                                                 gg * 512 + 4 * gq * 128 * H,
                                                 [[H, 128], [128 * H, 4], [1, 512]])
                                     ).then_inc(d_w1 if gq == 0 else d_w1b, 16)
                    if gg >= 1:
                        for tt in range(NT5):
                            h1store((gg - 1) * NT5 + tt)
                    for kk in range(4 * gg, 4 * gg + 4):
                        if kk >= 2:
                            sy.wait_ge(s_w2m, kk - 1)
                        sy.dma_start(out=wmx2[:, kk % 2, :],
                                     in_=bass.AP(w2T_h.tensor, kk * 128 * 1024,
                                                 [[1024, 128], [1, 1024]])
                                     ).then_inc(d_wmx2, 16)
                for tt in range(NT5):
                    h1store((NG - 1) * NT5 + tt)

            @blk.vector
            def _(v):
                def w1round(gg):
                    v.wait_ge(s_bx1s, 1)
                    if gg >= 2:
                        v.wait_ge(s_f1, NT5 * (gg - 1))
                    for hh in range(4):
                        if hh == 0:
                            v.wait_ge(d_w1, 16 * (gg + 1))
                        if hh == 2:
                            v.wait_ge(d_w1b, 16 * (gg + 1))
                        c_v.pre()
                        c_v(v.tensor_scalar(rscr[:, :],
                                            w1f_buf[:, 2 * hh:2 * (hh + 1), :],
                                            invsw1_c[:, 0:1], MAGIC, ALU.mult, ALU.add))
                        c_v.pre()
                        c_v(v.tensor_scalar(w1q[:, gg % 2, 2 * hh:2 * (hh + 1), :],
                                            rscr[:, :], MAGIC, None, ALU.subtract),
                            s_w1rh if hh in (1, 3) else None)
                    c_v.pre()
                    c_v(v.memset(scl[:, 15:16], 0.0), s_w1rnd)
                def h2rnd(tg):
                    v.wait_ge(s_bx1s, 1)
                    for j in range(NJ):
                        c_v.pre()
                        c_v(v.tensor_scalar(rscr[:, 0:512],
                                            h2q[:, j, tg * 512:(tg + 1) * 512],
                                            invsx1_c[:, 0:1], MAGIC, ALU.mult, ALU.add))
                        c_v.pre()
                        c_v(v.tensor_scalar(h2q[:, j, tg * 512:(tg + 1) * 512],
                                            rscr[:, 0:512], MAGIC, None, ALU.subtract)
                            , s_h2rnd)
                h2rnd(0)
                w1round(0)
                if NT5 > 1:
                    h2rnd(1)
                for gg in range(NG):
                    for kk in range(4 * gg, 4 * gg + 4):
                        v.wait_ge(d_wmx2, 32 * (kk // 2 + 1))
                        c_v.pre()
                        c_v(v.reduce_max(w2mxc[:, kk:kk + 1], wmx2[:, kk % 2, :], AX,
                                         apply_absolute_value=True), s_w2m)
                    if gg + 1 < NG:
                        w1round(gg + 1)
                    for tt in range(NT5):
                        gt = gg * NT5 + tt
                        if gg == 0 and tt + 2 < NT5:
                            h2rnd(tt + 2)
                        v.wait_ge(s_gelu, gt + 1)
                        for m in range(4):
                            c_v.pre()
                            c_v(v.reduce_max(
                                hmax1[:, 4 * gt + m:4 * gt + m + 1],
                                h1st[:, gt % 4, m, :], AX,
                                apply_absolute_value=True),
                                s_h1mx if m == 3 else None)

            @blk.tensor
            def _(t):
                for gg in range(NG):
                    t.wait_ge(s_w1rnd, gg + 1)
                    for tt in range(NT5):
                        gt = gg * NT5 + tt
                        pset = SETA if gt % 2 == 0 else SETB
                        t.wait_ge(s_h2rnd, NJ * (tt + 1))
                        if gt >= 2:
                            t.wait_ge(s_gelu, gt - 1)
                        last = None
                        for m in range(4):
                            for j in range(NJ):
                                last = t.matmul(
                                    pset[m],
                                    w1q[:, gg % 2, j, m * 128:(m + 1) * 128],
                                    h2q[:, j, tt * 512:(tt + 1) * 512],
                                    start=(j == 0), stop=(j == NJ - 1))
                        last.then_inc(s_f1, 1)

            @blk.scalar
            def _(sc):
                for gg in range(NG):
                    for tt in range(NT5):
                        gt = gg * NT5 + tt
                        pset = SETA if gt % 2 == 0 else SETB
                        sc.wait_ge(s_f1, gt + 1)
                        if gt >= 4:
                            sc.wait_ge(d_h1st, 64 * (gt // 4))
                            sc.wait_ge(s_h1mx, gt - 3)
                        for m in range(4):
                            c_s.pre()
                            c_s(sc.activation(
                                h1st[:, gt % 4, m, :], pset[m], AF.Gelu,
                                bias=b1s[:, gg * 4 + m:gg * 4 + m + 1],
                                scale=s1_c[:, 0:1]),
                                s_gelu if m == 3 else None)


        ph2s.close()
        s12.close()
        w2f_buf = sbuf3("w2f_buf", [128, 4, D], F32)
        b2row = sbuf3("b2row", [1, D], F32)
        b2s_bf = sbuf3("b2s_bf", [1, D], BF16)
        w2qT = sbuf3("w2qT", [128, NM, D], BF16)
        h1buf = sbuf3("h1buf", [128, NM // 2, 2, 512], BF16)
        xp_t = sbuf3("xp_t", [128, 2, 4, D], F32)
        epi = sbuf3("epi", [128, 4, D], F32)
        gb_rep = sbuf3("gb_rep", [128, D], F32)

        with nc.Block("bx2") as blk:

            @blk.vector
            def _(v):
                v.wait_ge(d_xpst, 16 * (NT - NT // 2))
                v.wait_ge(d_xps2, 16 * (NT // 2))
                v.wait_ge(d_h1st, 16 * NG * NT5)
                c_v.pre()
                c_v(v.reduce_max(cols[:, 0, 0:1], hmax1[:, :], AX))
                c_v.pre()
                c_v(v.reduce_max(cols[:, 0, 2:3], w2mxc[:, :], AX), s_bx2v)
                v.wait_ge(s_bx2t, 1)
                c_v.pre(); c_v(v.reduce_max(scl[:, 1:2], F1[0][0:1, 128:256], AX))
                c_v.pre(); c_v(v.tensor_scalar_max(scl[:, 1:2], scl[:, 1:2], 1e-6))
                c_v.pre(); c_v(v.reciprocal(scl[:, 11:12], scl[:, 1:2]))
                c_v.pre(); c_v(v.tensor_scalar_mul(scl[:, 11:12], scl[:, 11:12], 127.0)
                               , s_bx2w)
                c_v.pre(); c_v(v.reduce_max(scl[:, 8:9], F1[0][0:1, 0:128], AX))
                c_v.pre(); c_v(v.tensor_scalar_max(scl[:, 8:9], scl[:, 8:9], 1e-6))
                c_v.pre(); c_v(v.memset(ccb[:, :], 0.0))
                c_v.pre(); c_v(v.tensor_copy(ccb[:, 0:1], scl[:, 8:9]), s_bx2v)
                v.wait_ge(s_bx2g, 2)
                c_v.pre(); c_v(v.tensor_scalar_max(scl[:, 9:10], ccb[:, 0:1], 1e-6))
                c_v.pre(); c_v(v.reciprocal(scl[:, 10:11], scl[:, 9:10]))
                c_v.pre(); c_v(v.tensor_scalar_mul(scl[:, 10:11], scl[:, 10:11], 127.0))
                c_v.pre(); c_v(v.tensor_mul(scl[:, 12:13], scl[:, 1:2], scl[:, 9:10]))
                c_v.pre(); c_v(v.tensor_scalar_mul(scl[:, 12:13], scl[:, 12:13],
                                                   1.0 / (127.0 * 127.0))
                               , s_bx2v)
                v.wait_ge(s_bx2s, 1)
                v.wait_ge(d_sm, 16 * 13)
                c_v.pre()
                c_v(v.tensor_scalar(b2s_bf[:, :], b2row[:, :], scl[:, 12:13], None,
                                    ALU.mult), s_b2s)

            @blk.tensor
            def _(t):
                t.wait_ge(s_bx2v, 1)
                t.transpose(F1[0][0:1, 0:128], cols[:, 0, 0:1], identf[:, :])
                t.transpose(F1[0][0:1, 128:256], cols[:, 0, 2:3],
                            identf[:, :]).then_inc(s_bx2t, 1)
                t.wait_ge(s_bx2w, 1)
                t.matmul(F1[1][:, 1:2], onesrf[0:1, :], scl[:, 11:12],
                         start=True, stop=True).then_inc(s_bx2t, 1)
                t.wait_ge(s_bx2v, 3)
                t.matmul(F1[1][:, 0:1], onesrf[0:1, :], scl[:, 10:11], start=True, stop=False)
                t.matmul(F1[1][:, 2:3], onesrf[0:1, :], scl[:, 12:13], start=False, stop=True)
                t.wait_ge(s_drow, 2)
                t.matmul(F1[2][:, :], onesrf[0:1, :], gb_row[:, 0:512],
                         start=True, stop=True)
                t.matmul(F1[3][:, :], onesrf[0:1, :], gb_row[:, 512:1024],
                         start=True, stop=True).then_inc(s_bx2t, 1)

            @blk.scalar
            def _(sc):
                sc.wait_ge(s_bx2t, 2)
                c_s.pre(); c_s(sc.activation(invsw2_c[:, :], F1[1][:, 1:2], AF.Identity)
                               , s_w2sc)
                sc.wait_ge(s_bx2t, 3)
                c_s.pre(); c_s(sc.activation(invsx2_c[:, :], F1[1][:, 0:1], AF.Identity))
                c_s.pre(); c_s(sc.activation(s2_c[:, :], F1[1][:, 2:3], AF.Identity))
                c_s.pre(); c_s(sc.activation(gb_rep[:, 0:512], F1[2][:, :], AF.Identity))
                c_s.pre(); c_s(sc.activation(gb_rep[:, 512:1024], F1[3][:, :], AF.Identity)
                               , s_bx2s)

            @blk.gpsimd
            def _(g):
                g.wait_ge(s_bx2v, 2)
                g.dma_start(out=cc2i.ap()[:, :], in_=ccb[:, :]).then_inc(d_cc, 16)
                g.wait_ge(d_cc, 48)
                g.collective_compute(
                    "AllReduce", ALU.max,
                    replica_groups=[list(range(n_cores))],
                    ins=[cc2i.ap().opt()], outs=[cc2o.ap().opt()],
                ).then_inc(s_bx2g, 1)
                g.wait_ge(s_bx2g, 1)
                g.dma_start(out=ccb[:, :], in_=cc2o.ap()[:, :]).then_inc(d_cc, 16)
                g.wait_ge(d_cc, 64)
                c_g.pre(); c_g(g.engine_nop(), s_bx2g)

            @blk.sync
            def _(sy):
                sy.wait_ge(d_xpst, 16 * (NT - NT // 2))
                sy.wait_ge(d_xps2, 16 * (NT // 2))
                sy.wait_ge(d_h1st, 16 * NG * NT5)
                sy.dma_start(out=b2row[:, :], in_=b2_h[:, :]).then_inc(d_sm, 16)
                for half in range(2):
                    for gq in range(4):
                        sy.dma_start(
                            out=h1buf[:, 4 * gq:4 * gq + 4, half, :],
                            in_=bass.AP(h1_h.tensor,
                                        (half * 16 + 4 * gq) * 128 * 512,
                                        [[512, 128], [128 * 512, 4], [1, 512]])
                            ).then_inc(d_h1ld if half == 0 else d_h1lb, 16)
                sy.dma_start(out=xp_t[:, 0, :, :],
                             in_=bass.AP(xp_h.tensor, 0,
                                         [[D, 128], [128 * D, 4], [1, D]])
                             ).then_inc(d_xpld, 16)
                for k in range(4):
                    sy.dma_start(out=w2f_buf[:, k % 4, :],
                                 in_=w2T_h[k * 128:(k + 1) * 128, :]).then_inc(d_w2, 16)

        p2.close()
        Q = [psum3("Q_%d" % q, [128, D]) for q in range(4)]

        with nc.Block("ph3") as blk:

            @blk.sync
            def _(sy):
                sy.wait_ge(d_xpst, 16 * (NT - NT // 2))
                sy.wait_ge(d_xps2, 16 * (NT // 2))
                for tt in range(1, NT5):
                    for half in range(2):
                        hh = 2 * tt + half
                        if hh >= 2:
                            sy.wait_ge(s_h1rnd, hh - 1)
                        sy.wait_ge(s_f2, 64 * (tt - 1) + 32 * (half + 1))
                        for gq in range(4):
                            sy.dma_start(
                                out=h1buf[:, 4 * gq:4 * gq + 4, half, :],
                                in_=bass.AP(h1_h.tensor,
                                            (tt * NM + half * 16 + 4 * gq) * 128 * 512,
                                            [[512, 128], [128 * 512, 4], [1, 512]])
                                ).then_inc(d_h1ld if half == 0 else d_h1lb, 16)
                sy.wait_ge(d_out, 16 * NT5)

            @blk.vector
            def _(v):
                w2k = [0]

                def w2rnd(n):
                    for k in range(w2k[0], min(w2k[0] + n, NM)):
                        if k == 0:
                            v.wait_ge(s_w2sc, 1)
                        v.wait_ge(d_w2, min(64 * (k // 4 + 1), 16 * NM))
                        c_v.pre()
                        c_v(v.tensor_scalar(rscr[:, 0:1024], w2f_buf[:, k % 4, :],
                                            invsw2_c[:, 0:1], MAGIC, ALU.mult,
                                            ALU.add))
                        c_v.pre()
                        c_v(v.tensor_scalar(w2qT[:, k, :], rscr[:, 0:1024], MAGIC,
                                            None, ALU.subtract), s_w2rnd)
                    w2k[0] = min(w2k[0] + n, NM)

                for tt in range(NT5):
                    for half in range(2):
                        v.wait_ge(d_h1ld if half == 0 else d_h1lb, 64 * (tt + 1))
                        v.wait_ge(s_bx2v, 3)
                        for qq in range(8):
                            c_v.pre()
                            c_v(v.tensor_scalar(
                                rscr[:, :],
                                h1buf[:, 2 * qq:2 * (qq + 1), half, :],
                                invsx2_c[:, 0:1], MAGIC, ALU.mult, ALU.add))
                            c_v.pre()
                            c_v(v.tensor_scalar(
                                h1buf[:, 2 * qq:2 * (qq + 1), half, :],
                                rscr[:, :], MAGIC, None, ALU.subtract))
                        c_v.pre()
                        c_v(v.memset(scl[:, 15:16], 0.0)
                            , s_h1rnd)
                        w2rnd(8)
                    if tt == 0:
                        w2rnd(NM)


            @blk.scalar
            def _(sc):
                for k in range(4, NM):
                    sc.wait_ge(s_w2rnd, k - 3)
                    sc.dma_start(out=w2f_buf[:, k % 4, :],
                                 in_=w2T_h[k * 128:(k + 1) * 128, :]).then_inc(d_w2, 16)
                for tt in range(NT5):
                    if tt >= 1:
                        sc.wait_ge(d_xpld, 16 * tt)
                        if tt >= 2:
                            sc.wait_ge(s_oadd, 4 * (tt - 1))
                            sc.wait_ge(d_out, 16 * (tt - 1))
                        sc.dma_start(out=xp_t[:, tt % 2, :, :],
                                     in_=bass.AP(xp_h.tensor, tt * 512 * D,
                                                 [[D, 128], [128 * D, 4], [1, D]])
                                     ).then_inc(d_xpld, 16)
                    if tt >= 1:
                        sc.wait_ge(s_oadd, 4 * tt)
                        sc.dma_start(out=bass.AP(out_h.tensor, (tt - 1) * 512 * D,
                                                 [[D, 128], [128 * D, 4], [1, D]]),
                                     in_=xp_t[:, (tt - 1) % 2, :, :]).then_inc(d_out, 16)
                    for q in range(4):
                        tq = 4 * tt + q
                        sc.wait_ge(s_f2b, 2 * tt + (1 if q < 2 else 2))
                        if tq >= 4:
                            sc.wait_ge(s_oadd, tq - 3)
                        c_s.pre()
                        c_s(sc.activation(epi[:, q, :], Q[q][:, :], AF.Identity,
                                          scale=s2_c[:, 0:1]), s_epi)
                sc.wait_ge(s_oadd, 4 * NT5)
                sc.dma_start(out=bass.AP(out_h.tensor, (NT5 - 1) * 512 * D,
                                         [[D, 128], [128 * D, 4], [1, D]]),
                             in_=xp_t[:, (NT5 - 1) % 2, :, :]).then_inc(d_out, 16)

            @blk.gpsimd
            def _(g):
                for tt in range(NT5):
                    for q in range(4):
                        tq = 4 * tt + q
                        g.wait_ge(s_epi, tq + 1)
                        if tq >= 2:
                            g.wait_ge(s_oadd, tq - 1)
                        if tq == 0:
                            g.wait_ge(s_bx2s, 1)
                        c_g.pre()
                        c_g(g.tensor_mul(epi[:, q, :], epi[:, q, :],
                                         gb_rep[:, :]), s_gbm)
                        g.wait_ge(d_xpld, 16 * (tt + 1))
                        c_g.pre()
                        c_g(g.tensor_add(xp_t[:, tt % 2, q, :], xp_t[:, tt % 2, q, :],
                                         epi[:, q, :]), s_oadd)

            @blk.tensor
            def _(t):
                for tt in range(NT5):
                    for ph, (qs, half) in enumerate((((0, 1), 0), ((2, 3), 0),
                                                     ((0, 1), 1), ((2, 3), 1))):
                        t.wait_ge(s_h1rnd, 2 * tt + half + 1)
                        if ph == 0:
                            t.wait_ge(s_b2s, 1)
                            if tt >= 1:
                                t.wait_ge(s_epi, 4 * (tt - 1) + 2)
                        if ph == 1 and tt >= 1:
                            t.wait_ge(s_epi, 4 * tt)
                        for kk in range(16):
                            k = half * 16 + kk
                            if tt == 0:
                                t.wait_ge(s_w2rnd, k + 1)
                            last = None
                            for q in qs:
                                t.matmul(Q[q][:, 0:512],
                                         h1buf[:, kk, half, :][:, q * 128:(q + 1) * 128],
                                         w2qT[:, k, 0:512], start=(k == 0), stop=False)
                                last = t.matmul(Q[q][:, 512:1024],
                                                h1buf[:, kk, half, :][:, q * 128:(q + 1) * 128],
                                                w2qT[:, k, 512:1024],
                                                start=(k == 0), stop=False)
                            last.then_inc(s_f2, 1)
                        if ph >= 2:
                            last = None
                            for q in qs:
                                t.matmul(Q[q][:, 0:512], onesr[:, :], b2s_bf[:, 0:512],
                                         start=False, stop=True)
                                last = t.matmul(Q[q][:, 512:1024], onesr[:, :],
                                                b2s_bf[:, 512:1024],
                                                start=False, stop=True)
                            last.then_inc(s_f2b, 1)

        s3.close()
        p3.close()

    return nc


def prep_inputs(inputs, n_cores=8):
    L = inputs["L0_attn_x"].shape[1]
    D = 1024
    NM = 32
    x = np.ascontiguousarray(np.asarray(inputs["L0_attn_x"], dtype=np.float32))
    B = x.shape[0]
    shared = {
        "bwT": np.ascontiguousarray(np.asarray(inputs["L0_attn_B_w"]).T),
        "cwT": np.ascontiguousarray(np.asarray(inputs["L0_attn_C_w"]).T),
        "gwT": np.ascontiguousarray(np.asarray(inputs["L0_attn_gate_w"]).T),
        "gate_b": np.asarray(inputs["L0_attn_gate_b"]).reshape(1, D),
        "log_dec": np.asarray(inputs["L0_attn_log_decay"]).reshape(1, D),
        "w1T": np.ascontiguousarray(np.asarray(inputs["L0_ffn_w1"]).T),
        "b1f": np.ascontiguousarray(np.asarray(inputs["L0_ffn_b1"]).reshape(NM, 128).T),
        "w2T": np.ascontiguousarray(np.asarray(inputs["L0_ffn_w2"]).T),
        "b2": np.asarray(inputs["L0_ffn_b2"]).reshape(1, D),
        "wmod": np.ascontiguousarray(np.asarray(inputs["L0_wmod"])),
    }
    shared = {k: v.astype(np.float32) for k, v in shared.items()}
    c = np.asarray(inputs["L0_c"], dtype=np.float32)
    in_maps = []
    for b in range(n_cores):
        m = dict(shared)
        m["x"] = np.ascontiguousarray(x[b % B])
        m["c_fm"] = np.ascontiguousarray(c[b % B].reshape(8, 128).T)
        in_maps.append(m)
    return in_maps





def kernel(**inputs):
    from concourse.bass_utils import run_bass_kernel_spmd
    n_cores = 8
    nc = build_nc(L=4096, n_cores=n_cores)
    in_maps = prep_inputs(inputs, n_cores=n_cores)
    res = run_bass_kernel_spmd(nc, in_maps, core_ids=list(range(n_cores)))
    return np.stack([np.asarray(r["out"]) for r in res.results])
